# version 22
# baseline (speedup 1.0000x reference)
import numpy as np
import concourse.bass as bass
import concourse.mybir as mybir
from concourse.bass_utils import run_bass_kernel_spmd
from concourse.alu_op_type import AluOpType as ALU

F32 = mybir.dt.float32
BF16 = mybir.dt.bfloat16
AF = mybir.ActivationFunctionType
AX = mybir.AxisListType

ENGINES = ("sync", "tensor", "vector", "scalar", "gpsimd")
DMA_ENGINES = ("sync", "scalar", "gpsimd")



class _Op:
    __slots__ = ("eng", "fn", "deps", "inc", "seq", "dma", "dma_idx", "dkey",
                 "idx", "tag", "bar_waits")

    def __init__(self, eng, fn, dma, idx, tag):
        self.eng, self.fn, self.dma, self.idx, self.tag = eng, fn, dma, idx, tag
        self.deps = set()
        self.inc = False
        self.seq = 0
        self.dma_idx = -1
        self.dkey = None
        self.bar_waits = None


class Prog:
    def __init__(self):
        self.ops = []
        self.state = {}
        self.dma_count = {}

    def _st(self, key):
        return self.state.setdefault(key, {"w": [], "r": []})

    @staticmethod
    def _bank(key, lo, hi):
        if key.startswith("ps"):
            return (lo // 512) * 512, ((hi + 511) // 512) * 512
        return lo, hi

    def op(self, eng, fn, reads=(), writes=(), dma=False, tag=""):
        o = _Op(eng, fn, dma, len(self.ops), tag)
        reads = [(k,) + self._bank(k, lo, hi) for (k, lo, hi) in reads]
        writes = [(k,) + self._bank(k, lo, hi) for (k, lo, hi) in writes]
        deps = set()
        for (k, lo, hi) in reads:
            st = self._st(k)
            for (l, h, w) in st["w"]:
                if l < hi and lo < h:
                    deps.add(w)
        for (k, lo, hi) in writes:
            st = self._st(k)
            for (l, h, w) in st["w"]:
                if l < hi and lo < h:
                    deps.add(w)
            for (l, h, r) in st["r"]:
                if l < hi and lo < h:
                    deps.add(r)
        deps.discard(o)
        for (k, lo, hi) in reads:
            self._st(k)["r"].append((lo, hi, o))
        for (k, lo, hi) in writes:
            st = self._st(k)
            st["w"] = [e_ for e_ in st["w"] if not (lo <= e_[0] and e_[1] <= hi)]
            st["r"] = [e_ for e_ in st["r"] if not (lo <= e_[0] and e_[1] <= hi)]
            st["w"].append((lo, hi, o))
        o.deps = deps
        if dma:
            slot = writes[0][0] if writes else "anon"
            if reads and tag.endswith(("_st", "_stq", "_stv", "_sth", "_stl")):
                slot = reads[0][0]
            o.dkey = (eng, slot)
            o.dma_idx = self.dma_count.get(o.dkey, 0)
            self.dma_count[o.dkey] = o.dma_idx + 1
        self.ops.append(o)
        return o

    def barrier(self):
        snap_d = dict(self.dma_count)
        for e in ENGINES:
            o = _Op(e, lambda eng: eng.nop(), False, len(self.ops), "barrier")
            o.bar_waits = snap_d
            self.ops.append(o)
        self.state = {}

    def finalize(self):
        for o in self.ops:
            for d in o.deps:
                if d.dma:
                    continue
                if d.eng == "tensor" and o.eng == "tensor":
                    continue
                d.inc = True
        lastop = {}
        for o in self.ops:
            if o.bar_waits is not None:
                for e2, lo in lastop.items():
                    if not lo.dma:
                        lo.inc = True
                    o.deps.add(lo)
            else:
                lastop[o.eng] = o
        seq = {e: 0 for e in ENGINES}
        for o in self.ops:
            if o.inc and not o.dma:
                seq[o.eng] += 1
            o.seq = seq[o.eng]
        per_eng = {e: [] for e in ENGINES}
        for o in self.ops:
            per_eng[o.eng].append(o)
        self.per_eng = per_eng

    def emit(self, eng_name, eng, psems, dsems):
        water = {}
        for o in self.per_eng[eng_name]:
            waits = {}
            for d in o.deps:
                if d.eng == "tensor" and o.eng == "tensor" and not d.dma:
                    continue
                if d.dma:
                    key, val = ("d", d.dkey), 16 * (d.dma_idx + 1)
                else:
                    key, val = ("p", d.eng), d.seq
                if waits.get(key, 0) < val:
                    waits[key] = val
            if o.bar_waits is not None:
                for dk, cnt in o.bar_waits.items():
                    k = ("d", dk)
                    if waits.get(k, 0) < 16 * cnt:
                        waits[k] = 16 * cnt
            for key, val in sorted(waits.items(), key=str):
                if water.get(key, 0) >= val:
                    continue
                water[key] = val
                sem = dsems[key[1]] if key[0] == "d" else psems[key[1]]
                eng.wait_ge(sem, val)
            ins = o.fn(eng)
            if o.dma:
                ins.then_inc(dsems[o.dkey], 16)
            elif o.inc:
                ins.then_inc(psems[o.eng], 1)


def _finish(p, nc, stack, y, pool, P, D, OT):
    for tt in range(OT):
        p.op("gpsimd", lambda e, tt=tt: e.dma_start(
                out=y[128 * tt:128 * (tt + 1), :], in_=P(D * tt, D)),
             reads=[("x2", D * tt, D * (tt + 1))], writes=[("ydram", 0, 1)],
             dma=True, tag="ph8_st")
    p.barrier()
    p.finalize()
    psems = {e: stack.enter_context(nc.semaphore("p_" + e)) for e in ENGINES}
    dsems = {dk: stack.enter_context(nc.semaphore(f"d_{dk[0]}_{dk[1]}"))
             for dk in p.dma_count}
    with nc.Block() as block:
        @block.sync
        def _(eng):
            p.emit("sync", eng, psems, dsems)

        @block.tensor
        def _(eng):
            p.emit("tensor", eng, psems, dsems)

        @block.vector
        def _(eng):
            p.emit("vector", eng, psems, dsems)

        @block.scalar
        def _(eng):
            p.emit("scalar", eng, psems, dsems)

        @block.gpsimd
        def _(eng):
            p.emit("gpsimd", eng, psems, dsems)
    nc._stack = stack
    nc._prog = p



class Cfg:
    def __init__(self, D=4096, S=2048, NQ=32, NKV=8):
        self.D, self.S, self.NQ, self.NKV = D, S, NQ, NKV
        self.HD = 128
        self.E, self.R = 8, 16
        assert NQ // NKV == 4
        self.QH = NQ // 2
        self.KH = NKV // 2
        self.OD = (self.QH + 2 * self.KH) * 128
        self.KC = D // 128
        self.TT = S // 128
        self.OWN = S // 2
        self.OT = self.OWN // 128
        self.eps = 1e-5
        self.scale = float(1.0 / np.sqrt(128.0))


def _rope_tables(cfg):
    j = np.arange(64, dtype=np.float64)
    inv = 1.0 / (10000.0 ** (2.0 * j / 128.0))
    t = np.arange(cfg.S, dtype=np.float64)
    ang = np.outer(inv, t)
    C2 = np.concatenate([np.cos(ang)] * 2, axis=0).astype(np.float32)
    S2 = np.concatenate([np.sin(ang)] * 2, axis=0).astype(np.float32)
    return C2, S2


def _chunks(total, step):
    out = []
    o = 0
    while o < total:
        n = min(step, total - o)
        out.append((o, n))
        o += n
    return out



def build_program(cfg, stop_after=99):
    nc = bass.Bass("TRN2", target_bir_lowering=False)
    D, S = cfg.D, cfg.S
    KC, TT, OWN, OT = cfg.KC, cfg.TT, cfg.OWN, cfg.OT
    QH, KH, OD = cfg.QH, cfg.KH, cfg.OD
    OC = OD // 128

    def inp(name, shape, dt=F32):
        return nc.dram_tensor(name, list(shape), dt, kind="ExternalInput").ap()

    xb = inp("xb", (S, D))
    xown = inp("xown", (OWN, D))
    wqkvT = inp("wqkvT", (D, OD))
    woT = inp("woT", (QH * 128, D))
    baseT = inp("baseT", (D, D))
    routT = inp("routT", (D, 8))
    aT = inp("aT", (D, 128))
    bflat = inp("bflat", (128, D))
    ropeC = inp("ropeC", (128, S))
    ropeS = inp("ropeS", (128, S))
    cid = inp("cid", (128, 128))
    cbid = inp("cbid", (128, 128), BF16)
    cmsk = inp("cmsk", (128, 128))
    crmat = inp("crmat", (128, 128))
    cbc = inp("cbc", (8, 128))

    y = nc.dram_tensor("y", [OWN, D], F32, kind="ExternalOutput").ap()

    def scr(name, shape, dt=F32):
        return nc.dram_tensor(name, list(shape), dt, kind="Internal").ap()

    wqh, wql = scr("wqh", (D, OD), BF16), scr("wql", (D, OD), BF16)
    woh, wol = scr("woh", (QH * 128, D), BF16), scr("wol", (QH * 128, D), BF16)
    bah, bal = scr("bah", (D, D), BF16), scr("bal", (D, D), BF16)
    ath, atl = scr("ath", (D, 128), BF16), scr("atl", (D, 128), BF16)
    rth, rtl = scr("rth", (D, 8), BF16), scr("rtl", (D, 8), BF16)
    bfh, bfl = scr("bfh", (128, D), BF16), scr("bfl", (128, D), BF16)
    xn1h, xn1l = scr("xn1h", (D, S), BF16), scr("xn1l", (D, S), BF16)
    qkh = scr("qkh", ((QH + KH) * 128, S), BF16)
    qkl = scr("qkl", ((QH + KH) * 128, S), BF16)
    vtms = scr("vtms", (S, KH * 128))
    atth, attl = scr("atth", (QH * 128, S), BF16), scr("attl", (QH * 128, S), BF16)
    _qch = [(0, S)]
    partsq = [scr(f"parts{i}", (tqn, D)) for i, (tq0, tqn) in enumerate(_qch)]
    rsoutsq = [scr(f"rsouts{i}", (tqn // 2, D)) for i, (tq0, tqn) in enumerate(_qch)]
    xn2h, xn2l = scr("xn2h", (D, OWN), BF16), scr("xn2l", (D, OWN), BF16)

    import contextlib
    stack = contextlib.ExitStack()
    sb = lambda name, shape, dt=F32: stack.enter_context(nc.sbuf_tensor(name, list(shape), dt))

    s_id = sb("s_id", (128, 128))
    s_bid = sb("s_bid", (128, 128), BF16)
    s_msk = sb("s_msk", (128, 128))
    s_R = sb("s_R", (128, 128))
    s_bc = sb("s_bc", (8, 128))
    s_C = sb("s_C", (128, S))
    s_S = sb("s_S", (128, S))

    SPL = min(3072, max(OD, D))
    PH0NEED = 512 + 2 * SPL + SPL
    POOLC = max(OT * D, 6 * D, 12 * S, QH * 512, 4096)
    WKC = max(3 * D, PH0NEED, 7 * OWN + 512, 4096)
    pool = sb("pool", (128, POOLC))
    wk = sb("wk", (128, WKC))
    sm = sb("sm", (128, 64))
    lgsm = sb("lgsm", (128, 8 * OT * 2 + 32))

    psA = stack.enter_context(nc.psum_tensor("psA", [128, 2048], F32))
    psD = stack.enter_context(nc.psum_tensor("psD", [128, 2048], F32))

    p = Prog()

    def P(lo, n):
        return pool[:, lo:lo + n]

    def Pb(lo, nb):
        return pool[:, lo:lo + nb // 2].bitcast(BF16)

    def W(lo, n):
        return wk[:, lo:lo + n]

    def Wb(lo, nb):
        return wk[:, lo:lo + nb // 2].bitcast(BF16)

    for t, src, key in ((s_id, cid, "c_id"), (s_bid, cbid, "c_bid"),
                        (s_msk, cmsk, "c_msk"), (s_R, crmat, "c_R"),
                        (s_bc, cbc, "c_bc"), (s_C, ropeC, "c_C"),
                        (s_S, ropeS, "c_S")):
        p.op("sync", lambda e, t=t, src=src: e.dma_start(out=t[:], in_=src[:]),
             writes=[(key, 0, 1)], dma=True, tag="const")

    def split_jobs(src, dsth, dstl, R, C, base, tg, eng=("scalar", "vector")):
        jobs = []
        ceng, seng = eng
        for ri, r0 in enumerate(range(0, R, 128)):
            for (c0, cn) in _chunks(C, SPL):
                def job(r0=r0, c0=c0, cn=cn, sl=ri % 2, src=src, dsth=dsth,
                        dstl=dstl, base=base, tg=tg):
                    ik = (f"sp_i{sl}", 0, 1)
                    ldq = "gpsimd" if ceng == "gpsimd" else "sync"
                    p.op(ldq, lambda e: e.dma_start(
                            out=W(base + sl * SPL, cn), in_=src[r0:r0 + 128, c0:c0 + cn]),
                         writes=[ik], dma=True, tag=f"{tg}_ld")
                    for (p0, pn) in _chunks(cn, 1024):
                        def _cast(e, p0=p0, pn=pn):
                            dst = Wb(base + 2 * SPL + p0 // 2, pn)
                            srcv = W(base + sl * SPL + p0, pn)
                            return e.copy(dst, srcv) if ceng == "scalar" else e.tensor_copy(dst, srcv)
                        p.op(ceng, _cast,
                             reads=[ik], writes=[("sp_h", 0, 1)], tag=f"{tg}_hi")
                        p.op(seng, lambda e, p0=p0, pn=pn: e.tensor_tensor(
                                Wb(base + 2 * SPL + SPL // 2 + p0 // 2, pn),
                                W(base + sl * SPL + p0, pn),
                                Wb(base + 2 * SPL + p0 // 2, pn), ALU.subtract),
                             reads=[ik, ("sp_h", 0, 1)], writes=[("sp_l", 0, 1)], tag=f"{tg}_lo")
                    p.op("gpsimd", lambda e: e.dma_start(
                            out=dsth[r0:r0 + 128, c0:c0 + cn], in_=Wb(base + 2 * SPL, cn)),
                         reads=[("sp_h", 0, 1)], writes=[("wsp", 0, 1)], dma=True, tag=f"{tg}_sth")
                    p.op("gpsimd", lambda e: e.dma_start(
                            out=dstl[r0:r0 + 128, c0:c0 + cn], in_=Wb(base + 2 * SPL + SPL // 2, cn)),
                         reads=[("sp_l", 0, 1)], writes=[("wsp", 0, 1)], dma=True, tag=f"{tg}_stl")
                jobs.append(job)
        return jobs

    for j_ in split_jobs(wqkvT, wqh, wql, D, OD, 0, "ph0a", ("scalar", "gpsimd")):
        j_()

    XH, XL, FH, FL = 4 * D, 4 * D + D // 2, 5 * D, 5 * D + D // 2
    for tt in range(TT):
        sl = tt % 2
        xtm = (f"xtm{sl}", 0, 1)
        p.op("sync", lambda e, tt=tt, sl=sl: e.dma_start(
                out=P(sl * D, D), in_=xb[128 * tt:128 * (tt + 1), :]),
             writes=[xtm], dma=True, tag="ph1_load")
        ms = tt % 2
        p.op("scalar", lambda e, sl=sl, ms=ms: e.activation(
                P(2 * D, D), P(sl * D, D), AF.Square, accum_out=sm[:, 2 + ms:3 + ms]),
             reads=[xtm], writes=[("xsq", 0, 1), ("sx2", ms, ms + 1)], tag="ph1_sq")
        p.op("vector", lambda e, sl=sl, ms=ms: e.reduce_sum(
                sm[:, 0 + ms:1 + ms], P(sl * D, D), axis=AX.X),
             reads=[xtm], writes=[("sx", ms, ms + 1)], tag="ph1_sum")
        p.op("vector", lambda e, ms=ms: e.tensor_scalar(
                sm[:, 4 + ms:5 + ms], sm[:, 0 + ms:1 + ms], 1.0 / D, None, ALU.mult),
             reads=[("sx", ms, ms + 1)], writes=[("mean", ms, ms + 1)], tag="ph1_mean")
        p.op("vector", lambda e, ms=ms: e.tensor_tensor(
                sm[:, 6 + ms:7 + ms], sm[:, 4 + ms:5 + ms], sm[:, 4 + ms:5 + ms], ALU.mult),
             reads=[("mean", ms, ms + 1)], writes=[("msq", ms, ms + 1)], tag="ph1_msq")
        p.op("vector", lambda e, ms=ms: e.tensor_scalar(
                sm[:, 8 + ms:9 + ms], sm[:, 2 + ms:3 + ms], 1.0 / D, None, ALU.mult),
             reads=[("sx2", ms, ms + 1)], writes=[("var0", ms, ms + 1)], tag="ph1_var0")
        p.op("vector", lambda e, ms=ms: e.tensor_scalar(
                sm[:, 10 + ms:11 + ms], sm[:, 8 + ms:9 + ms],
                sm[:, 6 + ms:7 + ms], cfg.eps, ALU.subtract, ALU.add),
             reads=[("var0", ms, ms + 1), ("msq", ms, ms + 1)],
             writes=[("vpe", ms, ms + 1)], tag="ph1_var")
        p.op("scalar", lambda e, ms=ms: e.activation(
                sm[:, 12 + ms:13 + ms], sm[:, 10 + ms:11 + ms], AF.Ln),
             reads=[("vpe", ms, ms + 1)], writes=[("lnv", ms, ms + 1)], tag="ph1_ln")
        p.op("scalar", lambda e, ms=ms: e.activation(
                sm[:, 14 + ms:15 + ms], sm[:, 12 + ms:13 + ms], AF.Exp, scale=-0.5),
             reads=[("lnv", ms, ms + 1)], writes=[("rstd", ms, ms + 1)], tag="ph1_rstd")
        p.op("vector", lambda e, sl=sl, ms=ms: e.tensor_scalar(
                P(3 * D, D), P(sl * D, D), sm[:, 4 + ms:5 + ms],
                sm[:, 14 + ms:15 + ms], ALU.subtract, ALU.mult),
             reads=[xtm, ("mean", ms, ms + 1), ("rstd", ms, ms + 1)],
             writes=[("xnt", 0, 1)], tag="ph1_apply")
        p.op("scalar", lambda e: e.copy(Pb(XH, D), P(3 * D, D)),
             reads=[("xnt", 0, 1)], writes=[("xnth", 0, 1)], tag="ph1_casth")
        p.op("vector", lambda e: e.tensor_tensor(
                Pb(XL, D), P(3 * D, D), Pb(XH, D), ALU.subtract),
             reads=[("xnt", 0, 1), ("xnth", 0, 1)], writes=[("xntl", 0, 1)], tag="ph1_castl")
        for hl, (srcoff, srck, trt, trk, fmoff, fmk) in enumerate(
                ((XH, "xnth", psA, "psA", FH, "fmh"), (XL, "xntl", psD, "psD", FL, "fml"))):
            for c in range(KC):
                bank = (c // 8) % 4
                slot = c % 8
                pcol = 512 * bank + 64 * slot
                p.op("tensor", lambda e, c=c, srcoff=srcoff, trt=trt, pcol=pcol: e.transpose(
                        trt[:, pcol:pcol + 64].bitcast(BF16),
                        Pb(srcoff, D)[:, 128 * c:128 * (c + 1)], s_bid[:]),
                     reads=[(srck, 0, 1), ("c_bid", 0, 1)],
                     writes=[(trk, pcol, pcol + 64)], tag="ph1_tr")
                if slot == 7 or c == KC - 1:
                    c0 = (c // 8) * 8
                    ncp = c - c0 + 1
                    p.op("vector", lambda e, c0=c0, ncp=ncp, bank=bank, trt=trt, fmoff=fmoff: e.tensor_copy(
                            Pb(fmoff + 64 * c0, 128 * ncp),
                            trt[:, 512 * bank:512 * bank + 64 * ncp].bitcast(BF16)),
                         reads=[(trk, 512 * bank, 512 * bank + 64 * ncp)],
                         writes=[(fmk, 64 * c0, 64 * (c0 + ncp))], tag="ph1_ev")
            dstf = (xn1h if hl == 0 else xn1l).rearrange("(c q) t -> q c t", q=128)[
                :, :, 128 * tt:128 * (tt + 1)]
            srcf = Pb(fmoff, D).rearrange("q (c j) -> q c j", c=KC)
            for c0 in range(0, KC, 16):
                c1 = min(c0 + 16, KC)
                p.op("scalar", lambda e, dst=dstf[:, c0:c1, :], src=srcf[:, c0:c1, :]:
                        e.dma_start(out=dst, in_=src),
                     reads=[(fmk, 64 * c0, 64 * c1)], writes=[("xn1s", 0, 1)],
                     dma=True, tag="ph1_st")

    p.barrier()

    if stop_after < 2:
        _finish(p, nc, stack, y, pool, P, D, OT)
        return nc


    def qkv_store(oc):
        is_q = oc < QH
        is_k = QH <= oc < QH + KH
        psrc, pkey = (psA, "psA") if oc % 2 == 0 else (psD, "psD")
        if is_q or is_k:
            if is_q:
                p.op("vector", lambda e, psrc=psrc: e.tensor_scalar(
                        P(2 * S, S), psrc[:, 0:S], cfg.scale, None, ALU.mult),
                     reads=[(pkey, 0, S)], writes=[("qraw", 0, 1)], tag="ph2_evq")
            else:
                p.op("scalar", lambda e, psrc=psrc: e.copy(P(2 * S, S), psrc[:, 0:S]),
                     reads=[(pkey, 0, S)], writes=[("qraw", 0, 1)], tag="ph2_evk")
            for (t0, n) in _chunks(S, 512):
                p.op("tensor", lambda e, t0=t0, n=n, psrc=psrc: e.matmul(
                        psrc[:, t0:t0 + n], s_R[:], P(2 * S + t0, n),
                        start=True, stop=True),
                     reads=[("qraw", 0, 1), ("c_R", 0, 1)],
                     writes=[(pkey, t0, t0 + n)], tag="ph2_rmm")
            p.op("vector", lambda e: e.tensor_tensor(
                    P(3 * S, S), P(2 * S, S), s_C[:], ALU.mult),
                 reads=[("qraw", 0, 1), ("c_C", 0, 1)], writes=[("q1", 0, 1)], tag="ph2_qc")
            p.op("vector", lambda e, psrc=psrc: e.tensor_tensor(
                    P(4 * S, S), psrc[:, 0:S], s_S[:], ALU.mult),
                 reads=[(pkey, 0, S), ("c_S", 0, 1)], writes=[("q2", 0, 1)], tag="ph2_qs")
            p.op("vector", lambda e: e.tensor_tensor(
                    P(5 * S, S), P(3 * S, S), P(4 * S, S), ALU.add),
                 reads=[("q1", 0, 1), ("q2", 0, 1)], writes=[("qout", 0, 1)], tag="ph2_qo")
            p.op("scalar", lambda e: e.copy(Pb(6 * S, S), P(5 * S, S)),
                 reads=[("qout", 0, 1)], writes=[("qouth", 0, 1)], tag="ph2_qh")
            p.op("vector", lambda e: e.tensor_tensor(
                    Pb(6 * S + S // 2, S), P(5 * S, S), Pb(6 * S, S), ALU.subtract),
                 reads=[("qout", 0, 1), ("qouth", 0, 1)], writes=[("qoutl", 0, 1)], tag="ph2_ql")
            row = 128 * oc
            p.op("gpsimd", lambda e, row=row: e.dma_start(
                    out=qkh[row:row + 128, :], in_=Pb(6 * S, S)),
                 reads=[("qouth", 0, 1)], writes=[("qks", 0, 1)], dma=True, tag="ph2_stq")
            p.op("gpsimd", lambda e, row=row: e.dma_start(
                    out=qkl[row:row + 128, :], in_=Pb(6 * S + S // 2, S)),
                 reads=[("qoutl", 0, 1)], writes=[("qks", 0, 1)], dma=True, tag="ph2_stq")
        else:
            vc = oc - QH - KH
            p.op("scalar", lambda e, psrc=psrc: e.copy(P(2 * S, S), psrc[:, 0:S]),
                 reads=[(pkey, 0, S)], writes=[("qraw", 0, 1)], tag="ph2_evv")
            for tb in range(TT):
                bank = tb % 4
                p.op("tensor", lambda e, tb=tb, bank=bank, psrc=psrc: e.transpose(
                        psrc[:, 512 * bank:512 * bank + 128], P(2 * S + 128 * tb, 128), s_id[:]),
                     reads=[("qraw", 0, 1), ("c_id", 0, 1)],
                     writes=[(pkey, 512 * bank, 512 * bank + 128)], tag="ph2_vtr")
                p.op("vector", lambda e, tb=tb, bank=bank, psrc=psrc: e.tensor_copy(
                        P(3 * S + 128 * tb, 128), psrc[:, 512 * bank:512 * bank + 128]),
                     reads=[(pkey, 512 * bank, 512 * bank + 128)],
                     writes=[("q1", 128 * tb, 128 * (tb + 1))], tag="ph2_vev")
            dstf = vtms.rearrange("(tb q) v -> q tb v", q=128)[:, :, 128 * vc:128 * (vc + 1)]
            srcf = P(3 * S, S).rearrange("q (tb j) -> q tb j", tb=TT)
            for b0 in range(0, TT, 8):
                b1 = min(b0 + 8, TT)
                p.op("gpsimd", lambda e, dst=dstf[:, b0:b1, :], src=srcf[:, b0:b1, :]:
                        e.dma_start(out=dst, in_=src),
                     reads=[("q1", 0, S)], writes=[("vtms", 0, 1)], dma=True, tag="ph2_stv")

    GG = ("gpsimd", "gpsimd")
    _ph0b = (split_jobs(woT, woh, wol, QH * 128, D, 512, "ph0b", GG)
             + split_jobs(baseT, bah, bal, D, D, 512, "ph0c", GG)
             + split_jobs(aT, ath, atl, D, 128, 512, "ph0d", GG)
             + split_jobs(routT, rth, rtl, D, 8, 512, "ph0e", GG)
             + split_jobs(bflat, bfh, bfl, 128, D, 512, "ph0f", GG))

    SB = S // 2
    _job_every = max(1, ((OC // 2) * KC) // max(1, (2 * len(_ph0b) // 3)))
    for og in range(OC // 2):
        for k in range(KC):
            if k % _job_every == _job_every - 1 and _ph0b and og > 0:
                _ph0b.pop(0)()
            wsl = k % 2
            p.op("sync", lambda e, og=og, k=k, wsl=wsl: e.dma_start(
                    out=Wb(256 * wsl, 256),
                    in_=wqh[128 * k:128 * (k + 1), 256 * og:256 * (og + 1)]),
                 reads=[("wqh_s", 0, 1)],
                 writes=[(f"wph{wsl}", 0, 1)], dma=True, tag="ph2_w")
            p.op("sync", lambda e, og=og, k=k, wsl=wsl: e.dma_start(
                    out=Wb(256 * wsl + 128, 256),
                    in_=wql[128 * k:128 * (k + 1), 256 * og:256 * (og + 1)]),
                 reads=[("wql_s", 0, 1)],
                 writes=[(f"wpl{wsl}", 0, 1)], dma=True, tag="ph2_w")
            msl = k % 2
            p.op("scalar", lambda e, k=k, msl=msl: e.dma_start(
                    out=Pb(msl * S, S), in_=xn1h[128 * k:128 * (k + 1), :]),
                 reads=[("xn1s", 0, 1)], writes=[(f"mvh{msl}", 0, 1)], dma=True, tag="ph2_x")
            p.op("scalar", lambda e, k=k, msl=msl: e.dma_start(
                    out=Pb(msl * S + SB, S), in_=xn1l[128 * k:128 * (k + 1), :]),
                 reads=[("xn1s", 0, 1)], writes=[(f"mvl{msl}", 0, 1)], dma=True, tag="ph2_x")
            whofs = 512 * wsl
            wlofs = 512 * wsl + 256
            for oci, ps, pkey in ((0, psA, "psA"), (1, psD, "psD")):
                for (t0, n) in _chunks(S, 512):
                    for pi, (wof, wkk, mof, mkk) in enumerate((
                            (whofs + 128 * oci, f"wph{wsl}", msl * S, f"mvh{msl}"),
                            (whofs + 128 * oci, f"wph{wsl}", msl * S + SB, f"mvl{msl}"),
                            (wlofs + 128 * oci, f"wpl{wsl}", msl * S, f"mvh{msl}"))):
                        p.op("tensor", lambda e, k=k, t0=t0, n=n, ps=ps, pi=pi,
                                wof=wof, mof=mof: e.matmul(
                                ps[:, t0:t0 + n],
                                Wb(0, 1024)[:, wof:wof + 128],
                                Pb(mof, S)[:, t0:t0 + n],
                                start=(k == 0 and pi == 0),
                                stop=(k == KC - 1 and pi == 2)),
                             reads=[(wkk, 0, 1), (mkk, 0, 1)],
                             writes=[(pkey, t0, t0 + n)], tag="ph2_mm")
        qkv_store(2 * og)
        qkv_store(2 * og + 1)

    p.barrier()

    if stop_after < 3:
        _finish(p, nc, stack, y, pool, P, D, OT)
        return nc

    for j in range(KH):
        jp = j % 2
        kb_ = 0 if jp == 0 else 10 * S
        vb_ = S if jp == 0 else 11 * S
        p.op("sync", lambda e, j=j, kb_=kb_: e.dma_start(
                out=Pb(kb_, S), in_=qkh[128 * (QH + j):128 * (QH + j + 1), :]),
             reads=[("qks", 0, 1)], writes=[(f"aKh{jp}", 0, 1)], dma=True, tag="ph3_K")
        p.op("sync", lambda e, j=j, kb_=kb_: e.dma_start(
                out=Pb(kb_ + SB, S), in_=qkl[128 * (QH + j):128 * (QH + j + 1), :]),
             reads=[("qks", 0, 1)], writes=[(f"aKl{jp}", 0, 1)], dma=True, tag="ph3_K2")
        vsrc = vtms.rearrange("(tb q) v -> q tb v", q=128)[:, :, 128 * j:128 * (j + 1)]
        vdst = P(vb_, S).rearrange("q (tb j) -> q tb j", tb=TT)
        for b0 in range(0, TT, 8):
            b1 = min(b0 + 8, TT)
            p.op("sync", lambda e, src=vsrc[:, b0:b1, :], dst=vdst[:, b0:b1, :]:
                    e.dma_start(out=dst, in_=src),
                 reads=[("vtms", 0, 1)], writes=[(f"aV{jp}", 0, 1)], dma=True, tag="ph3_V")
        for qh in range(4):
            hq = 4 * j + qh
            hp = hq % 2
            qb_ = 2 * S if hp == 0 else 9 * S
            nper3 = max(1, len(_ph0b) // max(1, (KH - j) * 4 - qh))
            for _ in range(min(nper3, len(_ph0b))):
                _ph0b.pop(0)()
            p.op("sync", lambda e, hq=hq, qb_=qb_: e.dma_start(
                    out=Pb(qb_, S), in_=qkh[128 * hq:128 * (hq + 1), :]),
                 reads=[("qks", 0, 1)], writes=[(f"aQh{hp}", 0, 1)], dma=True, tag="ph3_Q")
            p.op("sync", lambda e, hq=hq, qb_=qb_: e.dma_start(
                    out=Pb(qb_ + SB, S), in_=qkl[128 * hq:128 * (hq + 1), :]),
                 reads=[("qks", 0, 1)], writes=[(f"aQl{hp}", 0, 1)], dma=True, tag="ph3_Q2")
            def emit_scores(g):
                klen = 128 * (g + 1)
                nblk = (g + 4) // 4
                par = g % 2
                smb = 16 + 10 * par
                for bi, kb in enumerate(range(0, g + 1, 4)):
                    n = min(512, klen - 128 * kb)
                    for pi, (qof, qk_, kof, kk_) in enumerate((
                            (qb_, f"aQh{hp}", kb_, f"aKh{jp}"),
                            (qb_ + SB, f"aQl{hp}", kb_, f"aKh{jp}"),
                            (qb_, f"aQh{hp}", kb_ + SB, f"aKl{jp}"))):
                        p.op("tensor", lambda e, g=g, kb=kb, n=n, pi=pi, qof=qof, kof=kof: e.matmul(
                                psA[:, 128 * kb:128 * kb + n],
                                Pb(qof, S)[:, 128 * g:128 * (g + 1)],
                                Pb(kof, S)[:, 128 * kb:128 * kb + n],
                                start=(pi == 0), stop=(pi == 2)),
                             reads=[(qk_, 0, 1), (kk_, 0, 1)],
                             writes=[("psA", 128 * kb, 128 * kb + n)], tag="ph3_s")
                    if kb + 4 <= g:
                        p.op("vector", lambda e, kb=kb, n=n, bi=bi, smb=smb: e.reduce_max(
                                sm[:, smb + bi:smb + bi + 1], psA[:, 128 * kb:128 * kb + n], axis=AX.X),
                             reads=[("psA", 128 * kb, 128 * kb + n)],
                             writes=[("rm", smb + bi, smb + bi + 1)], tag="ph3_m1")
                p.op("vector", lambda e, g=g: e.tensor_tensor(
                        psA[:, 128 * g:128 * (g + 1)], psA[:, 128 * g:128 * (g + 1)],
                        s_msk[:], ALU.add),
                     reads=[("psA", 128 * g, 128 * (g + 1)), ("c_msk", 0, 1)],
                     writes=[("psA", 128 * g, 128 * (g + 1))], tag="ph3_msk")
                lb = (g // 4) * 4
                lbn = klen - 128 * lb
                p.op("vector", lambda e, lb=lb, lbn=lbn, nblk=nblk, smb=smb: e.reduce_max(
                        sm[:, smb + nblk - 1:smb + nblk], psA[:, 128 * lb:128 * lb + lbn], axis=AX.X),
                     reads=[("psA", 128 * lb, 128 * lb + lbn)],
                     writes=[("rm", smb + nblk - 1, smb + nblk)], tag="ph3_m2")
                if nblk > 1:
                    p.op("vector", lambda e, nblk=nblk, smb=smb: e.reduce_max(
                            sm[:, smb + 4:smb + 5], sm[:, smb:smb + nblk], axis=AX.X, negate=True),
                         reads=[("rm", smb, smb + nblk)],
                         writes=[("negmx", smb, smb + 1)], tag="ph3_mx")
                else:
                    p.op("vector", lambda e, smb=smb: e.tensor_scalar(
                            sm[:, smb + 4:smb + 5], sm[:, smb:smb + 1], -1.0, None, ALU.mult),
                         reads=[("rm", smb, smb + 1)], writes=[("negmx", smb, smb + 1)], tag="ph3_mx")

            def emit_soft(g):
                klen = 128 * (g + 1)
                par = g % 2
                smb = 16 + 10 * par
                pnoff = 4 * S if par == 0 else 8 * S
                p.op("scalar", lambda e, klen=klen, smb=smb: e.activation(
                        P(3 * S, klen), psA[:, 0:klen], AF.Exp,
                        bias=sm[:, smb + 4:smb + 5], scale=1.0, accum_out=sm[:, smb + 5:smb + 6]),
                     reads=[("psA", 0, klen), ("negmx", smb, smb + 1)],
                     writes=[("aP", 0, klen), ("ssum", smb, smb + 1)], tag="ph3_exp")
                p.op("vector", lambda e, smb=smb: e.reciprocal(
                        sm[:, smb + 6:smb + 7], sm[:, smb + 5:smb + 6]),
                     reads=[("ssum", smb, smb + 1)], writes=[("rinv", smb, smb + 1)], tag="ph3_ri")
                p.op("vector", lambda e, klen=klen, smb=smb, pnoff=pnoff: e.tensor_scalar(
                        P(pnoff, klen), P(3 * S, klen), sm[:, smb + 6:smb + 7], None, ALU.mult),
                     reads=[("aP", 0, klen), ("rinv", smb, smb + 1)],
                     writes=[(f"aPn{par}", 0, klen)], tag="ph3_n")

            def emit_ptpv(g):
                par = g % 2
                pnoff = 4 * S if par == 0 else 8 * S
                for kb in range(g + 1):
                    pr = (kb // 2) % 2
                    tco = (512 if pr == 0 else 1536) + 128 * (kb % 2)
                    p.op("tensor", lambda e, kb=kb, tco=tco, pnoff=pnoff: e.transpose(
                            psD[:, tco:tco + 128],
                            P(pnoff + 128 * kb, 128), s_id[:]),
                         reads=[(f"aPn{par}", 128 * kb, 128 * (kb + 1)), ("c_id", 0, 1)],
                         writes=[("psD", tco, tco + 128)], tag="ph3_tr")
                    if kb % 2 == 1 or kb == g:
                        kb0 = kb - (kb % 2)
                        n2 = 128 * (kb - kb0 + 1)
                        eco = 512 if pr == 0 else 1536
                        if pr == 0:
                            p.op("vector", lambda e, kb0=kb0, n2=n2, eco=eco: e.tensor_copy(
                                    P(5 * S + 128 * kb0, n2), psD[:, eco:eco + n2]),
                                 reads=[("psD", eco, eco + n2)],
                                 writes=[("aPT", 128 * kb0, 128 * kb0 + n2)], tag="ph3_trev")
                        else:
                            p.op("scalar", lambda e, kb0=kb0, n2=n2, eco=eco: e.copy(
                                    P(5 * S + 128 * kb0, n2), psD[:, eco:eco + n2]),
                                 reads=[("psD", eco, eco + n2)],
                                 writes=[("aPT", 128 * kb0, 128 * kb0 + n2)], tag="ph3_trev")
                for kb in range(g + 1):
                    p.op("tensor", lambda e, kb=kb, g=g, vb_=vb_: e.matmul(
                            psD[:, 1024:1152], P(vb_ + 128 * kb, 128),
                            P(5 * S + 128 * kb, 128), start=(kb == 0), stop=(kb == g)),
                         reads=[(f"aV{jp}", 0, 1), ("aPT", 128 * kb, 128 * (kb + 1))],
                         writes=[("psD", 1024, 1152)], tag="ph3_pv")
                p.op("vector", lambda e, g=g: e.tensor_copy(
                        P(6 * S + 128 * g, 128), psD[:, 1024:1152]),
                     reads=[("psD", 1024, 1152)],
                     writes=[("aao", 128 * g, 128 * (g + 1))], tag="ph3_aev")

            for g in range(TT):
                emit_scores(g)
                if g > 0:
                    emit_ptpv(g - 1)
                emit_soft(g)
            emit_ptpv(TT - 1)
            p.op("scalar", lambda e: e.copy(Pb(7 * S, S), P(6 * S, S)),
                 reads=[("aao", 0, S)], writes=[("aoh", 0, 1)], tag="ph3_aoh")
            p.op("vector", lambda e: e.tensor_tensor(
                    Pb(7 * S + SB, S), P(6 * S, S), Pb(7 * S, S), ALU.subtract),
                 reads=[("aao", 0, S), ("aoh", 0, 1)], writes=[("aol", 0, 1)], tag="ph3_aol")
            p.op("gpsimd", lambda e, hq=hq: e.dma_start(
                    out=atth[128 * hq:128 * (hq + 1), :], in_=Pb(7 * S, S)),
                 reads=[("aoh", 0, 1)], writes=[("attns", 0, 1)], dma=True, tag="ph3_st")
            p.op("gpsimd", lambda e, hq=hq: e.dma_start(
                    out=attl[128 * hq:128 * (hq + 1), :], in_=Pb(7 * S + SB, S)),
                 reads=[("aol", 0, 1)], writes=[("attns", 0, 1)], dma=True, tag="ph3_st")

    for j_ in _ph0b:
        j_()
    _ph0b.clear()

    p.barrier()

    if stop_after < 4:
        _finish(p, nc, stack, y, pool, P, D, OT)
        return nc

    for (tq0, tqn) in _chunks(S, 512):
        ATH, ATL = 0, QH * tqn // 2
        for hl, (src, dstoff, kk_) in enumerate(((atth, ATH, "wath"), (attl, ATL, "watl"))):
            asrc = src.rearrange("(kc q) t -> q kc t", q=128)[:, :, tq0:tq0 + tqn]
            adst = Pb(dstoff, QH * tqn).rearrange("q (kc j) -> q kc j", kc=QH)
            for b0 in range(0, QH, 8):
                b1 = min(b0 + 8, QH)
                p.op("sync", lambda e, src2=asrc[:, b0:b1, :], dst2=adst[:, b0:b1, :]:
                        e.dma_start(out=dst2, in_=src2),
                     reads=[("attns", 0, 1)], writes=[(kk_, 0, 1)], dma=True, tag="ph4_a")
        for og in range(D // 512):
            par = og % 2
            ps, pkey = (psA, "psA") if par == 0 else (psD, "psD")
            for k in range(QH):
                wsl = k % 2
                p.op("sync", lambda e, k=k, og=og, wsl=wsl: e.dma_start(
                        out=Wb(512 * wsl, 512),
                        in_=woh[128 * k:128 * (k + 1), 512 * og:512 * (og + 1)]),
                     reads=[("woh_s", 0, 1)],
                     writes=[(f"woh{wsl}", 0, 1)], dma=True, tag="ph4_w")
                p.op("sync", lambda e, k=k, og=og, wsl=wsl: e.dma_start(
                        out=Wb(512 * wsl + 256, 512),
                        in_=wol[128 * k:128 * (k + 1), 512 * og:512 * (og + 1)]),
                     reads=[("wol_s", 0, 1)],
                     writes=[(f"wol{wsl}", 0, 1)], dma=True, tag="ph4_w")
                for tt in range(tqn // 128):
                    for pi, (aof, akk, wof, wkk) in enumerate((
                            (ATH, "wath", 512 * wsl, f"woh{wsl}"),
                            (ATL, "watl", 512 * wsl, f"woh{wsl}"),
                            (ATH, "wath", 512 * wsl + 256, f"wol{wsl}"))):
                        p.op("tensor", lambda e, k=k, tt=tt, pi=pi, aof=aof, wof=wof,
                                ps=ps, tqn=tqn: e.matmul(
                                ps[:, 512 * tt:512 * (tt + 1)],
                                Pb(aof, QH * tqn)[:, tqn * k + 128 * tt:tqn * k + 128 * (tt + 1)],
                                Wb(wof, 512),
                                start=(k == 0 and pi == 0),
                                stop=(k == QH - 1 and pi == 2)),
                             reads=[(akk, 0, 1), (wkk, 0, 1)],
                             writes=[(pkey, 512 * tt, 512 * (tt + 1))], tag="ph4_mm")
            for tt in range(tqn // 128):
                psl = tt % 2
                p.op("vector", lambda e, tt=tt, psl=psl, ps=ps: e.tensor_copy(
                        W(1024 + 512 * psl, 512), ps[:, 512 * tt:512 * (tt + 1)]),
                     reads=[(pkey, 512 * tt, 512 * (tt + 1))],
                     writes=[(f"po{psl}", 0, 1)], tag="ph4_ev")
                qi = next(i for i, (q0, qn) in enumerate(_qch) if q0 <= tq0 < q0 + qn)
                rr = tq0 - _qch[qi][0]
                p.op("scalar", lambda e, qi=qi, rr=rr, og=og, tt=tt, psl=psl: e.dma_start(
                        out=partsq[qi][rr + 128 * tt:rr + 128 * (tt + 1),
                                       512 * og:512 * (og + 1)],
                        in_=W(1024 + 512 * psl, 512)),
                     reads=[(f"po{psl}", 0, 1)], writes=[(f"parts{qi}", 0, 1)],
                     dma=True, tag="ph4_st")
        qi = next(i for i, (q0, qn) in enumerate(_qch) if q0 <= tq0 < q0 + qn)
        if tq0 + tqn != _qch[qi][0] + _qch[qi][1]:
            continue_rs = True
        else:
            continue_rs = False
        if continue_rs:
            pass
        else:
            p.op("gpsimd", lambda e, qi=qi: e.collective_compute(
                    "ReduceScatter", ALU.add,
                    replica_groups=[[0, 1], [2, 3], [4, 5], [6, 7]],
                    ins=[partsq[qi][:]],
                    outs=[rsoutsq[qi][:]]),
                 reads=[(f"parts{qi}", 0, 1)],
                 writes=[(f"rsouts{qi}", 0, 1)], tag="ph5_rs")

    p.barrier()

    if stop_after < 6:
        _finish(p, nc, stack, y, pool, P, D, OT)
        return nc

    for tt in range(OT):
        x2k = ("x2", D * tt, D * (tt + 1))
        ms = 24 + 10 * (tt % 2)
        r = 128 * tt
        qi = next(i for i, (tq0, tqn) in enumerate(_qch)
                  if tq0 // 2 <= r < (tq0 + tqn) // 2)
        ro = r - _qch[qi][0] // 2
        p.op("sync", lambda e, tt=tt, qi=qi, ro=ro: e.dma_start(
                out=P(D * tt, D), in_=rsoutsq[qi][ro:ro + 128, :]),
             reads=[(f"rsouts{qi}", 0, 1)], writes=[x2k],
             dma=True, tag="ph6_rs")
        p.op("sync", lambda e, tt=tt: e.dma_start(
                out=W(2 * D, D), in_=xown[128 * tt:128 * (tt + 1), :]),
             writes=[("xo6", 0, 1)], dma=True, tag="ph6_xo")
        p.op("vector", lambda e, tt=tt: e.tensor_tensor(
                P(D * tt, D), P(D * tt, D), W(2 * D, D), ALU.add),
             reads=[x2k, ("xo6", 0, 1)], writes=[x2k], tag="ph6_add")
        sq = (tt + 1) % 2
        p.op("scalar", lambda e, tt=tt, ms=ms: e.activation(
                W(2 * D, D), P(D * tt, D), AF.Square, accum_out=sm[:, ms + 2:ms + 3]),
             reads=[x2k], writes=[("xo6", 0, 1), ("sx26", ms, ms + 1)], tag="ph6_sq")
        p.op("vector", lambda e, tt=tt, ms=ms: e.reduce_sum(
                sm[:, ms:ms + 1], P(D * tt, D), axis=AX.X),
             reads=[x2k], writes=[("sx6", ms, ms + 1)], tag="ph6_sum")
        p.op("vector", lambda e, ms=ms: e.tensor_scalar(
                sm[:, ms + 4:ms + 5], sm[:, ms:ms + 1], 1.0 / D, None, ALU.mult),
             reads=[("sx6", ms, ms + 1)], writes=[("mean6", ms, ms + 1)], tag="ph6_mean")
        p.op("vector", lambda e, ms=ms: e.tensor_tensor(
                sm[:, ms + 5:ms + 6], sm[:, ms + 4:ms + 5], sm[:, ms + 4:ms + 5], ALU.mult),
             reads=[("mean6", ms, ms + 1)], writes=[("msq6", ms, ms + 1)], tag="ph6_msq")
        p.op("vector", lambda e, ms=ms: e.tensor_scalar(
                sm[:, ms + 6:ms + 7], sm[:, ms + 2:ms + 3], 1.0 / D, None, ALU.mult),
             reads=[("sx26", ms, ms + 1)], writes=[("var6", ms, ms + 1)], tag="ph6_var0")
        p.op("vector", lambda e, ms=ms: e.tensor_scalar(
                sm[:, ms + 7:ms + 8], sm[:, ms + 6:ms + 7], sm[:, ms + 5:ms + 6],
                cfg.eps, ALU.subtract, ALU.add),
             reads=[("var6", ms, ms + 1), ("msq6", ms, ms + 1)],
             writes=[("vpe6", ms, ms + 1)], tag="ph6_var")
        p.op("scalar", lambda e, ms=ms: e.activation(
                sm[:, ms + 8:ms + 9], sm[:, ms + 7:ms + 8], AF.Ln),
             reads=[("vpe6", ms, ms + 1)], writes=[("lnv6", ms, ms + 1)], tag="ph6_ln")
        p.op("scalar", lambda e, ms=ms: e.activation(
                sm[:, ms + 9:ms + 10], sm[:, ms + 8:ms + 9], AF.Exp, scale=-0.5),
             reads=[("lnv6", ms, ms + 1)], writes=[("rstd6", ms, ms + 1)], tag="ph6_rstd")
        sx = tt % 2
        p.op("vector", lambda e, tt=tt, sx=sx, ms=ms: e.tensor_scalar(
                W(sx * D, D), P(D * tt, D), sm[:, ms + 4:ms + 5], sm[:, ms + 9:ms + 10],
                ALU.subtract, ALU.mult),
             reads=[x2k, ("mean6", ms, ms + 1), ("rstd6", ms, ms + 1)],
             writes=[(f"w6{sx}", 0, 1)], tag="ph6_apply")
        p.op("scalar", lambda e, sx=sx, sq=sq: e.copy(Wb(sq * D, D), W(sx * D, D)),
             reads=[(f"w6{sx}", 0, 1)], writes=[(f"w6{sq}", 0, 1)], tag="ph6_ch")
        p.op("vector", lambda e, sx=sx, sq=sq: e.tensor_tensor(
                Wb(sq * D + D // 2, D), W(sx * D, D), Wb(sq * D, D), ALU.subtract),
             reads=[(f"w6{sx}", 0, 1), (f"w6{sq}", 0, 1)],
             writes=[(f"w6{sq}", 0, 1)], tag="ph6_cl")
        for hl, (trt, trk) in enumerate(((psA, "psA"), (psD, "psD"))):
            srcoff = sq * D if hl == 0 else sq * D + D // 2
            fmoff = sx * D if hl == 0 else sx * D + D // 2
            for c in range(KC):
                bank = (c // 8) % 4
                slot = c % 8
                pcol = 512 * bank + 64 * slot
                p.op("tensor", lambda e, c=c, srcoff=srcoff, trt=trt, pcol=pcol: e.transpose(
                        trt[:, pcol:pcol + 64].bitcast(BF16),
                        Wb(srcoff, D)[:, 128 * c:128 * (c + 1)], s_bid[:]),
                     reads=[(f"w6{sq}", 0, 1), ("c_bid", 0, 1)],
                     writes=[(trk, pcol, pcol + 64)], tag="ph6_tr")
                if slot == 7 or c == KC - 1:
                    c0 = (c // 8) * 8
                    ncp = c - c0 + 1
                    p.op("vector", lambda e, c0=c0, ncp=ncp, bank=bank, trt=trt,
                            fmoff=fmoff: e.tensor_copy(
                            Wb(fmoff + 64 * c0, 128 * ncp),
                            trt[:, 512 * bank:512 * bank + 64 * ncp].bitcast(BF16)),
                         reads=[(trk, 512 * bank, 512 * bank + 64 * ncp)],
                         writes=[(f"w6{sx}", 0, 1)], tag="ph6_ev")
            dstf = (xn2h if hl == 0 else xn2l).rearrange("(c q) t -> q c t", q=128)[
                :, :, 128 * tt:128 * (tt + 1)]
            srcf = Wb(fmoff, D).rearrange("q (c j) -> q c j", c=KC)
            for c0 in range(0, KC, 16):
                c1 = min(c0 + 16, KC)
                p.op("scalar", lambda e, dst=dstf[:, c0:c1, :], src=srcf[:, c0:c1, :]:
                        e.dma_start(out=dst, in_=src),
                     reads=[(f"w6{sx}", 0, 1)], writes=[("xn2s", 0, 1)],
                     dma=True, tag="ph6_st")

    p.barrier()

    if stop_after < 7:
        _finish(p, nc, stack, y, pool, P, D, OT)
        return nc

    NW = OWN
    A0 = 2 * NW
    LG = A0 + 512
    WFM = LG + NW
    WBIG = WFM + NW
    MIDS = WBIG + NW
    MIDSP = WKC - NW
    HW_ = NW // 2
    assert MIDSP >= MIDS + NW and MIDSP >= 2 * NW + 1536 + NW

    def fm_loads(k, msl, tagp):
        p.op("scalar", lambda e, k=k, msl=msl: e.dma_start(
                out=Wb(msl * NW, NW), in_=xn2h[128 * k:128 * (k + 1), :]),
             reads=[("xn2s", 0, 1)], writes=[(f"fmh{msl}", 0, 1)], dma=True, tag=tagp)
        p.op("scalar", lambda e, k=k, msl=msl: e.dma_start(
                out=Wb(msl * NW + HW_, NW), in_=xn2l[128 * k:128 * (k + 1), :]),
             reads=[("xn2s", 0, 1)], writes=[(f"fml{msl}", 0, 1)], dma=True, tag=tagp)

    for k in range(KC):
        msl = k % 2
        fm_loads(k, msl, "ph7_x")
        p.op("sync", lambda e, k=k, msl=msl: e.dma_start(
                out=Wb(A0 + 136 * msl, 128), in_=ath[128 * k:128 * (k + 1), :]),
             reads=[("ath_s", 0, 1)], writes=[(f"ath{msl}", 0, 1)], dma=True, tag="ph7_a")
        p.op("sync", lambda e, k=k, msl=msl: e.dma_start(
                out=Wb(A0 + 136 * msl + 64, 128), in_=atl[128 * k:128 * (k + 1), :]),
             reads=[("atl_s", 0, 1)], writes=[(f"atl{msl}", 0, 1)], dma=True, tag="ph7_a")
        p.op("sync", lambda e, k=k, msl=msl: e.dma_start(
                out=Wb(A0 + 136 * msl + 128, 8), in_=rth[128 * k:128 * (k + 1), :]),
             reads=[("rth_s", 0, 1)], writes=[(f"rth{msl}", 0, 1)], dma=True, tag="ph7_r")
        p.op("sync", lambda e, k=k, msl=msl: e.dma_start(
                out=Wb(A0 + 136 * msl + 132, 8), in_=rtl[128 * k:128 * (k + 1), :]),
             reads=[("rtl_s", 0, 1)], writes=[(f"rtl{msl}", 0, 1)], dma=True, tag="ph7_r")
        for (t0, n) in _chunks(NW, 512):
            for pi, (wof, wkk, mof, mkk) in enumerate((
                    (A0 + 136 * msl, f"ath{msl}", msl * NW, f"fmh{msl}"),
                    (A0 + 136 * msl, f"ath{msl}", msl * NW + HW_, f"fml{msl}"),
                    (A0 + 136 * msl + 64, f"atl{msl}", msl * NW, f"fmh{msl}"))):
                p.op("tensor", lambda e, k=k, t0=t0, n=n, pi=pi, wof=wof, mof=mof: e.matmul(
                        psA[:, t0:t0 + n], Wb(wof, 128),
                        Wb(mof, NW)[:, t0:t0 + n],
                        start=(k == 0 and pi == 0), stop=(k == KC - 1 and pi == 2)),
                     reads=[(wkk, 0, 1), (mkk, 0, 1)],
                     writes=[("psA", t0, t0 + n)], tag="ph7_mmA")
            for pi, (wof, wkk, mof, mkk) in enumerate((
                    (A0 + 136 * msl + 128, f"rth{msl}", msl * NW, f"fmh{msl}"),
                    (A0 + 136 * msl + 128, f"rth{msl}", msl * NW + HW_, f"fml{msl}"),
                    (A0 + 136 * msl + 132, f"rtl{msl}", msl * NW, f"fmh{msl}"))):
                p.op("tensor", lambda e, k=k, t0=t0, n=n, pi=pi, wof=wof, mof=mof: e.matmul(
                        psA[0:8, 1024 + t0:1024 + t0 + n], Wb(wof, 8),
                        Wb(mof, NW)[:, t0:t0 + n],
                        start=(k == 0 and pi == 0), stop=(k == KC - 1 and pi == 2)),
                     reads=[(wkk, 0, 1), (mkk, 0, 1)],
                     writes=[("psA", 1024 + t0, 1024 + t0 + n)], tag="ph7_mmR")

    p.op("vector", lambda e: e.tensor_copy(W(LG, NW)[0:8, :], psA[0:8, 1024:1024 + NW]),
         reads=[("psA", 1024, 1024 + NW)], writes=[("lg", 0, 1)], tag="ph7_lgev")
    for tt in range(OT):
        p.op("tensor", lambda e, tt=tt: e.transpose(
                psD[:, 8 * tt:8 * (tt + 1)], W(LG, NW)[0:8, 128 * tt:128 * (tt + 1)],
                s_id[0:8, 0:8]),
             reads=[("lg", 0, 1), ("c_id", 0, 1)],
             writes=[("psD", 8 * tt, 8 * (tt + 1))], tag="ph7_lgtr")
    p.op("vector", lambda e: e.tensor_copy(lgsm[:, 0:8 * OT], psD[:, 0:8 * OT]),
         reads=[("psD", 0, 8 * OT)], writes=[("lgt", 0, 8 * OT)], tag="ph7_lgtev")
    T1 = 8 * OT
    T2 = 8 * OT + 8
    WD = 8 * OT + 16
    for tt in range(OT):
        lgt = lgsm[:, 8 * tt:8 * (tt + 1)]
        lk = ("lgt", 8 * tt, 8 * (tt + 1))
        p.op("vector", lambda e, lgt=lgt: e.reduce_max(sm[:, 34:35], lgt, axis=AX.X),
             reads=[lk], writes=[("m1", 0, 1)], tag="ph7_m1")
        p.op("vector", lambda e, lgt=lgt: e.tensor_scalar(
                lgsm[:, T1:T1 + 8], lgt, sm[:, 34:35], -1e30, ALU.is_equal, ALU.mult),
             reads=[lk, ("m1", 0, 1)], writes=[("t1", 0, 1)], tag="ph7_t1")
        p.op("vector", lambda e, lgt=lgt: e.tensor_tensor(
                lgsm[:, T2:T2 + 8], lgt, lgsm[:, T1:T1 + 8], ALU.add),
             reads=[lk, ("t1", 0, 1)], writes=[("t2", 0, 1)], tag="ph7_t2")
        p.op("vector", lambda e: e.reduce_max(sm[:, 35:36], lgsm[:, T2:T2 + 8], axis=AX.X),
             reads=[("t2", 0, 1)], writes=[("m2", 0, 1)], tag="ph7_m2")
        p.op("vector", lambda e: e.tensor_tensor(
                sm[:, 36:37], sm[:, 35:36], sm[:, 34:35], ALU.subtract),
             reads=[("m1", 0, 1), ("m2", 0, 1)], writes=[("dd", 0, 1)], tag="ph7_d")
        p.op("scalar", lambda e: e.activation(sm[:, 37:38], sm[:, 36:37], AF.Exp),
             reads=[("dd", 0, 1)], writes=[("e01", 0, 1)], tag="ph7_e")
        p.op("vector", lambda e: e.tensor_scalar(sm[:, 38:39], sm[:, 37:38], 1.0, None, ALU.add),
             reads=[("e01", 0, 1)], writes=[("t3", 0, 1)], tag="ph7_t3")
        p.op("vector", lambda e: e.reciprocal(sm[:, 39:40], sm[:, 38:39]),
             reads=[("t3", 0, 1)], writes=[("w0r", 0, 1)], tag="ph7_w0r")
        p.op("vector", lambda e: e.tensor_scalar(sm[:, 40:41], sm[:, 39:40], 2.0, None, ALU.mult),
             reads=[("w0r", 0, 1)], writes=[("w0", 0, 1)], tag="ph7_w0")
        p.op("vector", lambda e: e.tensor_scalar(sm[:, 41:42], sm[:, 40:41], -1.0, 2.0, ALU.mult, ALU.add),
             reads=[("w0", 0, 1)], writes=[("w1", 0, 1)], tag="ph7_w1")
        p.op("vector", lambda e, lgt=lgt: e.tensor_scalar(
                lgsm[:, T1:T1 + 8], lgt, sm[:, 34:35], sm[:, 40:41], ALU.is_equal, ALU.mult),
             reads=[lk, ("m1", 0, 1), ("w0", 0, 1)], writes=[("t1", 0, 1)], tag="ph7_c0")
        p.op("vector", lambda e, lgt=lgt: e.tensor_scalar(
                lgsm[:, T2:T2 + 8], lgt, sm[:, 35:36], sm[:, 41:42], ALU.is_equal, ALU.mult),
             reads=[lk, ("m2", 0, 1), ("w1", 0, 1)], writes=[("t2", 0, 1)], tag="ph7_c1")
        p.op("vector", lambda e, tt=tt: e.tensor_tensor(
                lgsm[:, WD + 8 * tt:WD + 8 * (tt + 1)],
                lgsm[:, T1:T1 + 8], lgsm[:, T2:T2 + 8], ALU.add),
             reads=[("t1", 0, 1), ("t2", 0, 1)],
             writes=[("wd", 8 * tt, 8 * (tt + 1))], tag="ph7_wd")
        p.op("tensor", lambda e, tt=tt: e.transpose(
                psD[0:8, 1024 + 128 * tt:1024 + 128 * (tt + 1)],
                lgsm[:, WD + 8 * tt:WD + 8 * (tt + 1)], s_id[:]),
             reads=[("wd", 8 * tt, 8 * (tt + 1)), ("c_id", 0, 1)],
             writes=[("psD", 1024 + 128 * tt, 1024 + 128 * (tt + 1))], tag="ph7_wtr")
    p.op("vector", lambda e: e.tensor_copy(W(WFM, NW)[0:8, :], psD[0:8, 1024:1024 + NW]),
         reads=[("psD", 1024, 1024 + NW)], writes=[("wfm", 0, 1)], tag="ph7_wfev")
    for (t0, n) in _chunks(NW, 512):
        p.op("tensor", lambda e, t0=t0, n=n: e.matmul(
                psD[:, t0:t0 + n], s_bc[:], W(WFM, NW)[0:8, t0:t0 + n],
                start=True, stop=True),
             reads=[("wfm", 0, 1), ("c_bc", 0, 1)],
             writes=[("psD", t0, t0 + n)], tag="ph7_bc")
    p.op("vector", lambda e: e.tensor_copy(W(WBIG, NW), psD[:, 0:NW]),
         reads=[("psD", 0, NW)], writes=[("wbig", 0, 1)], tag="ph7_bigev")
    p.op("vector", lambda e: e.tensor_tensor(
            W(MIDS, NW), psA[:, 0:NW], W(WBIG, NW), ALU.mult),
         reads=[("psA", 0, NW), ("wbig", 0, 1)], writes=[("mids", 0, 1)], tag="ph7_mids")
    p.op("scalar", lambda e: e.copy(Wb(MIDSP, NW), W(MIDS, NW)),
         reads=[("mids", 0, 1)], writes=[("midsh", 0, 1)], tag="ph7_mh")
    p.op("vector", lambda e: e.tensor_tensor(
            Wb(MIDSP + HW_, NW), W(MIDS, NW), Wb(MIDSP, NW), ALU.subtract),
         reads=[("mids", 0, 1), ("midsh", 0, 1)], writes=[("midsl", 0, 1)], tag="ph7_ml")

    p.barrier()

    MB = 2 * OWN
    LB = MB + 1024
    for og in range(D // 512):
        for k in range(KC):
            msl = k % 2
            p.op("scalar", lambda e, k=k, msl=msl: e.dma_start(
                    out=Wb(OWN * msl, OWN), in_=xn2h[128 * k:128 * (k + 1), :]),
                 reads=[("xn2s", 0, 1)], writes=[(f"xh{msl}", 0, 1)], dma=True, tag="ph7b_x")
            p.op("scalar", lambda e, k=k, msl=msl: e.dma_start(
                    out=Wb(OWN * msl + OWN // 2, OWN), in_=xn2l[128 * k:128 * (k + 1), :]),
                 reads=[("xn2s", 0, 1)], writes=[(f"xl{msl}", 0, 1)], dma=True, tag="ph7b_x")
            p.op("sync", lambda e, k=k, og=og, msl=msl: e.dma_start(
                    out=Wb(MB + 512 * msl, 512),
                    in_=bah[128 * k:128 * (k + 1), 512 * og:512 * (og + 1)]),
                 writes=[(f"bmh{msl}", 0, 1)], dma=True, tag="ph7b_w")
            p.op("sync", lambda e, k=k, og=og, msl=msl: e.dma_start(
                    out=Wb(MB + 512 * msl + 256, 512),
                    in_=bal[128 * k:128 * (k + 1), 512 * og:512 * (og + 1)]),
                 writes=[(f"bml{msl}", 0, 1)], dma=True, tag="ph7b_w")
            for tt in range(OT):
                ps, pkey = (psA, "psA") if tt % 2 == 0 else (psD, "psD")
                poff = 512 * ((tt // 2) % 4)
                for pi, (xof, xkk, wof, wkk) in enumerate((
                        (OWN * msl, f"xh{msl}", MB + 512 * msl, f"bmh{msl}"),
                        (OWN * msl + OWN // 2, f"xl{msl}", MB + 512 * msl, f"bmh{msl}"),
                        (OWN * msl, f"xh{msl}", MB + 512 * msl + 256, f"bml{msl}"))):
                    p.op("tensor", lambda e, k=k, tt=tt, pi=pi, xof=xof, wof=wof,
                            ps=ps, poff=poff: e.matmul(
                            ps[:, poff:poff + 512],
                            Wb(xof, OWN)[:, 128 * tt:128 * (tt + 1)],
                            Wb(wof, 512),
                            start=(k == 0 and pi == 0), stop=False),
                         reads=[(xkk, 0, 1), (wkk, 0, 1)],
                         writes=[(pkey, poff, poff + 512)], tag="ph7b_mm")
        p.op("sync", lambda e, og=og: e.dma_start(
                out=Wb(LB, 512), in_=bfh[:, 512 * og:512 * (og + 1)]),
             writes=[("lbh", 0, 1)], dma=True, tag="ph7b_lb")
        p.op("sync", lambda e, og=og: e.dma_start(
                out=Wb(LB + 256, 512), in_=bfl[:, 512 * og:512 * (og + 1)]),
             writes=[("lbl", 0, 1)], dma=True, tag="ph7b_lb")
        for tt in range(OT):
            ps, pkey = (psA, "psA") if tt % 2 == 0 else (psD, "psD")
            poff = 512 * ((tt // 2) % 4)
            for pi, (xof, xkk, wof, wkk) in enumerate((
                    (MIDSP, "midsh", LB, "lbh"),
                    (MIDSP + HW_, "midsl", LB, "lbh"),
                    (MIDSP, "midsh", LB + 256, "lbl"))):
                p.op("tensor", lambda e, tt=tt, pi=pi, xof=xof, wof=wof,
                        ps=ps, poff=poff: e.matmul(
                        ps[:, poff:poff + 512],
                        Wb(xof, NW)[:, 128 * tt:128 * (tt + 1)],
                        Wb(wof, 512),
                        start=False, stop=(pi == 2)),
                     reads=[(xkk, 0, 1), (wkk, 0, 1)],
                     writes=[(pkey, poff, poff + 512)], tag="ph7b_mlb")
        for tt in range(OT):
            ps, pkey = (psA, "psA") if tt % 2 == 0 else (psD, "psD")
            poff = 512 * ((tt // 2) % 4)
            p.op("vector", lambda e, tt=tt, og=og, ps=ps, poff=poff: e.tensor_tensor(
                    P(D * tt + 512 * og, 512), P(D * tt + 512 * og, 512),
                    ps[:, poff:poff + 512], ALU.add),
                 reads=[(pkey, poff, poff + 512),
                        ("x2", D * tt + 512 * og, D * tt + 512 * (og + 1))],
                 writes=[("x2", D * tt + 512 * og, D * tt + 512 * (og + 1))], tag="ph7b_add")

    _finish(p, nc, stack, y, pool, P, D, OT)
    return nc



def make_in_maps(cfg, x, Wqkv, Wo, base_W, router_W, lora_A, lora_B):
    D, S, NQ, NKV = cfg.D, cfg.S, cfg.NQ, cfg.NKV
    QH, KH = cfg.QH, cfg.KH
    C2, S2 = _rope_tables(cfg)
    ident = np.eye(128, dtype=np.float32)
    import ml_dtypes
    identb = np.eye(128, dtype=ml_dtypes.bfloat16)
    msk = np.where(np.arange(128)[None, :] > np.arange(128)[:, None],
                   np.float32(-1e30), np.float32(0.0)).astype(np.float32)
    rmat = np.zeros((128, 128), dtype=np.float32)
    for m in range(64):
        rmat[m + 64, m] = -1.0
        rmat[m, m + 64] = 1.0
    bc = np.zeros((8, 128), dtype=np.float32)
    for e in range(8):
        bc[e, 16 * e:16 * (e + 1)] = 1.0

    WqkvT = np.ascontiguousarray(Wqkv.T)
    WoT = np.ascontiguousarray(Wo.T)
    baseT = np.ascontiguousarray(base_W.T)
    routT = np.ascontiguousarray(router_W.T)
    aT = np.ascontiguousarray(lora_A.transpose(1, 0, 2).reshape(D, 128))
    bflat = np.ascontiguousarray(lora_B.reshape(128, D))

    qd, kd = NQ * 128, NKV * 128
    in_maps = []
    for c in range(8):
        b, h = c // 2, c % 2
        wq = WqkvT[:, 128 * QH * h:128 * QH * (h + 1)]
        wk_ = WqkvT[:, qd + 128 * KH * h: qd + 128 * KH * (h + 1)]
        wv = WqkvT[:, qd + kd + 128 * KH * h: qd + kd + 128 * KH * (h + 1)]
        m = {
            "xb": np.ascontiguousarray(x[b]),
            "xown": np.ascontiguousarray(x[b, (S // 2) * h:(S // 2) * (h + 1)]),
            "wqkvT": np.ascontiguousarray(np.concatenate([wq, wk_, wv], axis=1)),
            "woT": np.ascontiguousarray(WoT[128 * QH * h:128 * QH * (h + 1), :]),
            "baseT": baseT, "routT": routT, "aT": aT, "bflat": bflat,
            "ropeC": C2, "ropeS": S2, "cid": ident, "cbid": identb, "cmsk": msk,
            "crmat": rmat, "cbc": bc,
        }
        in_maps.append(m)
    return in_maps


def kernel(L0_attn_hidden_states, L0_attn_Wqkv, L0_attn_Wo,
           L0_ffn_base_W, L0_ffn_router_W, L0_ffn_lora_A, L0_ffn_lora_B):
    cfg = Cfg()
    x = np.ascontiguousarray(np.asarray(L0_attn_hidden_states, dtype=np.float32))
    in_maps = make_in_maps(cfg, x,
                           np.asarray(L0_attn_Wqkv, dtype=np.float32),
                           np.asarray(L0_attn_Wo, dtype=np.float32),
                           np.asarray(L0_ffn_base_W, dtype=np.float32),
                           np.asarray(L0_ffn_router_W, dtype=np.float32),
                           np.asarray(L0_ffn_lora_A, dtype=np.float32),
                           np.asarray(L0_ffn_lora_B, dtype=np.float32))
    nc = build_program(cfg)
    res = run_bass_kernel_spmd(nc, in_maps, core_ids=list(range(8)))
    out = np.empty((4, cfg.S, cfg.D), dtype=np.float32)
    half = cfg.S // 2
    for c in range(8):
        b, h = c // 2, c % 2
        out[b, half * h:half * (h + 1)] = res.results[c]["y"]
    return out


# version 23
# speedup vs baseline: 1.0162x; 1.0162x over previous
import numpy as np
import concourse.bass as bass
import concourse.mybir as mybir
from concourse.bass_utils import run_bass_kernel_spmd
from concourse.alu_op_type import AluOpType as ALU

F32 = mybir.dt.float32
BF16 = mybir.dt.bfloat16
AF = mybir.ActivationFunctionType
AX = mybir.AxisListType

ENGINES = ("sync", "tensor", "vector", "scalar", "gpsimd")
DMA_ENGINES = ("sync", "scalar", "gpsimd")



class _Op:
    __slots__ = ("eng", "fn", "deps", "inc", "seq", "dma", "dma_idx", "dkey",
                 "idx", "tag", "bar_waits")

    def __init__(self, eng, fn, dma, idx, tag):
        self.eng, self.fn, self.dma, self.idx, self.tag = eng, fn, dma, idx, tag
        self.deps = set()
        self.inc = False
        self.seq = 0
        self.dma_idx = -1
        self.dkey = None
        self.bar_waits = None


class Prog:
    def __init__(self):
        self.ops = []
        self.state = {}
        self.dma_count = {}

    def _st(self, key):
        return self.state.setdefault(key, {"w": [], "r": []})

    @staticmethod
    def _bank(key, lo, hi):
        if key.startswith("ps"):
            return (lo // 512) * 512, ((hi + 511) // 512) * 512
        return lo, hi

    def op(self, eng, fn, reads=(), writes=(), dma=False, tag=""):
        o = _Op(eng, fn, dma, len(self.ops), tag)
        reads = [(k,) + self._bank(k, lo, hi) for (k, lo, hi) in reads]
        writes = [(k,) + self._bank(k, lo, hi) for (k, lo, hi) in writes]
        deps = set()
        for (k, lo, hi) in reads:
            st = self._st(k)
            for (l, h, w) in st["w"]:
                if l < hi and lo < h:
                    deps.add(w)
        for (k, lo, hi) in writes:
            st = self._st(k)
            for (l, h, w) in st["w"]:
                if l < hi and lo < h:
                    deps.add(w)
            for (l, h, r) in st["r"]:
                if l < hi and lo < h:
                    deps.add(r)
        deps.discard(o)
        for (k, lo, hi) in reads:
            self._st(k)["r"].append((lo, hi, o))
        for (k, lo, hi) in writes:
            st = self._st(k)
            st["w"] = [e_ for e_ in st["w"] if not (lo <= e_[0] and e_[1] <= hi)]
            st["r"] = [e_ for e_ in st["r"] if not (lo <= e_[0] and e_[1] <= hi)]
            st["w"].append((lo, hi, o))
        o.deps = deps
        if dma:
            slot = writes[0][0] if writes else "anon"
            if reads and tag.endswith(("_st", "_stq", "_stv", "_sth", "_stl")):
                slot = reads[0][0]
            o.dkey = (eng, slot)
            o.dma_idx = self.dma_count.get(o.dkey, 0)
            self.dma_count[o.dkey] = o.dma_idx + 1
        self.ops.append(o)
        return o

    def barrier(self):
        snap_d = dict(self.dma_count)
        for e in ENGINES:
            o = _Op(e, lambda eng: eng.nop(), False, len(self.ops), "barrier")
            o.bar_waits = snap_d
            self.ops.append(o)
        self.state = {}

    def finalize(self):
        for o in self.ops:
            for d in o.deps:
                if d.dma:
                    continue
                if d.eng == "tensor" and o.eng == "tensor":
                    continue
                d.inc = True
        lastop = {}
        for o in self.ops:
            if o.bar_waits is not None:
                for e2, lo in lastop.items():
                    if not lo.dma:
                        lo.inc = True
                    o.deps.add(lo)
            else:
                lastop[o.eng] = o
        seq = {e: 0 for e in ENGINES}
        for o in self.ops:
            if o.inc and not o.dma:
                seq[o.eng] += 1
            o.seq = seq[o.eng]
        per_eng = {e: [] for e in ENGINES}
        for o in self.ops:
            per_eng[o.eng].append(o)
        self.per_eng = per_eng

    def emit(self, eng_name, eng, psems, dsems):
        water = {}
        for o in self.per_eng[eng_name]:
            waits = {}
            for d in o.deps:
                if d.eng == "tensor" and o.eng == "tensor" and not d.dma:
                    continue
                if d.dma:
                    key, val = ("d", d.dkey), 16 * (d.dma_idx + 1)
                else:
                    key, val = ("p", d.eng), d.seq
                if waits.get(key, 0) < val:
                    waits[key] = val
            if o.bar_waits is not None:
                for dk, cnt in o.bar_waits.items():
                    k = ("d", dk)
                    if waits.get(k, 0) < 16 * cnt:
                        waits[k] = 16 * cnt
            for key, val in sorted(waits.items(), key=str):
                if water.get(key, 0) >= val:
                    continue
                water[key] = val
                sem = dsems[key[1]] if key[0] == "d" else psems[key[1]]
                eng.wait_ge(sem, val)
            ins = o.fn(eng)
            if o.dma:
                ins.then_inc(dsems[o.dkey], 16)
            elif o.inc:
                ins.then_inc(psems[o.eng], 1)


def _finish(p, nc, stack, y, pool, P, D, OT):
    for tt in range(OT):
        p.op("gpsimd", lambda e, tt=tt: e.dma_start(
                out=y[128 * tt:128 * (tt + 1), :], in_=P(D * tt, D)),
             reads=[("x2", D * tt, D * (tt + 1))], writes=[("ydram", 0, 1)],
             dma=True, tag="ph8_st")
    p.barrier()
    p.finalize()
    psems = {e: stack.enter_context(nc.semaphore("p_" + e)) for e in ENGINES}
    dsems = {dk: stack.enter_context(nc.semaphore(f"d_{dk[0]}_{dk[1]}"))
             for dk in p.dma_count}
    with nc.Block() as block:
        @block.sync
        def _(eng):
            p.emit("sync", eng, psems, dsems)

        @block.tensor
        def _(eng):
            p.emit("tensor", eng, psems, dsems)

        @block.vector
        def _(eng):
            p.emit("vector", eng, psems, dsems)

        @block.scalar
        def _(eng):
            p.emit("scalar", eng, psems, dsems)

        @block.gpsimd
        def _(eng):
            p.emit("gpsimd", eng, psems, dsems)
    nc._stack = stack
    nc._prog = p



class Cfg:
    def __init__(self, D=4096, S=2048, NQ=32, NKV=8):
        self.D, self.S, self.NQ, self.NKV = D, S, NQ, NKV
        self.HD = 128
        self.E, self.R = 8, 16
        assert NQ // NKV == 4
        self.QH = NQ // 2
        self.KH = NKV // 2
        self.OD = (self.QH + 2 * self.KH) * 128
        self.KC = D // 128
        self.TT = S // 128
        self.OWN = S // 2
        self.OT = self.OWN // 128
        self.eps = 1e-5
        self.scale = float(1.0 / np.sqrt(128.0))


def _rope_tables(cfg):
    j = np.arange(64, dtype=np.float64)
    inv = 1.0 / (10000.0 ** (2.0 * j / 128.0))
    t = np.arange(cfg.S, dtype=np.float64)
    ang = np.outer(inv, t)
    C2 = np.concatenate([np.cos(ang)] * 2, axis=0).astype(np.float32)
    S2 = np.concatenate([np.sin(ang)] * 2, axis=0).astype(np.float32)
    return C2, S2


def _chunks(total, step):
    out = []
    o = 0
    while o < total:
        n = min(step, total - o)
        out.append((o, n))
        o += n
    return out



def build_program(cfg, stop_after=99):
    nc = bass.Bass("TRN2", target_bir_lowering=False)
    D, S = cfg.D, cfg.S
    KC, TT, OWN, OT = cfg.KC, cfg.TT, cfg.OWN, cfg.OT
    QH, KH, OD = cfg.QH, cfg.KH, cfg.OD
    OC = OD // 128

    def inp(name, shape, dt=F32):
        return nc.dram_tensor(name, list(shape), dt, kind="ExternalInput").ap()

    xb = inp("xb", (S, D))
    xown = inp("xown", (OWN, D))
    wqkvT = inp("wqkvT", (D, OD))
    woT = inp("woT", (QH * 128, D))
    baseT = inp("baseT", (D, D))
    routT = inp("routT", (D, 8))
    aT = inp("aT", (D, 128))
    bflat = inp("bflat", (128, D))
    ropeC = inp("ropeC", (128, S))
    ropeS = inp("ropeS", (128, S))
    cid = inp("cid", (128, 128))
    cbid = inp("cbid", (128, 128), BF16)
    cmsk = inp("cmsk", (128, 128))
    crmat = inp("crmat", (128, 128))
    cbc = inp("cbc", (8, 128))

    y = nc.dram_tensor("y", [OWN, D], F32, kind="ExternalOutput").ap()

    def scr(name, shape, dt=F32):
        return nc.dram_tensor(name, list(shape), dt, kind="Internal").ap()

    wqh, wql = scr("wqh", (D, OD), BF16), scr("wql", (D, OD), BF16)
    woh, wol = scr("woh", (QH * 128, D), BF16), scr("wol", (QH * 128, D), BF16)
    bah, bal = scr("bah", (D, D), BF16), scr("bal", (D, D), BF16)
    ath, atl = scr("ath", (D, 128), BF16), scr("atl", (D, 128), BF16)
    rth, rtl = scr("rth", (D, 8), BF16), scr("rtl", (D, 8), BF16)
    bfh, bfl = scr("bfh", (128, D), BF16), scr("bfl", (128, D), BF16)
    xn1h, xn1l = scr("xn1h", (D, S), BF16), scr("xn1l", (D, S), BF16)
    qkh = scr("qkh", ((QH + KH) * 128, S), BF16)
    qkl = scr("qkl", ((QH + KH) * 128, S), BF16)
    vtms = scr("vtms", (S, KH * 128))
    atth, attl = scr("atth", (QH * 128, S), BF16), scr("attl", (QH * 128, S), BF16)
    _qch = [(0, S)]
    partsq = [scr(f"parts{i}", (tqn, D)) for i, (tq0, tqn) in enumerate(_qch)]
    rsoutsq = [scr(f"rsouts{i}", (tqn // 2, D)) for i, (tq0, tqn) in enumerate(_qch)]
    xn2h, xn2l = scr("xn2h", (D, OWN), BF16), scr("xn2l", (D, OWN), BF16)

    import contextlib
    stack = contextlib.ExitStack()
    sb = lambda name, shape, dt=F32: stack.enter_context(nc.sbuf_tensor(name, list(shape), dt))

    s_id = sb("s_id", (128, 128))
    s_bid = sb("s_bid", (128, 128), BF16)
    s_msk = sb("s_msk", (128, 128))
    s_R = sb("s_R", (128, 128))
    s_bc = sb("s_bc", (8, 128))
    s_C = sb("s_C", (128, S))
    s_S = sb("s_S", (128, S))

    SPL = min(3072, max(OD, D))
    PH0NEED = 512 + 2 * SPL + SPL
    POOLC = max(OT * D, 6 * D, 12 * S, 2 * QH * 512, 4096)
    WKC = max(3 * D, PH0NEED, 7 * OWN + 512, 4096)
    pool = sb("pool", (128, POOLC))
    wk = sb("wk", (128, WKC))
    sm = sb("sm", (128, 64))
    lgsm = sb("lgsm", (128, 8 * OT * 2 + 32))

    psA = stack.enter_context(nc.psum_tensor("psA", [128, 2048], F32))
    psD = stack.enter_context(nc.psum_tensor("psD", [128, 2048], F32))

    p = Prog()

    def P(lo, n):
        return pool[:, lo:lo + n]

    def Pb(lo, nb):
        return pool[:, lo:lo + nb // 2].bitcast(BF16)

    def W(lo, n):
        return wk[:, lo:lo + n]

    def Wb(lo, nb):
        return wk[:, lo:lo + nb // 2].bitcast(BF16)

    for t, src, key in ((s_id, cid, "c_id"), (s_bid, cbid, "c_bid"),
                        (s_msk, cmsk, "c_msk"), (s_R, crmat, "c_R"),
                        (s_bc, cbc, "c_bc"), (s_C, ropeC, "c_C"),
                        (s_S, ropeS, "c_S")):
        p.op("sync", lambda e, t=t, src=src: e.dma_start(out=t[:], in_=src[:]),
             writes=[(key, 0, 1)], dma=True, tag="const")

    def split_jobs(src, dsth, dstl, R, C, base, tg, eng=("scalar", "vector")):
        jobs = []
        ceng, seng = eng
        for ri, r0 in enumerate(range(0, R, 128)):
            for (c0, cn) in _chunks(C, SPL):
                def job(r0=r0, c0=c0, cn=cn, sl=ri % 2, src=src, dsth=dsth,
                        dstl=dstl, base=base, tg=tg):
                    ik = (f"sp_i{sl}", 0, 1)
                    ldq = "gpsimd" if ceng == "gpsimd" else "sync"
                    p.op(ldq, lambda e: e.dma_start(
                            out=W(base + sl * SPL, cn), in_=src[r0:r0 + 128, c0:c0 + cn]),
                         writes=[ik], dma=True, tag=f"{tg}_ld")
                    for (p0, pn) in _chunks(cn, 1024):
                        def _cast(e, p0=p0, pn=pn):
                            dst = Wb(base + 2 * SPL + p0 // 2, pn)
                            srcv = W(base + sl * SPL + p0, pn)
                            return e.copy(dst, srcv) if ceng == "scalar" else e.tensor_copy(dst, srcv)
                        p.op(ceng, _cast,
                             reads=[ik], writes=[("sp_h", 0, 1)], tag=f"{tg}_hi")
                        p.op(seng, lambda e, p0=p0, pn=pn: e.tensor_tensor(
                                Wb(base + 2 * SPL + SPL // 2 + p0 // 2, pn),
                                W(base + sl * SPL + p0, pn),
                                Wb(base + 2 * SPL + p0 // 2, pn), ALU.subtract),
                             reads=[ik, ("sp_h", 0, 1)], writes=[("sp_l", 0, 1)], tag=f"{tg}_lo")
                    p.op("gpsimd", lambda e: e.dma_start(
                            out=dsth[r0:r0 + 128, c0:c0 + cn], in_=Wb(base + 2 * SPL, cn)),
                         reads=[("sp_h", 0, 1)], writes=[("wsp", 0, 1)], dma=True, tag=f"{tg}_sth")
                    p.op("gpsimd", lambda e: e.dma_start(
                            out=dstl[r0:r0 + 128, c0:c0 + cn], in_=Wb(base + 2 * SPL + SPL // 2, cn)),
                         reads=[("sp_l", 0, 1)], writes=[("wsp", 0, 1)], dma=True, tag=f"{tg}_stl")
                jobs.append(job)
        return jobs

    for j_ in split_jobs(wqkvT, wqh, wql, D, OD, 0, "ph0a", ("scalar", "gpsimd")):
        j_()

    XH, XL, FH, FL = 4 * D, 4 * D + D // 2, 5 * D, 5 * D + D // 2
    for tt in range(TT):
        sl = tt % 2
        xtm = (f"xtm{sl}", 0, 1)
        p.op("sync", lambda e, tt=tt, sl=sl: e.dma_start(
                out=P(sl * D, D), in_=xb[128 * tt:128 * (tt + 1), :]),
             writes=[xtm], dma=True, tag="ph1_load")
        ms = tt % 2
        p.op("scalar", lambda e, sl=sl, ms=ms: e.activation(
                P(2 * D, D), P(sl * D, D), AF.Square, accum_out=sm[:, 2 + ms:3 + ms]),
             reads=[xtm], writes=[("xsq", 0, 1), ("sx2", ms, ms + 1)], tag="ph1_sq")
        p.op("vector", lambda e, sl=sl, ms=ms: e.reduce_sum(
                sm[:, 0 + ms:1 + ms], P(sl * D, D), axis=AX.X),
             reads=[xtm], writes=[("sx", ms, ms + 1)], tag="ph1_sum")
        p.op("vector", lambda e, ms=ms: e.tensor_scalar(
                sm[:, 4 + ms:5 + ms], sm[:, 0 + ms:1 + ms], 1.0 / D, None, ALU.mult),
             reads=[("sx", ms, ms + 1)], writes=[("mean", ms, ms + 1)], tag="ph1_mean")
        p.op("vector", lambda e, ms=ms: e.tensor_tensor(
                sm[:, 6 + ms:7 + ms], sm[:, 4 + ms:5 + ms], sm[:, 4 + ms:5 + ms], ALU.mult),
             reads=[("mean", ms, ms + 1)], writes=[("msq", ms, ms + 1)], tag="ph1_msq")
        p.op("vector", lambda e, ms=ms: e.tensor_scalar(
                sm[:, 8 + ms:9 + ms], sm[:, 2 + ms:3 + ms], 1.0 / D, None, ALU.mult),
             reads=[("sx2", ms, ms + 1)], writes=[("var0", ms, ms + 1)], tag="ph1_var0")
        p.op("vector", lambda e, ms=ms: e.tensor_scalar(
                sm[:, 10 + ms:11 + ms], sm[:, 8 + ms:9 + ms],
                sm[:, 6 + ms:7 + ms], cfg.eps, ALU.subtract, ALU.add),
             reads=[("var0", ms, ms + 1), ("msq", ms, ms + 1)],
             writes=[("vpe", ms, ms + 1)], tag="ph1_var")
        p.op("scalar", lambda e, ms=ms: e.activation(
                sm[:, 12 + ms:13 + ms], sm[:, 10 + ms:11 + ms], AF.Ln),
             reads=[("vpe", ms, ms + 1)], writes=[("lnv", ms, ms + 1)], tag="ph1_ln")
        p.op("scalar", lambda e, ms=ms: e.activation(
                sm[:, 14 + ms:15 + ms], sm[:, 12 + ms:13 + ms], AF.Exp, scale=-0.5),
             reads=[("lnv", ms, ms + 1)], writes=[("rstd", ms, ms + 1)], tag="ph1_rstd")
        p.op("vector", lambda e, sl=sl, ms=ms: e.tensor_scalar(
                P(3 * D, D), P(sl * D, D), sm[:, 4 + ms:5 + ms],
                sm[:, 14 + ms:15 + ms], ALU.subtract, ALU.mult),
             reads=[xtm, ("mean", ms, ms + 1), ("rstd", ms, ms + 1)],
             writes=[("xnt", 0, 1)], tag="ph1_apply")
        p.op("scalar", lambda e: e.copy(Pb(XH, D), P(3 * D, D)),
             reads=[("xnt", 0, 1)], writes=[("xnth", 0, 1)], tag="ph1_casth")
        p.op("vector", lambda e: e.tensor_tensor(
                Pb(XL, D), P(3 * D, D), Pb(XH, D), ALU.subtract),
             reads=[("xnt", 0, 1), ("xnth", 0, 1)], writes=[("xntl", 0, 1)], tag="ph1_castl")
        for hl, (srcoff, srck, trt, trk, fmoff, fmk) in enumerate(
                ((XH, "xnth", psA, "psA", FH, "fmh"), (XL, "xntl", psD, "psD", FL, "fml"))):
            for c in range(KC):
                bank = (c // 8) % 4
                slot = c % 8
                pcol = 512 * bank + 64 * slot
                p.op("tensor", lambda e, c=c, srcoff=srcoff, trt=trt, pcol=pcol: e.transpose(
                        trt[:, pcol:pcol + 64].bitcast(BF16),
                        Pb(srcoff, D)[:, 128 * c:128 * (c + 1)], s_bid[:]),
                     reads=[(srck, 0, 1), ("c_bid", 0, 1)],
                     writes=[(trk, pcol, pcol + 64)], tag="ph1_tr")
                if slot == 7 or c == KC - 1:
                    c0 = (c // 8) * 8
                    ncp = c - c0 + 1
                    p.op("vector", lambda e, c0=c0, ncp=ncp, bank=bank, trt=trt, fmoff=fmoff: e.tensor_copy(
                            Pb(fmoff + 64 * c0, 128 * ncp),
                            trt[:, 512 * bank:512 * bank + 64 * ncp].bitcast(BF16)),
                         reads=[(trk, 512 * bank, 512 * bank + 64 * ncp)],
                         writes=[(fmk, 64 * c0, 64 * (c0 + ncp))], tag="ph1_ev")
            dstf = (xn1h if hl == 0 else xn1l).rearrange("(c q) t -> q c t", q=128)[
                :, :, 128 * tt:128 * (tt + 1)]
            srcf = Pb(fmoff, D).rearrange("q (c j) -> q c j", c=KC)
            for c0 in range(0, KC, 16):
                c1 = min(c0 + 16, KC)
                p.op("scalar", lambda e, dst=dstf[:, c0:c1, :], src=srcf[:, c0:c1, :]:
                        e.dma_start(out=dst, in_=src),
                     reads=[(fmk, 64 * c0, 64 * c1)], writes=[("xn1s", 0, 1)],
                     dma=True, tag="ph1_st")

    p.barrier()

    if stop_after < 2:
        _finish(p, nc, stack, y, pool, P, D, OT)
        return nc


    def qkv_store(oc):
        is_q = oc < QH
        is_k = QH <= oc < QH + KH
        psrc, pkey = (psA, "psA") if oc % 2 == 0 else (psD, "psD")
        if is_q or is_k:
            if is_q:
                p.op("vector", lambda e, psrc=psrc: e.tensor_scalar(
                        P(2 * S, S), psrc[:, 0:S], cfg.scale, None, ALU.mult),
                     reads=[(pkey, 0, S)], writes=[("qraw", 0, 1)], tag="ph2_evq")
            else:
                p.op("scalar", lambda e, psrc=psrc: e.copy(P(2 * S, S), psrc[:, 0:S]),
                     reads=[(pkey, 0, S)], writes=[("qraw", 0, 1)], tag="ph2_evk")
            for (t0, n) in _chunks(S, 512):
                p.op("tensor", lambda e, t0=t0, n=n, psrc=psrc: e.matmul(
                        psrc[:, t0:t0 + n], s_R[:], P(2 * S + t0, n),
                        start=True, stop=True),
                     reads=[("qraw", 0, 1), ("c_R", 0, 1)],
                     writes=[(pkey, t0, t0 + n)], tag="ph2_rmm")
            p.op("vector", lambda e: e.tensor_tensor(
                    P(3 * S, S), P(2 * S, S), s_C[:], ALU.mult),
                 reads=[("qraw", 0, 1), ("c_C", 0, 1)], writes=[("q1", 0, 1)], tag="ph2_qc")
            p.op("vector", lambda e, psrc=psrc: e.tensor_tensor(
                    P(4 * S, S), psrc[:, 0:S], s_S[:], ALU.mult),
                 reads=[(pkey, 0, S), ("c_S", 0, 1)], writes=[("q2", 0, 1)], tag="ph2_qs")
            p.op("vector", lambda e: e.tensor_tensor(
                    P(5 * S, S), P(3 * S, S), P(4 * S, S), ALU.add),
                 reads=[("q1", 0, 1), ("q2", 0, 1)], writes=[("qout", 0, 1)], tag="ph2_qo")
            p.op("scalar", lambda e: e.copy(Pb(6 * S, S), P(5 * S, S)),
                 reads=[("qout", 0, 1)], writes=[("qouth", 0, 1)], tag="ph2_qh")
            p.op("vector", lambda e: e.tensor_tensor(
                    Pb(6 * S + S // 2, S), P(5 * S, S), Pb(6 * S, S), ALU.subtract),
                 reads=[("qout", 0, 1), ("qouth", 0, 1)], writes=[("qoutl", 0, 1)], tag="ph2_ql")
            row = 128 * oc
            p.op("gpsimd", lambda e, row=row: e.dma_start(
                    out=qkh[row:row + 128, :], in_=Pb(6 * S, S)),
                 reads=[("qouth", 0, 1)], writes=[("qks", 0, 1)], dma=True, tag="ph2_stq")
            p.op("gpsimd", lambda e, row=row: e.dma_start(
                    out=qkl[row:row + 128, :], in_=Pb(6 * S + S // 2, S)),
                 reads=[("qoutl", 0, 1)], writes=[("qks", 0, 1)], dma=True, tag="ph2_stq")
        else:
            vc = oc - QH - KH
            p.op("scalar", lambda e, psrc=psrc: e.copy(P(2 * S, S), psrc[:, 0:S]),
                 reads=[(pkey, 0, S)], writes=[("qraw", 0, 1)], tag="ph2_evv")
            for tb in range(TT):
                bank = tb % 4
                p.op("tensor", lambda e, tb=tb, bank=bank, psrc=psrc: e.transpose(
                        psrc[:, 512 * bank:512 * bank + 128], P(2 * S + 128 * tb, 128), s_id[:]),
                     reads=[("qraw", 0, 1), ("c_id", 0, 1)],
                     writes=[(pkey, 512 * bank, 512 * bank + 128)], tag="ph2_vtr")
                p.op("vector", lambda e, tb=tb, bank=bank, psrc=psrc: e.tensor_copy(
                        P(3 * S + 128 * tb, 128), psrc[:, 512 * bank:512 * bank + 128]),
                     reads=[(pkey, 512 * bank, 512 * bank + 128)],
                     writes=[("q1", 128 * tb, 128 * (tb + 1))], tag="ph2_vev")
            dstf = vtms.rearrange("(tb q) v -> q tb v", q=128)[:, :, 128 * vc:128 * (vc + 1)]
            srcf = P(3 * S, S).rearrange("q (tb j) -> q tb j", tb=TT)
            for b0 in range(0, TT, 8):
                b1 = min(b0 + 8, TT)
                p.op("gpsimd", lambda e, dst=dstf[:, b0:b1, :], src=srcf[:, b0:b1, :]:
                        e.dma_start(out=dst, in_=src),
                     reads=[("q1", 0, S)], writes=[("vtms", 0, 1)], dma=True, tag="ph2_stv")

    GG = ("gpsimd", "gpsimd")
    _ph0b = (split_jobs(woT, woh, wol, QH * 128, D, 512, "ph0b", GG)
             + split_jobs(baseT, bah, bal, D, D, 512, "ph0c", GG)
             + split_jobs(aT, ath, atl, D, 128, 512, "ph0d", GG)
             + split_jobs(routT, rth, rtl, D, 8, 512, "ph0e", GG)
             + split_jobs(bflat, bfh, bfl, 128, D, 512, "ph0f", GG))

    SB = S // 2
    _job_every = max(1, ((OC // 2) * KC) // max(1, (2 * len(_ph0b) // 3)))
    for og in range(OC // 2):
        for k in range(KC):
            if k % _job_every == _job_every - 1 and _ph0b and og > 0:
                _ph0b.pop(0)()
            wsl = k % 2
            p.op("sync", lambda e, og=og, k=k, wsl=wsl: e.dma_start(
                    out=Wb(256 * wsl, 256),
                    in_=wqh[128 * k:128 * (k + 1), 256 * og:256 * (og + 1)]),
                 reads=[("wqh_s", 0, 1)],
                 writes=[(f"wph{wsl}", 0, 1)], dma=True, tag="ph2_w")
            p.op("sync", lambda e, og=og, k=k, wsl=wsl: e.dma_start(
                    out=Wb(256 * wsl + 128, 256),
                    in_=wql[128 * k:128 * (k + 1), 256 * og:256 * (og + 1)]),
                 reads=[("wql_s", 0, 1)],
                 writes=[(f"wpl{wsl}", 0, 1)], dma=True, tag="ph2_w")
            msl = k % 2
            p.op("scalar", lambda e, k=k, msl=msl: e.dma_start(
                    out=Pb(msl * S, S), in_=xn1h[128 * k:128 * (k + 1), :]),
                 reads=[("xn1s", 0, 1)], writes=[(f"mvh{msl}", 0, 1)], dma=True, tag="ph2_x")
            p.op("scalar", lambda e, k=k, msl=msl: e.dma_start(
                    out=Pb(msl * S + SB, S), in_=xn1l[128 * k:128 * (k + 1), :]),
                 reads=[("xn1s", 0, 1)], writes=[(f"mvl{msl}", 0, 1)], dma=True, tag="ph2_x")
            whofs = 512 * wsl
            wlofs = 512 * wsl + 256
            for oci, ps, pkey in ((0, psA, "psA"), (1, psD, "psD")):
                for (t0, n) in _chunks(S, 512):
                    for pi, (wof, wkk, mof, mkk) in enumerate((
                            (whofs + 128 * oci, f"wph{wsl}", msl * S, f"mvh{msl}"),
                            (whofs + 128 * oci, f"wph{wsl}", msl * S + SB, f"mvl{msl}"),
                            (wlofs + 128 * oci, f"wpl{wsl}", msl * S, f"mvh{msl}"))):
                        p.op("tensor", lambda e, k=k, t0=t0, n=n, ps=ps, pi=pi,
                                wof=wof, mof=mof: e.matmul(
                                ps[:, t0:t0 + n],
                                Wb(0, 1024)[:, wof:wof + 128],
                                Pb(mof, S)[:, t0:t0 + n],
                                start=(k == 0 and pi == 0),
                                stop=(k == KC - 1 and pi == 2)),
                             reads=[(wkk, 0, 1), (mkk, 0, 1)],
                             writes=[(pkey, t0, t0 + n)], tag="ph2_mm")
        qkv_store(2 * og)
        qkv_store(2 * og + 1)

    p.barrier()

    if stop_after < 3:
        _finish(p, nc, stack, y, pool, P, D, OT)
        return nc

    for j in range(KH):
        jp = j % 2
        kb_ = 0 if jp == 0 else 10 * S
        vb_ = S if jp == 0 else 11 * S
        p.op("sync", lambda e, j=j, kb_=kb_: e.dma_start(
                out=Pb(kb_, S), in_=qkh[128 * (QH + j):128 * (QH + j + 1), :]),
             reads=[("qks", 0, 1)], writes=[(f"aKh{jp}", 0, 1)], dma=True, tag="ph3_K")
        p.op("sync", lambda e, j=j, kb_=kb_: e.dma_start(
                out=Pb(kb_ + SB, S), in_=qkl[128 * (QH + j):128 * (QH + j + 1), :]),
             reads=[("qks", 0, 1)], writes=[(f"aKl{jp}", 0, 1)], dma=True, tag="ph3_K2")
        vsrc = vtms.rearrange("(tb q) v -> q tb v", q=128)[:, :, 128 * j:128 * (j + 1)]
        vdst = P(vb_, S).rearrange("q (tb j) -> q tb j", tb=TT)
        for b0 in range(0, TT, 8):
            b1 = min(b0 + 8, TT)
            p.op("sync", lambda e, src=vsrc[:, b0:b1, :], dst=vdst[:, b0:b1, :]:
                    e.dma_start(out=dst, in_=src),
                 reads=[("vtms", 0, 1)], writes=[(f"aV{jp}", 0, 1)], dma=True, tag="ph3_V")
        for qh in range(4):
            hq = 4 * j + qh
            hp = hq % 2
            qb_ = 2 * S if hp == 0 else 9 * S
            nper3 = max(1, len(_ph0b) // max(1, (KH - j) * 4 - qh))
            for _ in range(min(nper3, len(_ph0b))):
                _ph0b.pop(0)()
            p.op("sync", lambda e, hq=hq, qb_=qb_: e.dma_start(
                    out=Pb(qb_, S), in_=qkh[128 * hq:128 * (hq + 1), :]),
                 reads=[("qks", 0, 1)], writes=[(f"aQh{hp}", 0, 1)], dma=True, tag="ph3_Q")
            p.op("sync", lambda e, hq=hq, qb_=qb_: e.dma_start(
                    out=Pb(qb_ + SB, S), in_=qkl[128 * hq:128 * (hq + 1), :]),
                 reads=[("qks", 0, 1)], writes=[(f"aQl{hp}", 0, 1)], dma=True, tag="ph3_Q2")
            def emit_scores(g):
                klen = 128 * (g + 1)
                nblk = (g + 4) // 4
                par = g % 2
                smb = 16 + 10 * par
                for bi, kb in enumerate(range(0, g + 1, 4)):
                    n = min(512, klen - 128 * kb)
                    for pi, (qof, qk_, kof, kk_) in enumerate((
                            (qb_, f"aQh{hp}", kb_, f"aKh{jp}"),
                            (qb_ + SB, f"aQl{hp}", kb_, f"aKh{jp}"),
                            (qb_, f"aQh{hp}", kb_ + SB, f"aKl{jp}"))):
                        p.op("tensor", lambda e, g=g, kb=kb, n=n, pi=pi, qof=qof, kof=kof: e.matmul(
                                psA[:, 128 * kb:128 * kb + n],
                                Pb(qof, S)[:, 128 * g:128 * (g + 1)],
                                Pb(kof, S)[:, 128 * kb:128 * kb + n],
                                start=(pi == 0), stop=(pi == 2)),
                             reads=[(qk_, 0, 1), (kk_, 0, 1)],
                             writes=[("psA", 128 * kb, 128 * kb + n)], tag="ph3_s")
                    if kb + 4 <= g:
                        p.op("vector", lambda e, kb=kb, n=n, bi=bi, smb=smb: e.reduce_max(
                                sm[:, smb + bi:smb + bi + 1], psA[:, 128 * kb:128 * kb + n], axis=AX.X),
                             reads=[("psA", 128 * kb, 128 * kb + n)],
                             writes=[("rm", smb + bi, smb + bi + 1)], tag="ph3_m1")
                p.op("vector", lambda e, g=g: e.tensor_tensor(
                        psA[:, 128 * g:128 * (g + 1)], psA[:, 128 * g:128 * (g + 1)],
                        s_msk[:], ALU.add),
                     reads=[("psA", 128 * g, 128 * (g + 1)), ("c_msk", 0, 1)],
                     writes=[("psA", 128 * g, 128 * (g + 1))], tag="ph3_msk")
                lb = (g // 4) * 4
                lbn = klen - 128 * lb
                p.op("vector", lambda e, lb=lb, lbn=lbn, nblk=nblk, smb=smb: e.reduce_max(
                        sm[:, smb + nblk - 1:smb + nblk], psA[:, 128 * lb:128 * lb + lbn], axis=AX.X),
                     reads=[("psA", 128 * lb, 128 * lb + lbn)],
                     writes=[("rm", smb + nblk - 1, smb + nblk)], tag="ph3_m2")
                if nblk > 1:
                    p.op("vector", lambda e, nblk=nblk, smb=smb: e.reduce_max(
                            sm[:, smb + 4:smb + 5], sm[:, smb:smb + nblk], axis=AX.X, negate=True),
                         reads=[("rm", smb, smb + nblk)],
                         writes=[("negmx", smb, smb + 1)], tag="ph3_mx")
                else:
                    p.op("vector", lambda e, smb=smb: e.tensor_scalar(
                            sm[:, smb + 4:smb + 5], sm[:, smb:smb + 1], -1.0, None, ALU.mult),
                         reads=[("rm", smb, smb + 1)], writes=[("negmx", smb, smb + 1)], tag="ph3_mx")

            def emit_soft(g):
                klen = 128 * (g + 1)
                par = g % 2
                smb = 16 + 10 * par
                pnoff = 4 * S if par == 0 else 8 * S
                p.op("scalar", lambda e, klen=klen, smb=smb: e.activation(
                        P(3 * S, klen), psA[:, 0:klen], AF.Exp,
                        bias=sm[:, smb + 4:smb + 5], scale=1.0, accum_out=sm[:, smb + 5:smb + 6]),
                     reads=[("psA", 0, klen), ("negmx", smb, smb + 1)],
                     writes=[("aP", 0, klen), ("ssum", smb, smb + 1)], tag="ph3_exp")
                p.op("vector", lambda e, smb=smb: e.reciprocal(
                        sm[:, smb + 6:smb + 7], sm[:, smb + 5:smb + 6]),
                     reads=[("ssum", smb, smb + 1)], writes=[("rinv", smb, smb + 1)], tag="ph3_ri")
                p.op("vector", lambda e, klen=klen, smb=smb, pnoff=pnoff: e.tensor_scalar(
                        P(pnoff, klen), P(3 * S, klen), sm[:, smb + 6:smb + 7], None, ALU.mult),
                     reads=[("aP", 0, klen), ("rinv", smb, smb + 1)],
                     writes=[(f"aPn{par}", 0, klen)], tag="ph3_n")

            def emit_ptpv(g):
                par = g % 2
                pnoff = 4 * S if par == 0 else 8 * S
                for kb in range(g + 1):
                    pr = (kb // 2) % 2
                    tco = (512 if pr == 0 else 1536) + 128 * (kb % 2)
                    p.op("tensor", lambda e, kb=kb, tco=tco, pnoff=pnoff: e.transpose(
                            psD[:, tco:tco + 128],
                            P(pnoff + 128 * kb, 128), s_id[:]),
                         reads=[(f"aPn{par}", 128 * kb, 128 * (kb + 1)), ("c_id", 0, 1)],
                         writes=[("psD", tco, tco + 128)], tag="ph3_tr")
                    if kb % 2 == 1 or kb == g:
                        kb0 = kb - (kb % 2)
                        n2 = 128 * (kb - kb0 + 1)
                        eco = 512 if pr == 0 else 1536
                        if pr == 0:
                            p.op("vector", lambda e, kb0=kb0, n2=n2, eco=eco: e.tensor_copy(
                                    P(5 * S + 128 * kb0, n2), psD[:, eco:eco + n2]),
                                 reads=[("psD", eco, eco + n2)],
                                 writes=[("aPT", 128 * kb0, 128 * kb0 + n2)], tag="ph3_trev")
                        else:
                            p.op("scalar", lambda e, kb0=kb0, n2=n2, eco=eco: e.copy(
                                    P(5 * S + 128 * kb0, n2), psD[:, eco:eco + n2]),
                                 reads=[("psD", eco, eco + n2)],
                                 writes=[("aPT", 128 * kb0, 128 * kb0 + n2)], tag="ph3_trev")
                for kb in range(g + 1):
                    p.op("tensor", lambda e, kb=kb, g=g, vb_=vb_: e.matmul(
                            psD[:, 1024:1152], P(vb_ + 128 * kb, 128),
                            P(5 * S + 128 * kb, 128), start=(kb == 0), stop=(kb == g)),
                         reads=[(f"aV{jp}", 0, 1), ("aPT", 128 * kb, 128 * (kb + 1))],
                         writes=[("psD", 1024, 1152)], tag="ph3_pv")
                p.op("vector", lambda e, g=g: e.tensor_copy(
                        P(6 * S + 128 * g, 128), psD[:, 1024:1152]),
                     reads=[("psD", 1024, 1152)],
                     writes=[("aao", 128 * g, 128 * (g + 1))], tag="ph3_aev")

            for g in range(TT):
                emit_scores(g)
                if g > 0:
                    emit_ptpv(g - 1)
                emit_soft(g)
            emit_ptpv(TT - 1)
            p.op("scalar", lambda e: e.copy(Pb(7 * S, S), P(6 * S, S)),
                 reads=[("aao", 0, S)], writes=[("aoh", 0, 1)], tag="ph3_aoh")
            p.op("vector", lambda e: e.tensor_tensor(
                    Pb(7 * S + SB, S), P(6 * S, S), Pb(7 * S, S), ALU.subtract),
                 reads=[("aao", 0, S), ("aoh", 0, 1)], writes=[("aol", 0, 1)], tag="ph3_aol")
            p.op("gpsimd", lambda e, hq=hq: e.dma_start(
                    out=atth[128 * hq:128 * (hq + 1), :], in_=Pb(7 * S, S)),
                 reads=[("aoh", 0, 1)], writes=[("attns", 0, 1)], dma=True, tag="ph3_st")
            p.op("gpsimd", lambda e, hq=hq: e.dma_start(
                    out=attl[128 * hq:128 * (hq + 1), :], in_=Pb(7 * S + SB, S)),
                 reads=[("aol", 0, 1)], writes=[("attns", 0, 1)], dma=True, tag="ph3_st")

    for j_ in _ph0b:
        j_()
    _ph0b.clear()

    p.barrier()

    if stop_after < 4:
        _finish(p, nc, stack, y, pool, P, D, OT)
        return nc

    for tqi, (tq0, tqn) in enumerate(_chunks(S, 512)):
        tqp = tqi % 2
        ATB = 0 if tqp == 0 else QH * tqn
        ATH, ATL = ATB, ATB + QH * tqn // 2
        for hl, (src, dstoff, kk_) in enumerate(((atth, ATH, f"wath{tqp}"), (attl, ATL, f"watl{tqp}"))):
            asrc = src.rearrange("(kc q) t -> q kc t", q=128)[:, :, tq0:tq0 + tqn]
            adst = Pb(dstoff, QH * tqn).rearrange("q (kc j) -> q kc j", kc=QH)
            for b0 in range(0, QH, 8):
                b1 = min(b0 + 8, QH)
                p.op("sync", lambda e, src2=asrc[:, b0:b1, :], dst2=adst[:, b0:b1, :]:
                        e.dma_start(out=dst2, in_=src2),
                     reads=[("attns", 0, 1)], writes=[(kk_, 0, 1)], dma=True, tag="ph4_a")
        for og in range(D // 512):
            par = og % 2
            ps, pkey = (psA, "psA") if par == 0 else (psD, "psD")
            for k in range(QH):
                wsl = k % 2
                p.op("sync", lambda e, k=k, og=og, wsl=wsl: e.dma_start(
                        out=Wb(512 * wsl, 512),
                        in_=woh[128 * k:128 * (k + 1), 512 * og:512 * (og + 1)]),
                     reads=[("woh_s", 0, 1)],
                     writes=[(f"woh{wsl}", 0, 1)], dma=True, tag="ph4_w")
                p.op("sync", lambda e, k=k, og=og, wsl=wsl: e.dma_start(
                        out=Wb(512 * wsl + 256, 512),
                        in_=wol[128 * k:128 * (k + 1), 512 * og:512 * (og + 1)]),
                     reads=[("wol_s", 0, 1)],
                     writes=[(f"wol{wsl}", 0, 1)], dma=True, tag="ph4_w")
                for tt in range(tqn // 128):
                    for pi, (aof, akk, wof, wkk) in enumerate((
                            (ATH, f"wath{tqp}", 512 * wsl, f"woh{wsl}"),
                            (ATL, f"watl{tqp}", 512 * wsl, f"woh{wsl}"),
                            (ATH, f"wath{tqp}", 512 * wsl + 256, f"wol{wsl}"))):
                        p.op("tensor", lambda e, k=k, tt=tt, pi=pi, aof=aof, wof=wof,
                                ps=ps, tqn=tqn: e.matmul(
                                ps[:, 512 * tt:512 * (tt + 1)],
                                Pb(aof, QH * tqn)[:, tqn * k + 128 * tt:tqn * k + 128 * (tt + 1)],
                                Wb(wof, 512),
                                start=(k == 0 and pi == 0),
                                stop=(k == QH - 1 and pi == 2)),
                             reads=[(akk, 0, 1), (wkk, 0, 1)],
                             writes=[(pkey, 512 * tt, 512 * (tt + 1))], tag="ph4_mm")
            for tt in range(tqn // 128):
                psl = tt % 2
                p.op("vector", lambda e, tt=tt, psl=psl, ps=ps: e.tensor_copy(
                        W(1024 + 512 * psl, 512), ps[:, 512 * tt:512 * (tt + 1)]),
                     reads=[(pkey, 512 * tt, 512 * (tt + 1))],
                     writes=[(f"po{psl}", 0, 1)], tag="ph4_ev")
                qi = next(i for i, (q0, qn) in enumerate(_qch) if q0 <= tq0 < q0 + qn)
                rr = tq0 - _qch[qi][0]
                p.op("scalar", lambda e, qi=qi, rr=rr, og=og, tt=tt, psl=psl: e.dma_start(
                        out=partsq[qi][rr + 128 * tt:rr + 128 * (tt + 1),
                                       512 * og:512 * (og + 1)],
                        in_=W(1024 + 512 * psl, 512)),
                     reads=[(f"po{psl}", 0, 1)], writes=[(f"parts{qi}", 0, 1)],
                     dma=True, tag="ph4_st")
        qi = next(i for i, (q0, qn) in enumerate(_qch) if q0 <= tq0 < q0 + qn)
        if tq0 + tqn != _qch[qi][0] + _qch[qi][1]:
            continue_rs = True
        else:
            continue_rs = False
        if continue_rs:
            pass
        else:
            p.op("gpsimd", lambda e, qi=qi: e.collective_compute(
                    "ReduceScatter", ALU.add,
                    replica_groups=[[0, 1], [2, 3], [4, 5], [6, 7]],
                    ins=[partsq[qi][:]],
                    outs=[rsoutsq[qi][:]]),
                 reads=[(f"parts{qi}", 0, 1)],
                 writes=[(f"rsouts{qi}", 0, 1)], tag="ph5_rs")

    p.barrier()

    if stop_after < 6:
        _finish(p, nc, stack, y, pool, P, D, OT)
        return nc

    for tt in range(OT):
        x2k = ("x2", D * tt, D * (tt + 1))
        ms = 24 + 10 * (tt % 2)
        r = 128 * tt
        qi = next(i for i, (tq0, tqn) in enumerate(_qch)
                  if tq0 // 2 <= r < (tq0 + tqn) // 2)
        ro = r - _qch[qi][0] // 2
        p.op("sync", lambda e, tt=tt, qi=qi, ro=ro: e.dma_start(
                out=P(D * tt, D), in_=rsoutsq[qi][ro:ro + 128, :]),
             reads=[(f"rsouts{qi}", 0, 1)], writes=[x2k],
             dma=True, tag="ph6_rs")
        p.op("sync", lambda e, tt=tt: e.dma_start(
                out=W(2 * D, D), in_=xown[128 * tt:128 * (tt + 1), :]),
             writes=[("xo6", 0, 1)], dma=True, tag="ph6_xo")
        p.op("vector", lambda e, tt=tt: e.tensor_tensor(
                P(D * tt, D), P(D * tt, D), W(2 * D, D), ALU.add),
             reads=[x2k, ("xo6", 0, 1)], writes=[x2k], tag="ph6_add")
        sq = (tt + 1) % 2
        p.op("scalar", lambda e, tt=tt, ms=ms: e.activation(
                W(2 * D, D), P(D * tt, D), AF.Square, accum_out=sm[:, ms + 2:ms + 3]),
             reads=[x2k], writes=[("xo6", 0, 1), ("sx26", ms, ms + 1)], tag="ph6_sq")
        p.op("vector", lambda e, tt=tt, ms=ms: e.reduce_sum(
                sm[:, ms:ms + 1], P(D * tt, D), axis=AX.X),
             reads=[x2k], writes=[("sx6", ms, ms + 1)], tag="ph6_sum")
        p.op("vector", lambda e, ms=ms: e.tensor_scalar(
                sm[:, ms + 4:ms + 5], sm[:, ms:ms + 1], 1.0 / D, None, ALU.mult),
             reads=[("sx6", ms, ms + 1)], writes=[("mean6", ms, ms + 1)], tag="ph6_mean")
        p.op("vector", lambda e, ms=ms: e.tensor_tensor(
                sm[:, ms + 5:ms + 6], sm[:, ms + 4:ms + 5], sm[:, ms + 4:ms + 5], ALU.mult),
             reads=[("mean6", ms, ms + 1)], writes=[("msq6", ms, ms + 1)], tag="ph6_msq")
        p.op("vector", lambda e, ms=ms: e.tensor_scalar(
                sm[:, ms + 6:ms + 7], sm[:, ms + 2:ms + 3], 1.0 / D, None, ALU.mult),
             reads=[("sx26", ms, ms + 1)], writes=[("var6", ms, ms + 1)], tag="ph6_var0")
        p.op("vector", lambda e, ms=ms: e.tensor_scalar(
                sm[:, ms + 7:ms + 8], sm[:, ms + 6:ms + 7], sm[:, ms + 5:ms + 6],
                cfg.eps, ALU.subtract, ALU.add),
             reads=[("var6", ms, ms + 1), ("msq6", ms, ms + 1)],
             writes=[("vpe6", ms, ms + 1)], tag="ph6_var")
        p.op("scalar", lambda e, ms=ms: e.activation(
                sm[:, ms + 8:ms + 9], sm[:, ms + 7:ms + 8], AF.Ln),
             reads=[("vpe6", ms, ms + 1)], writes=[("lnv6", ms, ms + 1)], tag="ph6_ln")
        p.op("scalar", lambda e, ms=ms: e.activation(
                sm[:, ms + 9:ms + 10], sm[:, ms + 8:ms + 9], AF.Exp, scale=-0.5),
             reads=[("lnv6", ms, ms + 1)], writes=[("rstd6", ms, ms + 1)], tag="ph6_rstd")
        sx = tt % 2
        p.op("vector", lambda e, tt=tt, sx=sx, ms=ms: e.tensor_scalar(
                W(sx * D, D), P(D * tt, D), sm[:, ms + 4:ms + 5], sm[:, ms + 9:ms + 10],
                ALU.subtract, ALU.mult),
             reads=[x2k, ("mean6", ms, ms + 1), ("rstd6", ms, ms + 1)],
             writes=[(f"w6{sx}", 0, 1)], tag="ph6_apply")
        p.op("scalar", lambda e, sx=sx, sq=sq: e.copy(Wb(sq * D, D), W(sx * D, D)),
             reads=[(f"w6{sx}", 0, 1)], writes=[(f"w6{sq}", 0, 1)], tag="ph6_ch")
        p.op("vector", lambda e, sx=sx, sq=sq: e.tensor_tensor(
                Wb(sq * D + D // 2, D), W(sx * D, D), Wb(sq * D, D), ALU.subtract),
             reads=[(f"w6{sx}", 0, 1), (f"w6{sq}", 0, 1)],
             writes=[(f"w6{sq}", 0, 1)], tag="ph6_cl")
        for hl, (trt, trk) in enumerate(((psA, "psA"), (psD, "psD"))):
            srcoff = sq * D if hl == 0 else sq * D + D // 2
            fmoff = sx * D if hl == 0 else sx * D + D // 2
            for c in range(KC):
                bank = (c // 8) % 4
                slot = c % 8
                pcol = 512 * bank + 64 * slot
                p.op("tensor", lambda e, c=c, srcoff=srcoff, trt=trt, pcol=pcol: e.transpose(
                        trt[:, pcol:pcol + 64].bitcast(BF16),
                        Wb(srcoff, D)[:, 128 * c:128 * (c + 1)], s_bid[:]),
                     reads=[(f"w6{sq}", 0, 1), ("c_bid", 0, 1)],
                     writes=[(trk, pcol, pcol + 64)], tag="ph6_tr")
                if slot == 7 or c == KC - 1:
                    c0 = (c // 8) * 8
                    ncp = c - c0 + 1
                    p.op("vector", lambda e, c0=c0, ncp=ncp, bank=bank, trt=trt,
                            fmoff=fmoff: e.tensor_copy(
                            Wb(fmoff + 64 * c0, 128 * ncp),
                            trt[:, 512 * bank:512 * bank + 64 * ncp].bitcast(BF16)),
                         reads=[(trk, 512 * bank, 512 * bank + 64 * ncp)],
                         writes=[(f"w6{sx}", 0, 1)], tag="ph6_ev")
            dstf = (xn2h if hl == 0 else xn2l).rearrange("(c q) t -> q c t", q=128)[
                :, :, 128 * tt:128 * (tt + 1)]
            srcf = Wb(fmoff, D).rearrange("q (c j) -> q c j", c=KC)
            for c0 in range(0, KC, 16):
                c1 = min(c0 + 16, KC)
                p.op("scalar", lambda e, dst=dstf[:, c0:c1, :], src=srcf[:, c0:c1, :]:
                        e.dma_start(out=dst, in_=src),
                     reads=[(f"w6{sx}", 0, 1)], writes=[("xn2s", 0, 1)],
                     dma=True, tag="ph6_st")

    p.barrier()

    if stop_after < 7:
        _finish(p, nc, stack, y, pool, P, D, OT)
        return nc

    NW = OWN
    A0 = 2 * NW
    LG = A0 + 512
    WFM = LG + NW
    WBIG = WFM + NW
    MIDS = WBIG + NW
    MIDSP = WKC - NW
    HW_ = NW // 2
    assert MIDSP >= MIDS + NW and MIDSP >= 2 * NW + 1536 + NW

    def fm_loads(k, msl, tagp):
        p.op("scalar", lambda e, k=k, msl=msl: e.dma_start(
                out=Wb(msl * NW, NW), in_=xn2h[128 * k:128 * (k + 1), :]),
             reads=[("xn2s", 0, 1)], writes=[(f"fmh{msl}", 0, 1)], dma=True, tag=tagp)
        p.op("scalar", lambda e, k=k, msl=msl: e.dma_start(
                out=Wb(msl * NW + HW_, NW), in_=xn2l[128 * k:128 * (k + 1), :]),
             reads=[("xn2s", 0, 1)], writes=[(f"fml{msl}", 0, 1)], dma=True, tag=tagp)

    for k in range(KC):
        msl = k % 2
        fm_loads(k, msl, "ph7_x")
        p.op("sync", lambda e, k=k, msl=msl: e.dma_start(
                out=Wb(A0 + 136 * msl, 128), in_=ath[128 * k:128 * (k + 1), :]),
             reads=[("ath_s", 0, 1)], writes=[(f"ath{msl}", 0, 1)], dma=True, tag="ph7_a")
        p.op("sync", lambda e, k=k, msl=msl: e.dma_start(
                out=Wb(A0 + 136 * msl + 64, 128), in_=atl[128 * k:128 * (k + 1), :]),
             reads=[("atl_s", 0, 1)], writes=[(f"atl{msl}", 0, 1)], dma=True, tag="ph7_a")
        p.op("sync", lambda e, k=k, msl=msl: e.dma_start(
                out=Wb(A0 + 136 * msl + 128, 8), in_=rth[128 * k:128 * (k + 1), :]),
             reads=[("rth_s", 0, 1)], writes=[(f"rth{msl}", 0, 1)], dma=True, tag="ph7_r")
        p.op("sync", lambda e, k=k, msl=msl: e.dma_start(
                out=Wb(A0 + 136 * msl + 132, 8), in_=rtl[128 * k:128 * (k + 1), :]),
             reads=[("rtl_s", 0, 1)], writes=[(f"rtl{msl}", 0, 1)], dma=True, tag="ph7_r")
        for (t0, n) in _chunks(NW, 512):
            for pi, (wof, wkk, mof, mkk) in enumerate((
                    (A0 + 136 * msl, f"ath{msl}", msl * NW, f"fmh{msl}"),
                    (A0 + 136 * msl, f"ath{msl}", msl * NW + HW_, f"fml{msl}"),
                    (A0 + 136 * msl + 64, f"atl{msl}", msl * NW, f"fmh{msl}"))):
                p.op("tensor", lambda e, k=k, t0=t0, n=n, pi=pi, wof=wof, mof=mof: e.matmul(
                        psA[:, t0:t0 + n], Wb(wof, 128),
                        Wb(mof, NW)[:, t0:t0 + n],
                        start=(k == 0 and pi == 0), stop=(k == KC - 1 and pi == 2)),
                     reads=[(wkk, 0, 1), (mkk, 0, 1)],
                     writes=[("psA", t0, t0 + n)], tag="ph7_mmA")
            for pi, (wof, wkk, mof, mkk) in enumerate((
                    (A0 + 136 * msl + 128, f"rth{msl}", msl * NW, f"fmh{msl}"),
                    (A0 + 136 * msl + 128, f"rth{msl}", msl * NW + HW_, f"fml{msl}"),
                    (A0 + 136 * msl + 132, f"rtl{msl}", msl * NW, f"fmh{msl}"))):
                p.op("tensor", lambda e, k=k, t0=t0, n=n, pi=pi, wof=wof, mof=mof: e.matmul(
                        psA[0:8, 1024 + t0:1024 + t0 + n], Wb(wof, 8),
                        Wb(mof, NW)[:, t0:t0 + n],
                        start=(k == 0 and pi == 0), stop=(k == KC - 1 and pi == 2)),
                     reads=[(wkk, 0, 1), (mkk, 0, 1)],
                     writes=[("psA", 1024 + t0, 1024 + t0 + n)], tag="ph7_mmR")

    p.op("vector", lambda e: e.tensor_copy(W(LG, NW)[0:8, :], psA[0:8, 1024:1024 + NW]),
         reads=[("psA", 1024, 1024 + NW)], writes=[("lg", 0, 1)], tag="ph7_lgev")
    for tt in range(OT):
        p.op("tensor", lambda e, tt=tt: e.transpose(
                psD[:, 8 * tt:8 * (tt + 1)], W(LG, NW)[0:8, 128 * tt:128 * (tt + 1)],
                s_id[0:8, 0:8]),
             reads=[("lg", 0, 1), ("c_id", 0, 1)],
             writes=[("psD", 8 * tt, 8 * (tt + 1))], tag="ph7_lgtr")
    p.op("vector", lambda e: e.tensor_copy(lgsm[:, 0:8 * OT], psD[:, 0:8 * OT]),
         reads=[("psD", 0, 8 * OT)], writes=[("lgt", 0, 8 * OT)], tag="ph7_lgtev")
    T1 = 8 * OT
    T2 = 8 * OT + 8
    WD = 8 * OT + 16
    for tt in range(OT):
        lgt = lgsm[:, 8 * tt:8 * (tt + 1)]
        lk = ("lgt", 8 * tt, 8 * (tt + 1))
        p.op("vector", lambda e, lgt=lgt: e.reduce_max(sm[:, 34:35], lgt, axis=AX.X),
             reads=[lk], writes=[("m1", 0, 1)], tag="ph7_m1")
        p.op("vector", lambda e, lgt=lgt: e.tensor_scalar(
                lgsm[:, T1:T1 + 8], lgt, sm[:, 34:35], -1e30, ALU.is_equal, ALU.mult),
             reads=[lk, ("m1", 0, 1)], writes=[("t1", 0, 1)], tag="ph7_t1")
        p.op("vector", lambda e, lgt=lgt: e.tensor_tensor(
                lgsm[:, T2:T2 + 8], lgt, lgsm[:, T1:T1 + 8], ALU.add),
             reads=[lk, ("t1", 0, 1)], writes=[("t2", 0, 1)], tag="ph7_t2")
        p.op("vector", lambda e: e.reduce_max(sm[:, 35:36], lgsm[:, T2:T2 + 8], axis=AX.X),
             reads=[("t2", 0, 1)], writes=[("m2", 0, 1)], tag="ph7_m2")
        p.op("vector", lambda e: e.tensor_tensor(
                sm[:, 36:37], sm[:, 35:36], sm[:, 34:35], ALU.subtract),
             reads=[("m1", 0, 1), ("m2", 0, 1)], writes=[("dd", 0, 1)], tag="ph7_d")
        p.op("scalar", lambda e: e.activation(sm[:, 37:38], sm[:, 36:37], AF.Exp),
             reads=[("dd", 0, 1)], writes=[("e01", 0, 1)], tag="ph7_e")
        p.op("vector", lambda e: e.tensor_scalar(sm[:, 38:39], sm[:, 37:38], 1.0, None, ALU.add),
             reads=[("e01", 0, 1)], writes=[("t3", 0, 1)], tag="ph7_t3")
        p.op("vector", lambda e: e.reciprocal(sm[:, 39:40], sm[:, 38:39]),
             reads=[("t3", 0, 1)], writes=[("w0r", 0, 1)], tag="ph7_w0r")
        p.op("vector", lambda e: e.tensor_scalar(sm[:, 40:41], sm[:, 39:40], 2.0, None, ALU.mult),
             reads=[("w0r", 0, 1)], writes=[("w0", 0, 1)], tag="ph7_w0")
        p.op("vector", lambda e: e.tensor_scalar(sm[:, 41:42], sm[:, 40:41], -1.0, 2.0, ALU.mult, ALU.add),
             reads=[("w0", 0, 1)], writes=[("w1", 0, 1)], tag="ph7_w1")
        p.op("vector", lambda e, lgt=lgt: e.tensor_scalar(
                lgsm[:, T1:T1 + 8], lgt, sm[:, 34:35], sm[:, 40:41], ALU.is_equal, ALU.mult),
             reads=[lk, ("m1", 0, 1), ("w0", 0, 1)], writes=[("t1", 0, 1)], tag="ph7_c0")
        p.op("vector", lambda e, lgt=lgt: e.tensor_scalar(
                lgsm[:, T2:T2 + 8], lgt, sm[:, 35:36], sm[:, 41:42], ALU.is_equal, ALU.mult),
             reads=[lk, ("m2", 0, 1), ("w1", 0, 1)], writes=[("t2", 0, 1)], tag="ph7_c1")
        p.op("vector", lambda e, tt=tt: e.tensor_tensor(
                lgsm[:, WD + 8 * tt:WD + 8 * (tt + 1)],
                lgsm[:, T1:T1 + 8], lgsm[:, T2:T2 + 8], ALU.add),
             reads=[("t1", 0, 1), ("t2", 0, 1)],
             writes=[("wd", 8 * tt, 8 * (tt + 1))], tag="ph7_wd")
        p.op("tensor", lambda e, tt=tt: e.transpose(
                psD[0:8, 1024 + 128 * tt:1024 + 128 * (tt + 1)],
                lgsm[:, WD + 8 * tt:WD + 8 * (tt + 1)], s_id[:]),
             reads=[("wd", 8 * tt, 8 * (tt + 1)), ("c_id", 0, 1)],
             writes=[("psD", 1024 + 128 * tt, 1024 + 128 * (tt + 1))], tag="ph7_wtr")
    p.op("vector", lambda e: e.tensor_copy(W(WFM, NW)[0:8, :], psD[0:8, 1024:1024 + NW]),
         reads=[("psD", 1024, 1024 + NW)], writes=[("wfm", 0, 1)], tag="ph7_wfev")
    for (t0, n) in _chunks(NW, 512):
        p.op("tensor", lambda e, t0=t0, n=n: e.matmul(
                psD[:, t0:t0 + n], s_bc[:], W(WFM, NW)[0:8, t0:t0 + n],
                start=True, stop=True),
             reads=[("wfm", 0, 1), ("c_bc", 0, 1)],
             writes=[("psD", t0, t0 + n)], tag="ph7_bc")
    p.op("vector", lambda e: e.tensor_copy(W(WBIG, NW), psD[:, 0:NW]),
         reads=[("psD", 0, NW)], writes=[("wbig", 0, 1)], tag="ph7_bigev")
    p.op("vector", lambda e: e.tensor_tensor(
            W(MIDS, NW), psA[:, 0:NW], W(WBIG, NW), ALU.mult),
         reads=[("psA", 0, NW), ("wbig", 0, 1)], writes=[("mids", 0, 1)], tag="ph7_mids")
    p.op("scalar", lambda e: e.copy(Wb(MIDSP, NW), W(MIDS, NW)),
         reads=[("mids", 0, 1)], writes=[("midsh", 0, 1)], tag="ph7_mh")
    p.op("vector", lambda e: e.tensor_tensor(
            Wb(MIDSP + HW_, NW), W(MIDS, NW), Wb(MIDSP, NW), ALU.subtract),
         reads=[("mids", 0, 1), ("midsh", 0, 1)], writes=[("midsl", 0, 1)], tag="ph7_ml")

    p.barrier()

    MB = 2 * OWN
    LB = MB + 1024
    for og in range(D // 512):
        for k in range(KC):
            msl = k % 2
            p.op("scalar", lambda e, k=k, msl=msl: e.dma_start(
                    out=Wb(OWN * msl, OWN), in_=xn2h[128 * k:128 * (k + 1), :]),
                 reads=[("xn2s", 0, 1)], writes=[(f"xh{msl}", 0, 1)], dma=True, tag="ph7b_x")
            p.op("scalar", lambda e, k=k, msl=msl: e.dma_start(
                    out=Wb(OWN * msl + OWN // 2, OWN), in_=xn2l[128 * k:128 * (k + 1), :]),
                 reads=[("xn2s", 0, 1)], writes=[(f"xl{msl}", 0, 1)], dma=True, tag="ph7b_x")
            p.op("sync", lambda e, k=k, og=og, msl=msl: e.dma_start(
                    out=Wb(MB + 512 * msl, 512),
                    in_=bah[128 * k:128 * (k + 1), 512 * og:512 * (og + 1)]),
                 writes=[(f"bmh{msl}", 0, 1)], dma=True, tag="ph7b_w")
            p.op("sync", lambda e, k=k, og=og, msl=msl: e.dma_start(
                    out=Wb(MB + 512 * msl + 256, 512),
                    in_=bal[128 * k:128 * (k + 1), 512 * og:512 * (og + 1)]),
                 writes=[(f"bml{msl}", 0, 1)], dma=True, tag="ph7b_w")
            for tt in range(OT):
                ps, pkey = (psA, "psA") if tt % 2 == 0 else (psD, "psD")
                poff = 512 * ((tt // 2) % 4)
                for pi, (xof, xkk, wof, wkk) in enumerate((
                        (OWN * msl, f"xh{msl}", MB + 512 * msl, f"bmh{msl}"),
                        (OWN * msl + OWN // 2, f"xl{msl}", MB + 512 * msl, f"bmh{msl}"),
                        (OWN * msl, f"xh{msl}", MB + 512 * msl + 256, f"bml{msl}"))):
                    p.op("tensor", lambda e, k=k, tt=tt, pi=pi, xof=xof, wof=wof,
                            ps=ps, poff=poff: e.matmul(
                            ps[:, poff:poff + 512],
                            Wb(xof, OWN)[:, 128 * tt:128 * (tt + 1)],
                            Wb(wof, 512),
                            start=(k == 0 and pi == 0), stop=False),
                         reads=[(xkk, 0, 1), (wkk, 0, 1)],
                         writes=[(pkey, poff, poff + 512)], tag="ph7b_mm")
        p.op("sync", lambda e, og=og: e.dma_start(
                out=Wb(LB, 512), in_=bfh[:, 512 * og:512 * (og + 1)]),
             writes=[("lbh", 0, 1)], dma=True, tag="ph7b_lb")
        p.op("sync", lambda e, og=og: e.dma_start(
                out=Wb(LB + 256, 512), in_=bfl[:, 512 * og:512 * (og + 1)]),
             writes=[("lbl", 0, 1)], dma=True, tag="ph7b_lb")
        for tt in range(OT):
            ps, pkey = (psA, "psA") if tt % 2 == 0 else (psD, "psD")
            poff = 512 * ((tt // 2) % 4)
            for pi, (xof, xkk, wof, wkk) in enumerate((
                    (MIDSP, "midsh", LB, "lbh"),
                    (MIDSP + HW_, "midsl", LB, "lbh"),
                    (MIDSP, "midsh", LB + 256, "lbl"))):
                p.op("tensor", lambda e, tt=tt, pi=pi, xof=xof, wof=wof,
                        ps=ps, poff=poff: e.matmul(
                        ps[:, poff:poff + 512],
                        Wb(xof, NW)[:, 128 * tt:128 * (tt + 1)],
                        Wb(wof, 512),
                        start=False, stop=(pi == 2)),
                     reads=[(xkk, 0, 1), (wkk, 0, 1)],
                     writes=[(pkey, poff, poff + 512)], tag="ph7b_mlb")
        for tt in range(OT):
            ps, pkey = (psA, "psA") if tt % 2 == 0 else (psD, "psD")
            poff = 512 * ((tt // 2) % 4)
            p.op("vector", lambda e, tt=tt, og=og, ps=ps, poff=poff: e.tensor_tensor(
                    P(D * tt + 512 * og, 512), P(D * tt + 512 * og, 512),
                    ps[:, poff:poff + 512], ALU.add),
                 reads=[(pkey, poff, poff + 512),
                        ("x2", D * tt + 512 * og, D * tt + 512 * (og + 1))],
                 writes=[("x2", D * tt + 512 * og, D * tt + 512 * (og + 1))], tag="ph7b_add")

    _finish(p, nc, stack, y, pool, P, D, OT)
    return nc



def make_in_maps(cfg, x, Wqkv, Wo, base_W, router_W, lora_A, lora_B):
    D, S, NQ, NKV = cfg.D, cfg.S, cfg.NQ, cfg.NKV
    QH, KH = cfg.QH, cfg.KH
    C2, S2 = _rope_tables(cfg)
    ident = np.eye(128, dtype=np.float32)
    import ml_dtypes
    identb = np.eye(128, dtype=ml_dtypes.bfloat16)
    msk = np.where(np.arange(128)[None, :] > np.arange(128)[:, None],
                   np.float32(-1e30), np.float32(0.0)).astype(np.float32)
    rmat = np.zeros((128, 128), dtype=np.float32)
    for m in range(64):
        rmat[m + 64, m] = -1.0
        rmat[m, m + 64] = 1.0
    bc = np.zeros((8, 128), dtype=np.float32)
    for e in range(8):
        bc[e, 16 * e:16 * (e + 1)] = 1.0

    WqkvT = np.ascontiguousarray(Wqkv.T)
    WoT = np.ascontiguousarray(Wo.T)
    baseT = np.ascontiguousarray(base_W.T)
    routT = np.ascontiguousarray(router_W.T)
    aT = np.ascontiguousarray(lora_A.transpose(1, 0, 2).reshape(D, 128))
    bflat = np.ascontiguousarray(lora_B.reshape(128, D))

    qd, kd = NQ * 128, NKV * 128
    in_maps = []
    for c in range(8):
        b, h = c // 2, c % 2
        wq = WqkvT[:, 128 * QH * h:128 * QH * (h + 1)]
        wk_ = WqkvT[:, qd + 128 * KH * h: qd + 128 * KH * (h + 1)]
        wv = WqkvT[:, qd + kd + 128 * KH * h: qd + kd + 128 * KH * (h + 1)]
        m = {
            "xb": np.ascontiguousarray(x[b]),
            "xown": np.ascontiguousarray(x[b, (S // 2) * h:(S // 2) * (h + 1)]),
            "wqkvT": np.ascontiguousarray(np.concatenate([wq, wk_, wv], axis=1)),
            "woT": np.ascontiguousarray(WoT[128 * QH * h:128 * QH * (h + 1), :]),
            "baseT": baseT, "routT": routT, "aT": aT, "bflat": bflat,
            "ropeC": C2, "ropeS": S2, "cid": ident, "cbid": identb, "cmsk": msk,
            "crmat": rmat, "cbc": bc,
        }
        in_maps.append(m)
    return in_maps


def kernel(L0_attn_hidden_states, L0_attn_Wqkv, L0_attn_Wo,
           L0_ffn_base_W, L0_ffn_router_W, L0_ffn_lora_A, L0_ffn_lora_B):
    cfg = Cfg()
    x = np.ascontiguousarray(np.asarray(L0_attn_hidden_states, dtype=np.float32))
    in_maps = make_in_maps(cfg, x,
                           np.asarray(L0_attn_Wqkv, dtype=np.float32),
                           np.asarray(L0_attn_Wo, dtype=np.float32),
                           np.asarray(L0_ffn_base_W, dtype=np.float32),
                           np.asarray(L0_ffn_router_W, dtype=np.float32),
                           np.asarray(L0_ffn_lora_A, dtype=np.float32),
                           np.asarray(L0_ffn_lora_B, dtype=np.float32))
    nc = build_program(cfg)
    res = run_bass_kernel_spmd(nc, in_maps, core_ids=list(range(8)))
    out = np.empty((4, cfg.S, cfg.D), dtype=np.float32)
    half = cfg.S // 2
    for c in range(8):
        b, h = c // 2, c % 2
        out[b, half * h:half * (h + 1)] = res.results[c]["y"]
    return out


# version 24
# speedup vs baseline: 1.0951x; 1.0777x over previous
import numpy as np
import concourse.bass as bass
import concourse.mybir as mybir
from concourse.bass_utils import run_bass_kernel_spmd
from concourse.alu_op_type import AluOpType as ALU

F32 = mybir.dt.float32
BF16 = mybir.dt.bfloat16
AF = mybir.ActivationFunctionType
AX = mybir.AxisListType

ENGINES = ("sync", "tensor", "vector", "scalar", "gpsimd")
DMA_ENGINES = ("sync", "scalar", "gpsimd")



class _Op:
    __slots__ = ("eng", "fn", "deps", "inc", "seq", "dma", "dma_idx", "dkey",
                 "idx", "tag", "bar_waits")

    def __init__(self, eng, fn, dma, idx, tag):
        self.eng, self.fn, self.dma, self.idx, self.tag = eng, fn, dma, idx, tag
        self.deps = set()
        self.inc = False
        self.seq = 0
        self.dma_idx = -1
        self.dkey = None
        self.bar_waits = None


class Prog:
    def __init__(self):
        self.ops = []
        self.state = {}
        self.dma_count = {}

    def _st(self, key):
        return self.state.setdefault(key, {"w": [], "r": []})

    @staticmethod
    def _bank(key, lo, hi):
        if key.startswith("ps"):
            return (lo // 512) * 512, ((hi + 511) // 512) * 512
        return lo, hi

    def op(self, eng, fn, reads=(), writes=(), dma=False, tag=""):
        o = _Op(eng, fn, dma, len(self.ops), tag)
        reads = [(k,) + self._bank(k, lo, hi) for (k, lo, hi) in reads]
        writes = [(k,) + self._bank(k, lo, hi) for (k, lo, hi) in writes]
        deps = set()
        for (k, lo, hi) in reads:
            st = self._st(k)
            for (l, h, w) in st["w"]:
                if l < hi and lo < h:
                    deps.add(w)
        for (k, lo, hi) in writes:
            st = self._st(k)
            for (l, h, w) in st["w"]:
                if l < hi and lo < h:
                    deps.add(w)
            for (l, h, r) in st["r"]:
                if l < hi and lo < h:
                    deps.add(r)
        deps.discard(o)
        for (k, lo, hi) in reads:
            self._st(k)["r"].append((lo, hi, o))
        for (k, lo, hi) in writes:
            st = self._st(k)
            st["w"] = [e_ for e_ in st["w"] if not (lo <= e_[0] and e_[1] <= hi)]
            st["r"] = [e_ for e_ in st["r"] if not (lo <= e_[0] and e_[1] <= hi)]
            st["w"].append((lo, hi, o))
        o.deps = deps
        if dma:
            slot = writes[0][0] if writes else "anon"
            if reads and tag.endswith(("_st", "_stq", "_stv", "_sth", "_stl")):
                slot = reads[0][0]
            o.dkey = (eng, slot)
            o.dma_idx = self.dma_count.get(o.dkey, 0)
            self.dma_count[o.dkey] = o.dma_idx + 1
        self.ops.append(o)
        return o

    def barrier(self):
        snap_d = dict(self.dma_count)
        for e in ENGINES:
            o = _Op(e, lambda eng: eng.nop(), False, len(self.ops), "barrier")
            o.bar_waits = snap_d
            self.ops.append(o)
        self.state = {}

    def finalize(self):
        for o in self.ops:
            for d in o.deps:
                if d.dma:
                    continue
                if d.eng == "tensor" and o.eng == "tensor":
                    continue
                d.inc = True
        lastop = {}
        for o in self.ops:
            if o.bar_waits is not None:
                for e2, lo in lastop.items():
                    if not lo.dma:
                        lo.inc = True
                    o.deps.add(lo)
            else:
                lastop[o.eng] = o
        seq = {e: 0 for e in ENGINES}
        for o in self.ops:
            if o.inc and not o.dma:
                seq[o.eng] += 1
            o.seq = seq[o.eng]
        per_eng = {e: [] for e in ENGINES}
        for o in self.ops:
            per_eng[o.eng].append(o)
        self.per_eng = per_eng

    def emit(self, eng_name, eng, psems, dsems):
        water = {}
        for o in self.per_eng[eng_name]:
            waits = {}
            for d in o.deps:
                if d.eng == "tensor" and o.eng == "tensor" and not d.dma:
                    continue
                if d.dma:
                    key, val = ("d", d.dkey), 16 * (d.dma_idx + 1)
                else:
                    key, val = ("p", d.eng), d.seq
                if waits.get(key, 0) < val:
                    waits[key] = val
            if o.bar_waits is not None:
                for dk, cnt in o.bar_waits.items():
                    k = ("d", dk)
                    if waits.get(k, 0) < 16 * cnt:
                        waits[k] = 16 * cnt
            for key, val in sorted(waits.items(), key=str):
                if water.get(key, 0) >= val:
                    continue
                water[key] = val
                sem = dsems[key[1]] if key[0] == "d" else psems[key[1]]
                eng.wait_ge(sem, val)
            ins = o.fn(eng)
            if o.dma:
                ins.then_inc(dsems[o.dkey], 16)
            elif o.inc:
                ins.then_inc(psems[o.eng], 1)


def _finish(p, nc, stack, y, pool, P, D, OT):
    for tt in range(OT):
        p.op("gpsimd", lambda e, tt=tt: e.dma_start(
                out=y[128 * tt:128 * (tt + 1), :], in_=P(D * tt, D)),
             reads=[("x2", D * tt, D * (tt + 1))], writes=[("ydram", 0, 1)],
             dma=True, tag="ph8_st")
    p.barrier()
    p.finalize()
    psems = {e: stack.enter_context(nc.semaphore("p_" + e)) for e in ENGINES}
    dsems = {dk: stack.enter_context(nc.semaphore(f"d_{dk[0]}_{dk[1]}"))
             for dk in p.dma_count}
    with nc.Block() as block:
        @block.sync
        def _(eng):
            p.emit("sync", eng, psems, dsems)

        @block.tensor
        def _(eng):
            p.emit("tensor", eng, psems, dsems)

        @block.vector
        def _(eng):
            p.emit("vector", eng, psems, dsems)

        @block.scalar
        def _(eng):
            p.emit("scalar", eng, psems, dsems)

        @block.gpsimd
        def _(eng):
            p.emit("gpsimd", eng, psems, dsems)
    nc._stack = stack
    nc._prog = p



class Cfg:
    def __init__(self, D=4096, S=2048, NQ=32, NKV=8):
        self.D, self.S, self.NQ, self.NKV = D, S, NQ, NKV
        self.HD = 128
        self.E, self.R = 8, 16
        assert NQ // NKV == 4
        self.QH = NQ // 2
        self.KH = NKV // 2
        self.OD = (self.QH + 2 * self.KH) * 128
        self.KC = D // 128
        self.TT = S // 128
        self.OWN = S // 2
        self.OT = self.OWN // 128
        self.eps = 1e-5
        self.scale = float(1.0 / np.sqrt(128.0))


def _rope_tables(cfg):
    j = np.arange(64, dtype=np.float64)
    inv = 1.0 / (10000.0 ** (2.0 * j / 128.0))
    t = np.arange(cfg.S, dtype=np.float64)
    ang = np.outer(inv, t)
    C2 = np.concatenate([np.cos(ang)] * 2, axis=0).astype(np.float32)
    S2 = np.concatenate([np.sin(ang)] * 2, axis=0).astype(np.float32)
    return C2, S2


def _chunks(total, step):
    out = []
    o = 0
    while o < total:
        n = min(step, total - o)
        out.append((o, n))
        o += n
    return out



def build_program(cfg, stop_after=99):
    nc = bass.Bass("TRN2", target_bir_lowering=False)
    D, S = cfg.D, cfg.S
    KC, TT, OWN, OT = cfg.KC, cfg.TT, cfg.OWN, cfg.OT
    QH, KH, OD = cfg.QH, cfg.KH, cfg.OD
    OC = OD // 128

    def inp(name, shape, dt=F32):
        return nc.dram_tensor(name, list(shape), dt, kind="ExternalInput").ap()

    xb = inp("xb", (S, D))
    xown = inp("xown", (OWN, D))
    wqkvT = inp("wqkvT", (D, OD))
    woT = inp("woT", (QH * 128, D))
    baseT = inp("baseT", (D, D))
    routT = inp("routT", (D, 8))
    aT = inp("aT", (D, 128))
    bflat = inp("bflat", (128, D))
    ropeC = inp("ropeC", (128, S))
    ropeS = inp("ropeS", (128, S))
    cid = inp("cid", (128, 128))
    cbid = inp("cbid", (128, 128), BF16)
    cmsk = inp("cmsk", (128, 128))
    crmat = inp("crmat", (128, 128))
    cbc = inp("cbc", (8, 128))

    y = nc.dram_tensor("y", [OWN, D], F32, kind="ExternalOutput").ap()

    def scr(name, shape, dt=F32):
        return nc.dram_tensor(name, list(shape), dt, kind="Internal").ap()

    woh, wol = scr("woh", (QH * 128, D), BF16), scr("wol", (QH * 128, D), BF16)
    bah, bal = scr("bah", (D, D), BF16), scr("bal", (D, D), BF16)
    ath, atl = scr("ath", (D, 128), BF16), scr("atl", (D, 128), BF16)
    rth, rtl = scr("rth", (D, 8), BF16), scr("rtl", (D, 8), BF16)
    bfh, bfl = scr("bfh", (128, D), BF16), scr("bfl", (128, D), BF16)
    xn1h, xn1l = scr("xn1h", (D, S), BF16), scr("xn1l", (D, S), BF16)
    qkh = scr("qkh", ((QH + KH) * 128, S), BF16)
    qkl = scr("qkl", ((QH + KH) * 128, S), BF16)
    vtms = scr("vtms", (S, KH * 128))
    atth, attl = scr("atth", (QH * 128, S), BF16), scr("attl", (QH * 128, S), BF16)
    _qch = [(0, S)]
    partsq = [scr(f"parts{i}", (tqn, D)) for i, (tq0, tqn) in enumerate(_qch)]
    rsoutsq = [scr(f"rsouts{i}", (tqn // 2, D)) for i, (tq0, tqn) in enumerate(_qch)]
    xn2h, xn2l = scr("xn2h", (D, OWN), BF16), scr("xn2l", (D, OWN), BF16)

    import contextlib
    stack = contextlib.ExitStack()
    sb = lambda name, shape, dt=F32: stack.enter_context(nc.sbuf_tensor(name, list(shape), dt))

    s_id = sb("s_id", (128, 128))
    s_bid = sb("s_bid", (128, 128), BF16)
    s_msk = sb("s_msk", (128, 128))
    s_R = sb("s_R", (128, 128))
    s_bc = sb("s_bc", (8, 128))
    s_C = sb("s_C", (128, S))
    s_S = sb("s_S", (128, S))

    SPL = min(3072, max(OD, D))
    PH0NEED = 512 + 2 * SPL + SPL
    WQF = PH0NEED
    POOLC = max(OT * D, 6 * D, 12 * S, 2 * QH * 512, 4096)
    WKC = max(3 * D, PH0NEED + 512, 7 * OWN + 512, 4096)
    pool = sb("pool", (128, POOLC))
    wk = sb("wk", (128, WKC))
    sm = sb("sm", (128, 64))
    lgsm = sb("lgsm", (128, 8 * OT * 2 + 32))

    psA = stack.enter_context(nc.psum_tensor("psA", [128, 2048], F32))
    psD = stack.enter_context(nc.psum_tensor("psD", [128, 2048], F32))

    p = Prog()

    def P(lo, n):
        return pool[:, lo:lo + n]

    def Pb(lo, nb):
        return pool[:, lo:lo + nb // 2].bitcast(BF16)

    def W(lo, n):
        return wk[:, lo:lo + n]

    def Wb(lo, nb):
        return wk[:, lo:lo + nb // 2].bitcast(BF16)

    for t, src, key in ((s_id, cid, "c_id"), (s_bid, cbid, "c_bid"),
                        (s_msk, cmsk, "c_msk"), (s_R, crmat, "c_R"),
                        (s_bc, cbc, "c_bc"), (s_C, ropeC, "c_C"),
                        (s_S, ropeS, "c_S")):
        p.op("sync", lambda e, t=t, src=src: e.dma_start(out=t[:], in_=src[:]),
             writes=[(key, 0, 1)], dma=True, tag="const")

    def split_jobs(src, dsth, dstl, R, C, base, tg, eng=("scalar", "vector")):
        jobs = []
        ceng, seng = eng
        for ri, r0 in enumerate(range(0, R, 128)):
            for (c0, cn) in _chunks(C, SPL):
                def job(r0=r0, c0=c0, cn=cn, sl=ri % 2, src=src, dsth=dsth,
                        dstl=dstl, base=base, tg=tg):
                    ik = (f"sp_i{sl}", 0, 1)
                    ldq = "gpsimd" if ceng == "gpsimd" else "sync"
                    p.op(ldq, lambda e: e.dma_start(
                            out=W(base + sl * SPL, cn), in_=src[r0:r0 + 128, c0:c0 + cn]),
                         writes=[ik], dma=True, tag=f"{tg}_ld")
                    for (p0, pn) in _chunks(cn, 1024):
                        def _cast(e, p0=p0, pn=pn):
                            dst = Wb(base + 2 * SPL + p0 // 2, pn)
                            srcv = W(base + sl * SPL + p0, pn)
                            return e.copy(dst, srcv) if ceng == "scalar" else e.tensor_copy(dst, srcv)
                        p.op(ceng, _cast,
                             reads=[ik], writes=[("sp_h", 0, 1)], tag=f"{tg}_hi")
                        p.op(seng, lambda e, p0=p0, pn=pn: e.tensor_tensor(
                                Wb(base + 2 * SPL + SPL // 2 + p0 // 2, pn),
                                W(base + sl * SPL + p0, pn),
                                Wb(base + 2 * SPL + p0 // 2, pn), ALU.subtract),
                             reads=[ik, ("sp_h", 0, 1)], writes=[("sp_l", 0, 1)], tag=f"{tg}_lo")
                    p.op("gpsimd", lambda e: e.dma_start(
                            out=dsth[r0:r0 + 128, c0:c0 + cn], in_=Wb(base + 2 * SPL, cn)),
                         reads=[("sp_h", 0, 1)], writes=[("wsp", 0, 1)], dma=True, tag=f"{tg}_sth")
                    p.op("gpsimd", lambda e: e.dma_start(
                            out=dstl[r0:r0 + 128, c0:c0 + cn], in_=Wb(base + 2 * SPL + SPL // 2, cn)),
                         reads=[("sp_l", 0, 1)], writes=[("wsp", 0, 1)], dma=True, tag=f"{tg}_stl")
                jobs.append(job)
        return jobs

    XH, XL, FH, FL = 4 * D, 4 * D + D // 2, 5 * D, 5 * D + D // 2
    for tt in range(TT):
        sl = tt % 2
        xtm = (f"xtm{sl}", 0, 1)
        p.op("sync" if tt % 2 == 0 else "gpsimd",
             lambda e, tt=tt, sl=sl: e.dma_start(
                out=P(sl * D, D), in_=xb[128 * tt:128 * (tt + 1), :]),
             writes=[xtm], dma=True, tag="ph1_load")
        ms = tt % 2
        p.op("scalar", lambda e, sl=sl, ms=ms: e.activation(
                P(2 * D, D), P(sl * D, D), AF.Square, accum_out=sm[:, 2 + ms:3 + ms]),
             reads=[xtm], writes=[("xsq", 0, 1), ("sx2", ms, ms + 1)], tag="ph1_sq")
        p.op("vector", lambda e, sl=sl, ms=ms: e.reduce_sum(
                sm[:, 0 + ms:1 + ms], P(sl * D, D), axis=AX.X),
             reads=[xtm], writes=[("sx", ms, ms + 1)], tag="ph1_sum")
        p.op("vector", lambda e, ms=ms: e.tensor_scalar(
                sm[:, 4 + ms:5 + ms], sm[:, 0 + ms:1 + ms], 1.0 / D, None, ALU.mult),
             reads=[("sx", ms, ms + 1)], writes=[("mean", ms, ms + 1)], tag="ph1_mean")
        p.op("vector", lambda e, ms=ms: e.tensor_tensor(
                sm[:, 6 + ms:7 + ms], sm[:, 4 + ms:5 + ms], sm[:, 4 + ms:5 + ms], ALU.mult),
             reads=[("mean", ms, ms + 1)], writes=[("msq", ms, ms + 1)], tag="ph1_msq")
        p.op("vector", lambda e, ms=ms: e.tensor_scalar(
                sm[:, 8 + ms:9 + ms], sm[:, 2 + ms:3 + ms], 1.0 / D, None, ALU.mult),
             reads=[("sx2", ms, ms + 1)], writes=[("var0", ms, ms + 1)], tag="ph1_var0")
        p.op("vector", lambda e, ms=ms: e.tensor_scalar(
                sm[:, 10 + ms:11 + ms], sm[:, 8 + ms:9 + ms],
                sm[:, 6 + ms:7 + ms], cfg.eps, ALU.subtract, ALU.add),
             reads=[("var0", ms, ms + 1), ("msq", ms, ms + 1)],
             writes=[("vpe", ms, ms + 1)], tag="ph1_var")
        p.op("scalar", lambda e, ms=ms: e.activation(
                sm[:, 12 + ms:13 + ms], sm[:, 10 + ms:11 + ms], AF.Ln),
             reads=[("vpe", ms, ms + 1)], writes=[("lnv", ms, ms + 1)], tag="ph1_ln")
        p.op("scalar", lambda e, ms=ms: e.activation(
                sm[:, 14 + ms:15 + ms], sm[:, 12 + ms:13 + ms], AF.Exp, scale=-0.5),
             reads=[("lnv", ms, ms + 1)], writes=[("rstd", ms, ms + 1)], tag="ph1_rstd")
        p.op("vector", lambda e, sl=sl, ms=ms: e.tensor_scalar(
                P(3 * D, D), P(sl * D, D), sm[:, 4 + ms:5 + ms],
                sm[:, 14 + ms:15 + ms], ALU.subtract, ALU.mult),
             reads=[xtm, ("mean", ms, ms + 1), ("rstd", ms, ms + 1)],
             writes=[("xnt", 0, 1)], tag="ph1_apply")
        p.op("scalar", lambda e: e.copy(Pb(XH, D), P(3 * D, D)),
             reads=[("xnt", 0, 1)], writes=[("xnth", 0, 1)], tag="ph1_casth")
        p.op("vector", lambda e: e.tensor_tensor(
                Pb(XL, D), P(3 * D, D), Pb(XH, D), ALU.subtract),
             reads=[("xnt", 0, 1), ("xnth", 0, 1)], writes=[("xntl", 0, 1)], tag="ph1_castl")
        for hl, (srcoff, srck, trt, trk, fmoff, fmk) in enumerate(
                ((XH, "xnth", psA, "psA", FH, "fmh"), (XL, "xntl", psD, "psD", FL, "fml"))):
            for c in range(KC):
                bank = (c // 8) % 4
                slot = c % 8
                pcol = 512 * bank + 64 * slot
                p.op("tensor", lambda e, c=c, srcoff=srcoff, trt=trt, pcol=pcol: e.transpose(
                        trt[:, pcol:pcol + 64].bitcast(BF16),
                        Pb(srcoff, D)[:, 128 * c:128 * (c + 1)], s_bid[:]),
                     reads=[(srck, 0, 1), ("c_bid", 0, 1)],
                     writes=[(trk, pcol, pcol + 64)], tag="ph1_tr")
                if slot == 7 or c == KC - 1:
                    c0 = (c // 8) * 8
                    ncp = c - c0 + 1
                    p.op("vector", lambda e, c0=c0, ncp=ncp, bank=bank, trt=trt, fmoff=fmoff: e.tensor_copy(
                            Pb(fmoff + 64 * c0, 128 * ncp),
                            trt[:, 512 * bank:512 * bank + 64 * ncp].bitcast(BF16)),
                         reads=[(trk, 512 * bank, 512 * bank + 64 * ncp)],
                         writes=[(fmk, 64 * c0, 64 * (c0 + ncp))], tag="ph1_ev")
            dstf = (xn1h if hl == 0 else xn1l).rearrange("(c q) t -> q c t", q=128)[
                :, :, 128 * tt:128 * (tt + 1)]
            srcf = Pb(fmoff, D).rearrange("q (c j) -> q c j", c=KC)
            for c0 in range(0, KC, 16):
                c1 = min(c0 + 16, KC)
                p.op("scalar" if hl == 0 else "sync",
                     lambda e, dst=dstf[:, c0:c1, :], src=srcf[:, c0:c1, :]:
                        e.dma_start(out=dst, in_=src),
                     reads=[(fmk, 64 * c0, 64 * c1)], writes=[("xn1s", 0, 1)],
                     dma=True, tag="ph1_st")

    p.barrier()

    if stop_after < 2:
        _finish(p, nc, stack, y, pool, P, D, OT)
        return nc


    def qkv_store(oc):
        is_q = oc < QH
        is_k = QH <= oc < QH + KH
        psrc, pkey = (psA, "psA") if oc % 2 == 0 else (psD, "psD")
        if is_q or is_k:
            if is_q:
                p.op("vector", lambda e, psrc=psrc: e.tensor_scalar(
                        P(2 * S, S), psrc[:, 0:S], cfg.scale, None, ALU.mult),
                     reads=[(pkey, 0, S)], writes=[("qraw", 0, 1)], tag="ph2_evq")
            else:
                p.op("scalar", lambda e, psrc=psrc: e.copy(P(2 * S, S), psrc[:, 0:S]),
                     reads=[(pkey, 0, S)], writes=[("qraw", 0, 1)], tag="ph2_evk")
            for (t0, n) in _chunks(S, 512):
                p.op("tensor", lambda e, t0=t0, n=n, psrc=psrc: e.matmul(
                        psrc[:, t0:t0 + n], s_R[:], P(2 * S + t0, n),
                        start=True, stop=True),
                     reads=[("qraw", 0, 1), ("c_R", 0, 1)],
                     writes=[(pkey, t0, t0 + n)], tag="ph2_rmm")
            p.op("vector", lambda e: e.tensor_tensor(
                    P(3 * S, S), P(2 * S, S), s_C[:], ALU.mult),
                 reads=[("qraw", 0, 1), ("c_C", 0, 1)], writes=[("q1", 0, 1)], tag="ph2_qc")
            p.op("vector", lambda e, psrc=psrc: e.tensor_tensor(
                    P(4 * S, S), psrc[:, 0:S], s_S[:], ALU.mult),
                 reads=[(pkey, 0, S), ("c_S", 0, 1)], writes=[("q2", 0, 1)], tag="ph2_qs")
            p.op("vector", lambda e: e.tensor_tensor(
                    P(5 * S, S), P(3 * S, S), P(4 * S, S), ALU.add),
                 reads=[("q1", 0, 1), ("q2", 0, 1)], writes=[("qout", 0, 1)], tag="ph2_qo")
            p.op("scalar", lambda e: e.copy(Pb(6 * S, S), P(5 * S, S)),
                 reads=[("qout", 0, 1)], writes=[("qouth", 0, 1)], tag="ph2_qh")
            p.op("vector", lambda e: e.tensor_tensor(
                    Pb(6 * S + S // 2, S), P(5 * S, S), Pb(6 * S, S), ALU.subtract),
                 reads=[("qout", 0, 1), ("qouth", 0, 1)], writes=[("qoutl", 0, 1)], tag="ph2_ql")
            row = 128 * oc
            p.op("gpsimd", lambda e, row=row: e.dma_start(
                    out=qkh[row:row + 128, :], in_=Pb(6 * S, S)),
                 reads=[("qouth", 0, 1)], writes=[("qks", 0, 1)], dma=True, tag="ph2_stq")
            p.op("gpsimd", lambda e, row=row: e.dma_start(
                    out=qkl[row:row + 128, :], in_=Pb(6 * S + S // 2, S)),
                 reads=[("qoutl", 0, 1)], writes=[("qks", 0, 1)], dma=True, tag="ph2_stq")
        else:
            vc = oc - QH - KH
            p.op("scalar", lambda e, psrc=psrc: e.copy(P(2 * S, S), psrc[:, 0:S]),
                 reads=[(pkey, 0, S)], writes=[("qraw", 0, 1)], tag="ph2_evv")
            for tb in range(TT):
                bank = tb % 4
                p.op("tensor", lambda e, tb=tb, bank=bank, psrc=psrc: e.transpose(
                        psrc[:, 512 * bank:512 * bank + 128], P(2 * S + 128 * tb, 128), s_id[:]),
                     reads=[("qraw", 0, 1), ("c_id", 0, 1)],
                     writes=[(pkey, 512 * bank, 512 * bank + 128)], tag="ph2_vtr")
                p.op("vector", lambda e, tb=tb, bank=bank, psrc=psrc: e.tensor_copy(
                        P(3 * S + 128 * tb, 128), psrc[:, 512 * bank:512 * bank + 128]),
                     reads=[(pkey, 512 * bank, 512 * bank + 128)],
                     writes=[("q1", 128 * tb, 128 * (tb + 1))], tag="ph2_vev")
            dstf = vtms.rearrange("(tb q) v -> q tb v", q=128)[:, :, 128 * vc:128 * (vc + 1)]
            srcf = P(3 * S, S).rearrange("q (tb j) -> q tb j", tb=TT)
            for b0 in range(0, TT, 8):
                b1 = min(b0 + 8, TT)
                p.op("gpsimd", lambda e, dst=dstf[:, b0:b1, :], src=srcf[:, b0:b1, :]:
                        e.dma_start(out=dst, in_=src),
                     reads=[("q1", 0, S)], writes=[("vtms", 0, 1)], dma=True, tag="ph2_stv")

    GG = ("gpsimd", "gpsimd")
    _ph0b = (split_jobs(woT, woh, wol, QH * 128, D, 512, "ph0b", GG)
             + split_jobs(baseT, bah, bal, D, D, 512, "ph0c", GG)
             + split_jobs(aT, ath, atl, D, 128, 512, "ph0d", GG)
             + split_jobs(routT, rth, rtl, D, 8, 512, "ph0e", GG)
             + split_jobs(bflat, bfh, bfl, 128, D, 512, "ph0f", GG))

    SB = S // 2
    _job_every = max(1, ((OC // 2) * KC) // max(1, (2 * len(_ph0b) // 3)))
    for og in range(OC // 2):
        for k in range(KC):
            if k % _job_every == _job_every - 1 and _ph0b and og > 0:
                _ph0b.pop(0)()
            wsl = k % 2
            p.op("scalar", lambda e, og=og, k=k, wsl=wsl: e.dma_start(
                    out=W(WQF + 256 * wsl, 256),
                    in_=wqkvT[128 * k:128 * (k + 1), 256 * og:256 * (og + 1)]),
                 writes=[(f"wpf{wsl}", 0, 1)], dma=True, tag="ph2_w")
            p.op("scalar", lambda e, wsl=wsl: e.copy(
                    Wb(256 * wsl, 256), W(WQF + 256 * wsl, 256)),
                 reads=[(f"wpf{wsl}", 0, 1)],
                 writes=[(f"wph{wsl}", 0, 1)], tag="ph2_wh")
            p.op("vector", lambda e, wsl=wsl: e.tensor_tensor(
                    Wb(256 * wsl + 128, 256), W(WQF + 256 * wsl, 256),
                    Wb(256 * wsl, 256), ALU.subtract),
                 reads=[(f"wpf{wsl}", 0, 1), (f"wph{wsl}", 0, 1)],
                 writes=[(f"wpl{wsl}", 0, 1)], tag="ph2_wl")
            msl = k % 2
            p.op("scalar", lambda e, k=k, msl=msl: e.dma_start(
                    out=Pb(msl * S, S), in_=xn1h[128 * k:128 * (k + 1), :]),
                 reads=[("xn1s", 0, 1)], writes=[(f"mvh{msl}", 0, 1)], dma=True, tag="ph2_x")
            p.op("sync", lambda e, k=k, msl=msl: e.dma_start(
                    out=Pb(msl * S + SB, S), in_=xn1l[128 * k:128 * (k + 1), :]),
                 reads=[("xn1s", 0, 1)], writes=[(f"mvl{msl}", 0, 1)], dma=True, tag="ph2_x")
            whofs = 512 * wsl
            wlofs = 512 * wsl + 256
            for oci, ps, pkey in ((0, psA, "psA"), (1, psD, "psD")):
                for (t0, n) in _chunks(S, 512):
                    for pi, (wof, wkk, mof, mkk) in enumerate((
                            (whofs + 128 * oci, f"wph{wsl}", msl * S, f"mvh{msl}"),
                            (whofs + 128 * oci, f"wph{wsl}", msl * S + SB, f"mvl{msl}"),
                            (wlofs + 128 * oci, f"wpl{wsl}", msl * S, f"mvh{msl}"))):
                        p.op("tensor", lambda e, k=k, t0=t0, n=n, ps=ps, pi=pi,
                                wof=wof, mof=mof: e.matmul(
                                ps[:, t0:t0 + n],
                                Wb(0, 1024)[:, wof:wof + 128],
                                Pb(mof, S)[:, t0:t0 + n],
                                start=(k == 0 and pi == 0),
                                stop=(k == KC - 1 and pi == 2)),
                             reads=[(wkk, 0, 1), (mkk, 0, 1)],
                             writes=[(pkey, t0, t0 + n)], tag="ph2_mm")
        qkv_store(2 * og)
        qkv_store(2 * og + 1)

    p.barrier()

    if stop_after < 3:
        _finish(p, nc, stack, y, pool, P, D, OT)
        return nc

    for j in range(KH):
        jp = j % 2
        kb_ = 0 if jp == 0 else 10 * S
        vb_ = S if jp == 0 else 11 * S
        p.op("sync", lambda e, j=j, kb_=kb_: e.dma_start(
                out=Pb(kb_, S), in_=qkh[128 * (QH + j):128 * (QH + j + 1), :]),
             reads=[("qks", 0, 1)], writes=[(f"aKh{jp}", 0, 1)], dma=True, tag="ph3_K")
        p.op("sync", lambda e, j=j, kb_=kb_: e.dma_start(
                out=Pb(kb_ + SB, S), in_=qkl[128 * (QH + j):128 * (QH + j + 1), :]),
             reads=[("qks", 0, 1)], writes=[(f"aKl{jp}", 0, 1)], dma=True, tag="ph3_K2")
        vsrc = vtms.rearrange("(tb q) v -> q tb v", q=128)[:, :, 128 * j:128 * (j + 1)]
        vdst = P(vb_, S).rearrange("q (tb j) -> q tb j", tb=TT)
        for b0 in range(0, TT, 8):
            b1 = min(b0 + 8, TT)
            p.op("sync", lambda e, src=vsrc[:, b0:b1, :], dst=vdst[:, b0:b1, :]:
                    e.dma_start(out=dst, in_=src),
                 reads=[("vtms", 0, 1)], writes=[(f"aV{jp}", 0, 1)], dma=True, tag="ph3_V")
        for qh in range(4):
            hq = 4 * j + qh
            hp = hq % 2
            qb_ = 2 * S if hp == 0 else 9 * S
            nper3 = max(1, len(_ph0b) // max(1, (KH - j) * 4 - qh))
            for _ in range(min(nper3, len(_ph0b))):
                _ph0b.pop(0)()
            p.op("sync", lambda e, hq=hq, qb_=qb_: e.dma_start(
                    out=Pb(qb_, S), in_=qkh[128 * hq:128 * (hq + 1), :]),
                 reads=[("qks", 0, 1)], writes=[(f"aQh{hp}", 0, 1)], dma=True, tag="ph3_Q")
            p.op("sync", lambda e, hq=hq, qb_=qb_: e.dma_start(
                    out=Pb(qb_ + SB, S), in_=qkl[128 * hq:128 * (hq + 1), :]),
                 reads=[("qks", 0, 1)], writes=[(f"aQl{hp}", 0, 1)], dma=True, tag="ph3_Q2")
            def emit_scores(g):
                klen = 128 * (g + 1)
                nblk = (g + 4) // 4
                par = g % 2
                smb = 16 + 10 * par
                for bi, kb in enumerate(range(0, g + 1, 4)):
                    n = min(512, klen - 128 * kb)
                    for pi, (qof, qk_, kof, kk_) in enumerate((
                            (qb_, f"aQh{hp}", kb_, f"aKh{jp}"),
                            (qb_ + SB, f"aQl{hp}", kb_, f"aKh{jp}"),
                            (qb_, f"aQh{hp}", kb_ + SB, f"aKl{jp}"))):
                        p.op("tensor", lambda e, g=g, kb=kb, n=n, pi=pi, qof=qof, kof=kof: e.matmul(
                                psA[:, 128 * kb:128 * kb + n],
                                Pb(qof, S)[:, 128 * g:128 * (g + 1)],
                                Pb(kof, S)[:, 128 * kb:128 * kb + n],
                                start=(pi == 0), stop=(pi == 2)),
                             reads=[(qk_, 0, 1), (kk_, 0, 1)],
                             writes=[("psA", 128 * kb, 128 * kb + n)], tag="ph3_s")
                    if kb + 4 <= g:
                        p.op("vector", lambda e, kb=kb, n=n, bi=bi, smb=smb: e.reduce_max(
                                sm[:, smb + bi:smb + bi + 1], psA[:, 128 * kb:128 * kb + n], axis=AX.X),
                             reads=[("psA", 128 * kb, 128 * kb + n)],
                             writes=[("rm", smb + bi, smb + bi + 1)], tag="ph3_m1")
                p.op("vector", lambda e, g=g: e.tensor_tensor(
                        psA[:, 128 * g:128 * (g + 1)], psA[:, 128 * g:128 * (g + 1)],
                        s_msk[:], ALU.add),
                     reads=[("psA", 128 * g, 128 * (g + 1)), ("c_msk", 0, 1)],
                     writes=[("psA", 128 * g, 128 * (g + 1))], tag="ph3_msk")
                lb = (g // 4) * 4
                lbn = klen - 128 * lb
                p.op("vector", lambda e, lb=lb, lbn=lbn, nblk=nblk, smb=smb: e.reduce_max(
                        sm[:, smb + nblk - 1:smb + nblk], psA[:, 128 * lb:128 * lb + lbn], axis=AX.X),
                     reads=[("psA", 128 * lb, 128 * lb + lbn)],
                     writes=[("rm", smb + nblk - 1, smb + nblk)], tag="ph3_m2")
                if nblk > 1:
                    p.op("vector", lambda e, nblk=nblk, smb=smb: e.reduce_max(
                            sm[:, smb + 4:smb + 5], sm[:, smb:smb + nblk], axis=AX.X, negate=True),
                         reads=[("rm", smb, smb + nblk)],
                         writes=[("negmx", smb, smb + 1)], tag="ph3_mx")
                else:
                    p.op("vector", lambda e, smb=smb: e.tensor_scalar(
                            sm[:, smb + 4:smb + 5], sm[:, smb:smb + 1], -1.0, None, ALU.mult),
                         reads=[("rm", smb, smb + 1)], writes=[("negmx", smb, smb + 1)], tag="ph3_mx")

            def emit_soft(g):
                klen = 128 * (g + 1)
                par = g % 2
                smb = 16 + 10 * par
                pnoff = 4 * S if par == 0 else 8 * S
                p.op("scalar", lambda e, klen=klen, smb=smb: e.activation(
                        P(3 * S, klen), psA[:, 0:klen], AF.Exp,
                        bias=sm[:, smb + 4:smb + 5], scale=1.0, accum_out=sm[:, smb + 5:smb + 6]),
                     reads=[("psA", 0, klen), ("negmx", smb, smb + 1)],
                     writes=[("aP", 0, klen), ("ssum", smb, smb + 1)], tag="ph3_exp")
                p.op("vector", lambda e, smb=smb: e.reciprocal(
                        sm[:, smb + 6:smb + 7], sm[:, smb + 5:smb + 6]),
                     reads=[("ssum", smb, smb + 1)], writes=[("rinv", smb, smb + 1)], tag="ph3_ri")
                p.op("vector", lambda e, klen=klen, smb=smb, pnoff=pnoff: e.tensor_scalar(
                        P(pnoff, klen), P(3 * S, klen), sm[:, smb + 6:smb + 7], None, ALU.mult),
                     reads=[("aP", 0, klen), ("rinv", smb, smb + 1)],
                     writes=[(f"aPn{par}", 0, klen)], tag="ph3_n")

            def emit_ptpv(g):
                par = g % 2
                pnoff = 4 * S if par == 0 else 8 * S
                for kb in range(g + 1):
                    pr = (kb // 2) % 2
                    tco = (512 if pr == 0 else 1536) + 128 * (kb % 2)
                    p.op("tensor", lambda e, kb=kb, tco=tco, pnoff=pnoff: e.transpose(
                            psD[:, tco:tco + 128],
                            P(pnoff + 128 * kb, 128), s_id[:]),
                         reads=[(f"aPn{par}", 128 * kb, 128 * (kb + 1)), ("c_id", 0, 1)],
                         writes=[("psD", tco, tco + 128)], tag="ph3_tr")
                    if kb % 2 == 1 or kb == g:
                        kb0 = kb - (kb % 2)
                        n2 = 128 * (kb - kb0 + 1)
                        eco = 512 if pr == 0 else 1536
                        if pr == 0:
                            p.op("vector", lambda e, kb0=kb0, n2=n2, eco=eco: e.tensor_copy(
                                    P(5 * S + 128 * kb0, n2), psD[:, eco:eco + n2]),
                                 reads=[("psD", eco, eco + n2)],
                                 writes=[("aPT", 128 * kb0, 128 * kb0 + n2)], tag="ph3_trev")
                        else:
                            p.op("scalar", lambda e, kb0=kb0, n2=n2, eco=eco: e.copy(
                                    P(5 * S + 128 * kb0, n2), psD[:, eco:eco + n2]),
                                 reads=[("psD", eco, eco + n2)],
                                 writes=[("aPT", 128 * kb0, 128 * kb0 + n2)], tag="ph3_trev")
                for kb in range(g + 1):
                    p.op("tensor", lambda e, kb=kb, g=g, vb_=vb_: e.matmul(
                            psD[:, 1024:1152], P(vb_ + 128 * kb, 128),
                            P(5 * S + 128 * kb, 128), start=(kb == 0), stop=(kb == g)),
                         reads=[(f"aV{jp}", 0, 1), ("aPT", 128 * kb, 128 * (kb + 1))],
                         writes=[("psD", 1024, 1152)], tag="ph3_pv")
                p.op("vector", lambda e, g=g: e.tensor_copy(
                        P(6 * S + 128 * g, 128), psD[:, 1024:1152]),
                     reads=[("psD", 1024, 1152)],
                     writes=[("aao", 128 * g, 128 * (g + 1))], tag="ph3_aev")

            for g in range(TT):
                emit_scores(g)
                if g > 0:
                    emit_ptpv(g - 1)
                emit_soft(g)
            emit_ptpv(TT - 1)
            p.op("scalar", lambda e: e.copy(Pb(7 * S, S), P(6 * S, S)),
                 reads=[("aao", 0, S)], writes=[("aoh", 0, 1)], tag="ph3_aoh")
            p.op("vector", lambda e: e.tensor_tensor(
                    Pb(7 * S + SB, S), P(6 * S, S), Pb(7 * S, S), ALU.subtract),
                 reads=[("aao", 0, S), ("aoh", 0, 1)], writes=[("aol", 0, 1)], tag="ph3_aol")
            p.op("gpsimd", lambda e, hq=hq: e.dma_start(
                    out=atth[128 * hq:128 * (hq + 1), :], in_=Pb(7 * S, S)),
                 reads=[("aoh", 0, 1)], writes=[("attns", 0, 1)], dma=True, tag="ph3_st")
            p.op("gpsimd", lambda e, hq=hq: e.dma_start(
                    out=attl[128 * hq:128 * (hq + 1), :], in_=Pb(7 * S + SB, S)),
                 reads=[("aol", 0, 1)], writes=[("attns", 0, 1)], dma=True, tag="ph3_st")

    for j_ in _ph0b:
        j_()
    _ph0b.clear()

    p.barrier()

    if stop_after < 4:
        _finish(p, nc, stack, y, pool, P, D, OT)
        return nc

    for tqi, (tq0, tqn) in enumerate(_chunks(S, 512)):
        tqp = tqi % 2
        ATB = 0 if tqp == 0 else QH * tqn
        ATH, ATL = ATB, ATB + QH * tqn // 2
        for hl, (src, dstoff, kk_) in enumerate(((atth, ATH, f"wath{tqp}"), (attl, ATL, f"watl{tqp}"))):
            asrc = src.rearrange("(kc q) t -> q kc t", q=128)[:, :, tq0:tq0 + tqn]
            adst = Pb(dstoff, QH * tqn).rearrange("q (kc j) -> q kc j", kc=QH)
            for b0 in range(0, QH, 8):
                b1 = min(b0 + 8, QH)
                p.op("sync", lambda e, src2=asrc[:, b0:b1, :], dst2=adst[:, b0:b1, :]:
                        e.dma_start(out=dst2, in_=src2),
                     reads=[("attns", 0, 1)], writes=[(kk_, 0, 1)], dma=True, tag="ph4_a")
        for og in range(D // 512):
            par = og % 2
            ps, pkey = (psA, "psA") if par == 0 else (psD, "psD")
            for k in range(QH):
                wsl = k % 2
                p.op("sync", lambda e, k=k, og=og, wsl=wsl: e.dma_start(
                        out=Wb(512 * wsl, 512),
                        in_=woh[128 * k:128 * (k + 1), 512 * og:512 * (og + 1)]),
                     reads=[("woh_s", 0, 1)],
                     writes=[(f"woh{wsl}", 0, 1)], dma=True, tag="ph4_w")
                p.op("sync", lambda e, k=k, og=og, wsl=wsl: e.dma_start(
                        out=Wb(512 * wsl + 256, 512),
                        in_=wol[128 * k:128 * (k + 1), 512 * og:512 * (og + 1)]),
                     reads=[("wol_s", 0, 1)],
                     writes=[(f"wol{wsl}", 0, 1)], dma=True, tag="ph4_w")
                for tt in range(tqn // 128):
                    for pi, (aof, akk, wof, wkk) in enumerate((
                            (ATH, f"wath{tqp}", 512 * wsl, f"woh{wsl}"),
                            (ATL, f"watl{tqp}", 512 * wsl, f"woh{wsl}"),
                            (ATH, f"wath{tqp}", 512 * wsl + 256, f"wol{wsl}"))):
                        p.op("tensor", lambda e, k=k, tt=tt, pi=pi, aof=aof, wof=wof,
                                ps=ps, tqn=tqn: e.matmul(
                                ps[:, 512 * tt:512 * (tt + 1)],
                                Pb(aof, QH * tqn)[:, tqn * k + 128 * tt:tqn * k + 128 * (tt + 1)],
                                Wb(wof, 512),
                                start=(k == 0 and pi == 0),
                                stop=(k == QH - 1 and pi == 2)),
                             reads=[(akk, 0, 1), (wkk, 0, 1)],
                             writes=[(pkey, 512 * tt, 512 * (tt + 1))], tag="ph4_mm")
            for tt in range(tqn // 128):
                psl = tt % 2
                p.op("vector", lambda e, tt=tt, psl=psl, ps=ps: e.tensor_copy(
                        W(1024 + 512 * psl, 512), ps[:, 512 * tt:512 * (tt + 1)]),
                     reads=[(pkey, 512 * tt, 512 * (tt + 1))],
                     writes=[(f"po{psl}", 0, 1)], tag="ph4_ev")
                qi = next(i for i, (q0, qn) in enumerate(_qch) if q0 <= tq0 < q0 + qn)
                rr = tq0 - _qch[qi][0]
                p.op("scalar", lambda e, qi=qi, rr=rr, og=og, tt=tt, psl=psl: e.dma_start(
                        out=partsq[qi][rr + 128 * tt:rr + 128 * (tt + 1),
                                       512 * og:512 * (og + 1)],
                        in_=W(1024 + 512 * psl, 512)),
                     reads=[(f"po{psl}", 0, 1)], writes=[(f"parts{qi}", 0, 1)],
                     dma=True, tag="ph4_st")
        qi = next(i for i, (q0, qn) in enumerate(_qch) if q0 <= tq0 < q0 + qn)
        if tq0 + tqn != _qch[qi][0] + _qch[qi][1]:
            continue_rs = True
        else:
            continue_rs = False
        if continue_rs:
            pass
        else:
            p.op("gpsimd", lambda e, qi=qi: e.collective_compute(
                    "ReduceScatter", ALU.add,
                    replica_groups=[[0, 1], [2, 3], [4, 5], [6, 7]],
                    ins=[partsq[qi][:]],
                    outs=[rsoutsq[qi][:]]),
                 reads=[(f"parts{qi}", 0, 1)],
                 writes=[(f"rsouts{qi}", 0, 1)], tag="ph5_rs")

    TQN = min(512, S)
    for tt in range(OT):
        wkeys = [("x2", D * tt, D * (tt + 1))]
        if D * tt < 2 * QH * TQN:
            for q_ in (0, 1):
                if D * tt < QH * TQN * (q_ + 1) and D * (tt + 1) > QH * TQN * q_:
                    wkeys += [(f"wath{q_}", 0, 1), (f"watl{q_}", 0, 1)]
        p.op("sync", lambda e, tt=tt: e.dma_start(
                out=P(D * tt, D), in_=xown[128 * tt:128 * (tt + 1), :]),
             writes=wkeys, dma=True, tag="ph6_xo")

    p.barrier()

    if stop_after < 6:
        _finish(p, nc, stack, y, pool, P, D, OT)
        return nc

    for tt in range(OT):
        x2k = ("x2", D * tt, D * (tt + 1))
        ms = 24 + 10 * (tt % 2)
        r = 128 * tt
        qi = next(i for i, (tq0, tqn) in enumerate(_qch)
                  if tq0 // 2 <= r < (tq0 + tqn) // 2)
        ro = r - _qch[qi][0] // 2
        p.op("sync", lambda e, tt=tt, qi=qi, ro=ro: e.dma_start(
                out=W(2 * D, D), in_=rsoutsq[qi][ro:ro + 128, :]),
             reads=[(f"rsouts{qi}", 0, 1)], writes=[("xo6", 0, 1)],
             dma=True, tag="ph6_rs")
        p.op("vector", lambda e, tt=tt: e.tensor_tensor(
                P(D * tt, D), P(D * tt, D), W(2 * D, D), ALU.add),
             reads=[x2k, ("xo6", 0, 1)], writes=[x2k], tag="ph6_add")
        sq = (tt + 1) % 2
        p.op("scalar", lambda e, tt=tt, ms=ms: e.activation(
                W(2 * D, D), P(D * tt, D), AF.Square, accum_out=sm[:, ms + 2:ms + 3]),
             reads=[x2k], writes=[("xo6", 0, 1), ("sx26", ms, ms + 1)], tag="ph6_sq")
        p.op("vector", lambda e, tt=tt, ms=ms: e.reduce_sum(
                sm[:, ms:ms + 1], P(D * tt, D), axis=AX.X),
             reads=[x2k], writes=[("sx6", ms, ms + 1)], tag="ph6_sum")
        p.op("vector", lambda e, ms=ms: e.tensor_scalar(
                sm[:, ms + 4:ms + 5], sm[:, ms:ms + 1], 1.0 / D, None, ALU.mult),
             reads=[("sx6", ms, ms + 1)], writes=[("mean6", ms, ms + 1)], tag="ph6_mean")
        p.op("vector", lambda e, ms=ms: e.tensor_tensor(
                sm[:, ms + 5:ms + 6], sm[:, ms + 4:ms + 5], sm[:, ms + 4:ms + 5], ALU.mult),
             reads=[("mean6", ms, ms + 1)], writes=[("msq6", ms, ms + 1)], tag="ph6_msq")
        p.op("vector", lambda e, ms=ms: e.tensor_scalar(
                sm[:, ms + 6:ms + 7], sm[:, ms + 2:ms + 3], 1.0 / D, None, ALU.mult),
             reads=[("sx26", ms, ms + 1)], writes=[("var6", ms, ms + 1)], tag="ph6_var0")
        p.op("vector", lambda e, ms=ms: e.tensor_scalar(
                sm[:, ms + 7:ms + 8], sm[:, ms + 6:ms + 7], sm[:, ms + 5:ms + 6],
                cfg.eps, ALU.subtract, ALU.add),
             reads=[("var6", ms, ms + 1), ("msq6", ms, ms + 1)],
             writes=[("vpe6", ms, ms + 1)], tag="ph6_var")
        p.op("scalar", lambda e, ms=ms: e.activation(
                sm[:, ms + 8:ms + 9], sm[:, ms + 7:ms + 8], AF.Ln),
             reads=[("vpe6", ms, ms + 1)], writes=[("lnv6", ms, ms + 1)], tag="ph6_ln")
        p.op("scalar", lambda e, ms=ms: e.activation(
                sm[:, ms + 9:ms + 10], sm[:, ms + 8:ms + 9], AF.Exp, scale=-0.5),
             reads=[("lnv6", ms, ms + 1)], writes=[("rstd6", ms, ms + 1)], tag="ph6_rstd")
        sx = tt % 2
        p.op("vector", lambda e, tt=tt, sx=sx, ms=ms: e.tensor_scalar(
                W(sx * D, D), P(D * tt, D), sm[:, ms + 4:ms + 5], sm[:, ms + 9:ms + 10],
                ALU.subtract, ALU.mult),
             reads=[x2k, ("mean6", ms, ms + 1), ("rstd6", ms, ms + 1)],
             writes=[(f"w6{sx}", 0, 1)], tag="ph6_apply")
        p.op("scalar", lambda e, sx=sx, sq=sq: e.copy(Wb(sq * D, D), W(sx * D, D)),
             reads=[(f"w6{sx}", 0, 1)], writes=[(f"w6{sq}", 0, 1)], tag="ph6_ch")
        p.op("vector", lambda e, sx=sx, sq=sq: e.tensor_tensor(
                Wb(sq * D + D // 2, D), W(sx * D, D), Wb(sq * D, D), ALU.subtract),
             reads=[(f"w6{sx}", 0, 1), (f"w6{sq}", 0, 1)],
             writes=[(f"w6{sq}", 0, 1)], tag="ph6_cl")
        for hl, (trt, trk) in enumerate(((psA, "psA"), (psD, "psD"))):
            srcoff = sq * D if hl == 0 else sq * D + D // 2
            fmoff = sx * D if hl == 0 else sx * D + D // 2
            for c in range(KC):
                bank = (c // 8) % 4
                slot = c % 8
                pcol = 512 * bank + 64 * slot
                p.op("tensor", lambda e, c=c, srcoff=srcoff, trt=trt, pcol=pcol: e.transpose(
                        trt[:, pcol:pcol + 64].bitcast(BF16),
                        Wb(srcoff, D)[:, 128 * c:128 * (c + 1)], s_bid[:]),
                     reads=[(f"w6{sq}", 0, 1), ("c_bid", 0, 1)],
                     writes=[(trk, pcol, pcol + 64)], tag="ph6_tr")
                if slot == 7 or c == KC - 1:
                    c0 = (c // 8) * 8
                    ncp = c - c0 + 1
                    p.op("vector", lambda e, c0=c0, ncp=ncp, bank=bank, trt=trt,
                            fmoff=fmoff: e.tensor_copy(
                            Wb(fmoff + 64 * c0, 128 * ncp),
                            trt[:, 512 * bank:512 * bank + 64 * ncp].bitcast(BF16)),
                         reads=[(trk, 512 * bank, 512 * bank + 64 * ncp)],
                         writes=[(f"w6{sx}", 0, 1)], tag="ph6_ev")
            dstf = (xn2h if hl == 0 else xn2l).rearrange("(c q) t -> q c t", q=128)[
                :, :, 128 * tt:128 * (tt + 1)]
            srcf = Wb(fmoff, D).rearrange("q (c j) -> q c j", c=KC)
            for c0 in range(0, KC, 16):
                c1 = min(c0 + 16, KC)
                p.op("scalar", lambda e, dst=dstf[:, c0:c1, :], src=srcf[:, c0:c1, :]:
                        e.dma_start(out=dst, in_=src),
                     reads=[(f"w6{sx}", 0, 1)], writes=[("xn2s", 0, 1)],
                     dma=True, tag="ph6_st")

    p.barrier()

    if stop_after < 7:
        _finish(p, nc, stack, y, pool, P, D, OT)
        return nc

    NW = OWN
    A0 = 2 * NW
    LG = A0 + 512
    WFM = LG + NW
    WBIG = WFM + NW
    MIDS = WBIG + NW
    MIDSP = WKC - NW
    HW_ = NW // 2
    assert MIDSP >= MIDS + NW and MIDSP >= 2 * NW + 1536 + NW

    def fm_loads(k, msl, tagp):
        p.op("scalar", lambda e, k=k, msl=msl: e.dma_start(
                out=Wb(msl * NW, NW), in_=xn2h[128 * k:128 * (k + 1), :]),
             reads=[("xn2s", 0, 1)], writes=[(f"fmh{msl}", 0, 1)], dma=True, tag=tagp)
        p.op("scalar", lambda e, k=k, msl=msl: e.dma_start(
                out=Wb(msl * NW + HW_, NW), in_=xn2l[128 * k:128 * (k + 1), :]),
             reads=[("xn2s", 0, 1)], writes=[(f"fml{msl}", 0, 1)], dma=True, tag=tagp)

    for k in range(KC):
        msl = k % 2
        fm_loads(k, msl, "ph7_x")
        p.op("sync", lambda e, k=k, msl=msl: e.dma_start(
                out=Wb(A0 + 136 * msl, 128), in_=ath[128 * k:128 * (k + 1), :]),
             reads=[("ath_s", 0, 1)], writes=[(f"ath{msl}", 0, 1)], dma=True, tag="ph7_a")
        p.op("sync", lambda e, k=k, msl=msl: e.dma_start(
                out=Wb(A0 + 136 * msl + 64, 128), in_=atl[128 * k:128 * (k + 1), :]),
             reads=[("atl_s", 0, 1)], writes=[(f"atl{msl}", 0, 1)], dma=True, tag="ph7_a")
        p.op("sync", lambda e, k=k, msl=msl: e.dma_start(
                out=Wb(A0 + 136 * msl + 128, 8), in_=rth[128 * k:128 * (k + 1), :]),
             reads=[("rth_s", 0, 1)], writes=[(f"rth{msl}", 0, 1)], dma=True, tag="ph7_r")
        p.op("sync", lambda e, k=k, msl=msl: e.dma_start(
                out=Wb(A0 + 136 * msl + 132, 8), in_=rtl[128 * k:128 * (k + 1), :]),
             reads=[("rtl_s", 0, 1)], writes=[(f"rtl{msl}", 0, 1)], dma=True, tag="ph7_r")
        for (t0, n) in _chunks(NW, 512):
            for pi, (wof, wkk, mof, mkk) in enumerate((
                    (A0 + 136 * msl, f"ath{msl}", msl * NW, f"fmh{msl}"),
                    (A0 + 136 * msl, f"ath{msl}", msl * NW + HW_, f"fml{msl}"),
                    (A0 + 136 * msl + 64, f"atl{msl}", msl * NW, f"fmh{msl}"))):
                p.op("tensor", lambda e, k=k, t0=t0, n=n, pi=pi, wof=wof, mof=mof: e.matmul(
                        psA[:, t0:t0 + n], Wb(wof, 128),
                        Wb(mof, NW)[:, t0:t0 + n],
                        start=(k == 0 and pi == 0), stop=(k == KC - 1 and pi == 2)),
                     reads=[(wkk, 0, 1), (mkk, 0, 1)],
                     writes=[("psA", t0, t0 + n)], tag="ph7_mmA")
            for pi, (wof, wkk, mof, mkk) in enumerate((
                    (A0 + 136 * msl + 128, f"rth{msl}", msl * NW, f"fmh{msl}"),
                    (A0 + 136 * msl + 128, f"rth{msl}", msl * NW + HW_, f"fml{msl}"),
                    (A0 + 136 * msl + 132, f"rtl{msl}", msl * NW, f"fmh{msl}"))):
                p.op("tensor", lambda e, k=k, t0=t0, n=n, pi=pi, wof=wof, mof=mof: e.matmul(
                        psA[0:8, 1024 + t0:1024 + t0 + n], Wb(wof, 8),
                        Wb(mof, NW)[:, t0:t0 + n],
                        start=(k == 0 and pi == 0), stop=(k == KC - 1 and pi == 2)),
                     reads=[(wkk, 0, 1), (mkk, 0, 1)],
                     writes=[("psA", 1024 + t0, 1024 + t0 + n)], tag="ph7_mmR")

    p.op("vector", lambda e: e.tensor_copy(W(LG, NW)[0:8, :], psA[0:8, 1024:1024 + NW]),
         reads=[("psA", 1024, 1024 + NW)], writes=[("lg", 0, 1)], tag="ph7_lgev")
    for tt in range(OT):
        p.op("tensor", lambda e, tt=tt: e.transpose(
                psD[:, 8 * tt:8 * (tt + 1)], W(LG, NW)[0:8, 128 * tt:128 * (tt + 1)],
                s_id[0:8, 0:8]),
             reads=[("lg", 0, 1), ("c_id", 0, 1)],
             writes=[("psD", 8 * tt, 8 * (tt + 1))], tag="ph7_lgtr")
    p.op("vector", lambda e: e.tensor_copy(lgsm[:, 0:8 * OT], psD[:, 0:8 * OT]),
         reads=[("psD", 0, 8 * OT)], writes=[("lgt", 0, 8 * OT)], tag="ph7_lgtev")
    T1 = 8 * OT
    T2 = 8 * OT + 8
    WD = 8 * OT + 16
    for tt in range(OT):
        lgt = lgsm[:, 8 * tt:8 * (tt + 1)]
        lk = ("lgt", 8 * tt, 8 * (tt + 1))
        p.op("vector", lambda e, lgt=lgt: e.reduce_max(sm[:, 34:35], lgt, axis=AX.X),
             reads=[lk], writes=[("m1", 0, 1)], tag="ph7_m1")
        p.op("vector", lambda e, lgt=lgt: e.tensor_scalar(
                lgsm[:, T1:T1 + 8], lgt, sm[:, 34:35], -1e30, ALU.is_equal, ALU.mult),
             reads=[lk, ("m1", 0, 1)], writes=[("t1", 0, 1)], tag="ph7_t1")
        p.op("vector", lambda e, lgt=lgt: e.tensor_tensor(
                lgsm[:, T2:T2 + 8], lgt, lgsm[:, T1:T1 + 8], ALU.add),
             reads=[lk, ("t1", 0, 1)], writes=[("t2", 0, 1)], tag="ph7_t2")
        p.op("vector", lambda e: e.reduce_max(sm[:, 35:36], lgsm[:, T2:T2 + 8], axis=AX.X),
             reads=[("t2", 0, 1)], writes=[("m2", 0, 1)], tag="ph7_m2")
        p.op("vector", lambda e: e.tensor_tensor(
                sm[:, 36:37], sm[:, 35:36], sm[:, 34:35], ALU.subtract),
             reads=[("m1", 0, 1), ("m2", 0, 1)], writes=[("dd", 0, 1)], tag="ph7_d")
        p.op("scalar", lambda e: e.activation(sm[:, 37:38], sm[:, 36:37], AF.Exp),
             reads=[("dd", 0, 1)], writes=[("e01", 0, 1)], tag="ph7_e")
        p.op("vector", lambda e: e.tensor_scalar(sm[:, 38:39], sm[:, 37:38], 1.0, None, ALU.add),
             reads=[("e01", 0, 1)], writes=[("t3", 0, 1)], tag="ph7_t3")
        p.op("vector", lambda e: e.reciprocal(sm[:, 39:40], sm[:, 38:39]),
             reads=[("t3", 0, 1)], writes=[("w0r", 0, 1)], tag="ph7_w0r")
        p.op("vector", lambda e: e.tensor_scalar(sm[:, 40:41], sm[:, 39:40], 2.0, None, ALU.mult),
             reads=[("w0r", 0, 1)], writes=[("w0", 0, 1)], tag="ph7_w0")
        p.op("vector", lambda e: e.tensor_scalar(sm[:, 41:42], sm[:, 40:41], -1.0, 2.0, ALU.mult, ALU.add),
             reads=[("w0", 0, 1)], writes=[("w1", 0, 1)], tag="ph7_w1")
        p.op("vector", lambda e, lgt=lgt: e.tensor_scalar(
                lgsm[:, T1:T1 + 8], lgt, sm[:, 34:35], sm[:, 40:41], ALU.is_equal, ALU.mult),
             reads=[lk, ("m1", 0, 1), ("w0", 0, 1)], writes=[("t1", 0, 1)], tag="ph7_c0")
        p.op("vector", lambda e, lgt=lgt: e.tensor_scalar(
                lgsm[:, T2:T2 + 8], lgt, sm[:, 35:36], sm[:, 41:42], ALU.is_equal, ALU.mult),
             reads=[lk, ("m2", 0, 1), ("w1", 0, 1)], writes=[("t2", 0, 1)], tag="ph7_c1")
        p.op("vector", lambda e, tt=tt: e.tensor_tensor(
                lgsm[:, WD + 8 * tt:WD + 8 * (tt + 1)],
                lgsm[:, T1:T1 + 8], lgsm[:, T2:T2 + 8], ALU.add),
             reads=[("t1", 0, 1), ("t2", 0, 1)],
             writes=[("wd", 8 * tt, 8 * (tt + 1))], tag="ph7_wd")
        p.op("tensor", lambda e, tt=tt: e.transpose(
                psD[0:8, 1024 + 128 * tt:1024 + 128 * (tt + 1)],
                lgsm[:, WD + 8 * tt:WD + 8 * (tt + 1)], s_id[:]),
             reads=[("wd", 8 * tt, 8 * (tt + 1)), ("c_id", 0, 1)],
             writes=[("psD", 1024 + 128 * tt, 1024 + 128 * (tt + 1))], tag="ph7_wtr")
    p.op("vector", lambda e: e.tensor_copy(W(WFM, NW)[0:8, :], psD[0:8, 1024:1024 + NW]),
         reads=[("psD", 1024, 1024 + NW)], writes=[("wfm", 0, 1)], tag="ph7_wfev")
    for (t0, n) in _chunks(NW, 512):
        p.op("tensor", lambda e, t0=t0, n=n: e.matmul(
                psD[:, t0:t0 + n], s_bc[:], W(WFM, NW)[0:8, t0:t0 + n],
                start=True, stop=True),
             reads=[("wfm", 0, 1), ("c_bc", 0, 1)],
             writes=[("psD", t0, t0 + n)], tag="ph7_bc")
    p.op("vector", lambda e: e.tensor_copy(W(WBIG, NW), psD[:, 0:NW]),
         reads=[("psD", 0, NW)], writes=[("wbig", 0, 1)], tag="ph7_bigev")
    p.op("vector", lambda e: e.tensor_tensor(
            W(MIDS, NW), psA[:, 0:NW], W(WBIG, NW), ALU.mult),
         reads=[("psA", 0, NW), ("wbig", 0, 1)], writes=[("mids", 0, 1)], tag="ph7_mids")
    p.op("scalar", lambda e: e.copy(Wb(MIDSP, NW), W(MIDS, NW)),
         reads=[("mids", 0, 1)], writes=[("midsh", 0, 1)], tag="ph7_mh")
    p.op("vector", lambda e: e.tensor_tensor(
            Wb(MIDSP + HW_, NW), W(MIDS, NW), Wb(MIDSP, NW), ALU.subtract),
         reads=[("mids", 0, 1), ("midsh", 0, 1)], writes=[("midsl", 0, 1)], tag="ph7_ml")

    p.barrier()

    MB = 2 * OWN
    LB = MB + 1024
    for og in range(D // 512):
        for k in range(KC):
            msl = k % 2
            p.op("scalar", lambda e, k=k, msl=msl: e.dma_start(
                    out=Wb(OWN * msl, OWN), in_=xn2h[128 * k:128 * (k + 1), :]),
                 reads=[("xn2s", 0, 1)], writes=[(f"xh{msl}", 0, 1)], dma=True, tag="ph7b_x")
            p.op("scalar", lambda e, k=k, msl=msl: e.dma_start(
                    out=Wb(OWN * msl + OWN // 2, OWN), in_=xn2l[128 * k:128 * (k + 1), :]),
                 reads=[("xn2s", 0, 1)], writes=[(f"xl{msl}", 0, 1)], dma=True, tag="ph7b_x")
            p.op("sync", lambda e, k=k, og=og, msl=msl: e.dma_start(
                    out=Wb(MB + 512 * msl, 512),
                    in_=bah[128 * k:128 * (k + 1), 512 * og:512 * (og + 1)]),
                 writes=[(f"bmh{msl}", 0, 1)], dma=True, tag="ph7b_w")
            p.op("sync", lambda e, k=k, og=og, msl=msl: e.dma_start(
                    out=Wb(MB + 512 * msl + 256, 512),
                    in_=bal[128 * k:128 * (k + 1), 512 * og:512 * (og + 1)]),
                 writes=[(f"bml{msl}", 0, 1)], dma=True, tag="ph7b_w")
            for tt in range(OT):
                ps, pkey = (psA, "psA") if tt % 2 == 0 else (psD, "psD")
                poff = 512 * ((tt // 2) % 4)
                for pi, (xof, xkk, wof, wkk) in enumerate((
                        (OWN * msl, f"xh{msl}", MB + 512 * msl, f"bmh{msl}"),
                        (OWN * msl + OWN // 2, f"xl{msl}", MB + 512 * msl, f"bmh{msl}"),
                        (OWN * msl, f"xh{msl}", MB + 512 * msl + 256, f"bml{msl}"))):
                    p.op("tensor", lambda e, k=k, tt=tt, pi=pi, xof=xof, wof=wof,
                            ps=ps, poff=poff: e.matmul(
                            ps[:, poff:poff + 512],
                            Wb(xof, OWN)[:, 128 * tt:128 * (tt + 1)],
                            Wb(wof, 512),
                            start=(k == 0 and pi == 0), stop=False),
                         reads=[(xkk, 0, 1), (wkk, 0, 1)],
                         writes=[(pkey, poff, poff + 512)], tag="ph7b_mm")
        p.op("sync", lambda e, og=og: e.dma_start(
                out=Wb(LB, 512), in_=bfh[:, 512 * og:512 * (og + 1)]),
             writes=[("lbh", 0, 1)], dma=True, tag="ph7b_lb")
        p.op("sync", lambda e, og=og: e.dma_start(
                out=Wb(LB + 256, 512), in_=bfl[:, 512 * og:512 * (og + 1)]),
             writes=[("lbl", 0, 1)], dma=True, tag="ph7b_lb")
        for tt in range(OT):
            ps, pkey = (psA, "psA") if tt % 2 == 0 else (psD, "psD")
            poff = 512 * ((tt // 2) % 4)
            for pi, (xof, xkk, wof, wkk) in enumerate((
                    (MIDSP, "midsh", LB, "lbh"),
                    (MIDSP + HW_, "midsl", LB, "lbh"),
                    (MIDSP, "midsh", LB + 256, "lbl"))):
                p.op("tensor", lambda e, tt=tt, pi=pi, xof=xof, wof=wof,
                        ps=ps, poff=poff: e.matmul(
                        ps[:, poff:poff + 512],
                        Wb(xof, NW)[:, 128 * tt:128 * (tt + 1)],
                        Wb(wof, 512),
                        start=False, stop=(pi == 2)),
                     reads=[(xkk, 0, 1), (wkk, 0, 1)],
                     writes=[(pkey, poff, poff + 512)], tag="ph7b_mlb")
        for tt in range(OT):
            ps, pkey = (psA, "psA") if tt % 2 == 0 else (psD, "psD")
            poff = 512 * ((tt // 2) % 4)
            p.op("vector", lambda e, tt=tt, og=og, ps=ps, poff=poff: e.tensor_tensor(
                    P(D * tt + 512 * og, 512), P(D * tt + 512 * og, 512),
                    ps[:, poff:poff + 512], ALU.add),
                 reads=[(pkey, poff, poff + 512),
                        ("x2", D * tt + 512 * og, D * tt + 512 * (og + 1))],
                 writes=[("x2", D * tt + 512 * og, D * tt + 512 * (og + 1))], tag="ph7b_add")

    _finish(p, nc, stack, y, pool, P, D, OT)
    return nc



def make_in_maps(cfg, x, Wqkv, Wo, base_W, router_W, lora_A, lora_B):
    D, S, NQ, NKV = cfg.D, cfg.S, cfg.NQ, cfg.NKV
    QH, KH = cfg.QH, cfg.KH
    C2, S2 = _rope_tables(cfg)
    ident = np.eye(128, dtype=np.float32)
    import ml_dtypes
    identb = np.eye(128, dtype=ml_dtypes.bfloat16)
    msk = np.where(np.arange(128)[None, :] > np.arange(128)[:, None],
                   np.float32(-1e30), np.float32(0.0)).astype(np.float32)
    rmat = np.zeros((128, 128), dtype=np.float32)
    for m in range(64):
        rmat[m + 64, m] = -1.0
        rmat[m, m + 64] = 1.0
    bc = np.zeros((8, 128), dtype=np.float32)
    for e in range(8):
        bc[e, 16 * e:16 * (e + 1)] = 1.0

    WqkvT = np.ascontiguousarray(Wqkv.T)
    WoT = np.ascontiguousarray(Wo.T)
    baseT = np.ascontiguousarray(base_W.T)
    routT = np.ascontiguousarray(router_W.T)
    aT = np.ascontiguousarray(lora_A.transpose(1, 0, 2).reshape(D, 128))
    bflat = np.ascontiguousarray(lora_B.reshape(128, D))

    qd, kd = NQ * 128, NKV * 128
    in_maps = []
    for c in range(8):
        b, h = c // 2, c % 2
        wq = WqkvT[:, 128 * QH * h:128 * QH * (h + 1)]
        wk_ = WqkvT[:, qd + 128 * KH * h: qd + 128 * KH * (h + 1)]
        wv = WqkvT[:, qd + kd + 128 * KH * h: qd + kd + 128 * KH * (h + 1)]
        m = {
            "xb": np.ascontiguousarray(x[b]),
            "xown": np.ascontiguousarray(x[b, (S // 2) * h:(S // 2) * (h + 1)]),
            "wqkvT": np.ascontiguousarray(np.concatenate([wq, wk_, wv], axis=1)),
            "woT": np.ascontiguousarray(WoT[128 * QH * h:128 * QH * (h + 1), :]),
            "baseT": baseT, "routT": routT, "aT": aT, "bflat": bflat,
            "ropeC": C2, "ropeS": S2, "cid": ident, "cbid": identb, "cmsk": msk,
            "crmat": rmat, "cbc": bc,
        }
        in_maps.append(m)
    return in_maps


def kernel(L0_attn_hidden_states, L0_attn_Wqkv, L0_attn_Wo,
           L0_ffn_base_W, L0_ffn_router_W, L0_ffn_lora_A, L0_ffn_lora_B):
    cfg = Cfg()
    x = np.ascontiguousarray(np.asarray(L0_attn_hidden_states, dtype=np.float32))
    in_maps = make_in_maps(cfg, x,
                           np.asarray(L0_attn_Wqkv, dtype=np.float32),
                           np.asarray(L0_attn_Wo, dtype=np.float32),
                           np.asarray(L0_ffn_base_W, dtype=np.float32),
                           np.asarray(L0_ffn_router_W, dtype=np.float32),
                           np.asarray(L0_ffn_lora_A, dtype=np.float32),
                           np.asarray(L0_ffn_lora_B, dtype=np.float32))
    nc = build_program(cfg)
    res = run_bass_kernel_spmd(nc, in_maps, core_ids=list(range(8)))
    out = np.empty((4, cfg.S, cfg.D), dtype=np.float32)
    half = cfg.S // 2
    for c in range(8):
        b, h = c // 2, c % 2
        out[b, half * h:half * (h + 1)] = res.results[c]["y"]
    return out
